# version 12
# speedup vs baseline: 1.2074x; 1.1185x over previous
import contextlib
import math
import numpy as np
import ml_dtypes
import concourse.bass as bass
import concourse.mybir as mybir
from concourse.alu_op_type import AluOpType as OP
from concourse.bass_utils import run_bass_kernel_spmd

F32 = mybir.dt.float32
BF16 = mybir.dt.bfloat16
U32 = mybir.dt.uint32
AF = mybir.ActivationFunctionType
BF16NP = ml_dtypes.bfloat16

B, T, E = 4, 4096, 1024
TC = T // 2
NF = E // 128
LN_EPS = 1e-5



class Sem:
    def __init__(self, stack, nc, name):
        self.h = stack.enter_context(nc.semaphore(name))
        self.n = 0

    def inc(self, instr, k=16):
        instr.then_inc(self.h, k)
        self.n += k
        return self.n

    def wait(self, eng, n=None):
        eng.wait_ge(self.h, self.n if n is None else n)


class Prog:
    def __init__(self, stack, name="prog"):
        self.nc = bass.Bass("TRN2", target_bir_lowering=False, debug=False)
        self.stack = stack
        self.name = name

    def din(self, name, shape, dt=F32):
        return self.nc.dram_tensor(name, list(shape), dt, kind="ExternalInput").ap()

    def dout(self, name, shape, dt=F32):
        return self.nc.dram_tensor(name, list(shape), dt, kind="ExternalOutput").ap()

    def dscratch(self, name, shape, dt=F32, shared=False):
        kw = dict(addr_space="Shared") if shared else {}
        return self.nc.dram_tensor(name, list(shape), dt, kind="Internal", **kw).ap()

    def sb(self, name, shape, dt=F32):
        return self.stack.enter_context(self.nc.sbuf_tensor(name, list(shape), dt))

    def ps(self, name, shape, dt=F32):
        return self.stack.enter_context(self.nc.psum_tensor(name, list(shape), dt))

    def sem(self, name):
        return Sem(self.stack, self.nc, name)


class LNTiles:

    def __init__(self, p, W=256, tag="", s1=None, s2=None, mb=None, rb=None):
        self.W = W
        self.sq = p.sb(f"ln_sq{tag}", [128, NF * W], F32)
        self.row = p.sb(f"ln_row{tag}", [1, 4 * W], F32)
        self.ones_c = p.sb(f"ln_onc{tag}", [128, 1], F32)
        self.ones_cb = p.sb(f"ln_oncb{tag}", [128, 1], BF16)
        self.ones_r = p.sb(f"ln_onr{tag}", [1, 128], F32)
        if s1 is None:
            s12 = p.ps(f"ln_s12{tag}", [1, 2 * W], F32)
            s1, s2 = s12[:, 0:W], s12[:, W:2 * W]
        self.s1 = s1
        self.s2 = s2
        if mb is None:
            mrb = p.ps(f"ln_mrb{tag}", [128, 2 * W], F32)
            mb, rb = mrb[:, 0:W], mrb[:, W:2 * W]
        self.mb = mb
        self.rb = rb

    def init(self, g):
        g.memset(self.ones_c[:], 1.0)
        g.memset(self.ones_cb[:], 1.0)
        return g.memset(self.ones_r[:], 1.0)


def ln_squares(a, ln, xs, W):
    for f in range(NF):
        i = a.activation(out=ln.sq[:, f * ln.W:f * ln.W + W], in_=xs[f], func=AF.Square)
    return i


def ln_stats(t, ln, xs, W):
    for f in range(NF):
        t.matmul(ln.s1[:, 0:W], ln.ones_c[:], xs[f], start=(f == 0), stop=(f == NF - 1))
    for f in range(NF):
        i = t.matmul(ln.s2[:, 0:W], ln.ones_c[:], ln.sq[:, f * ln.W:f * ln.W + W],
                     start=(f == 0), stop=(f == NF - 1))
    return i


def ln_rows(v, ln, W, eps=LN_EPS):
    mean = ln.row[0:1, 0:W]
    var = ln.row[0:1, ln.W:ln.W + W]
    tmp = ln.row[0:1, 3 * ln.W:3 * ln.W + W]
    v.tensor_scalar(out=mean, in0=ln.s1[:, 0:W], scalar1=1.0 / E, scalar2=None, op0=OP.mult)
    v.tensor_scalar(out=var, in0=ln.s2[:, 0:W], scalar1=1.0 / E, scalar2=eps,
                    op0=OP.mult, op1=OP.add)
    v.drain()
    v.tensor_tensor(out=tmp, in0=mean, in1=mean, op=OP.mult)
    v.drain()
    return v.tensor_tensor(out=var, in0=var, in1=tmp, op=OP.subtract)


def ln_rstd(a, ln, W):
    var = ln.row[0:1, ln.W:ln.W + W]
    tmp = ln.row[0:1, 3 * ln.W:3 * ln.W + W]
    rstd = ln.row[0:1, 2 * ln.W:2 * ln.W + W]
    a.activation(out=tmp, in_=var, func=AF.Ln)
    a.drain()
    return a.activation(out=rstd, in_=tmp, func=AF.Exp, scale=-0.5)


def ln_bcast(t, ln, W):
    t.matmul(ln.mb[:, 0:W], ln.ones_r[:], ln.row[0:1, 0:W], start=True, stop=True)
    return t.matmul(ln.rb[:, 0:W], ln.ones_r[:], ln.row[0:1, 2 * ln.W:2 * ln.W + W], start=True, stop=True)



def build_l0_ffn(stack, Tc=TC, name="l0ffn"):
    p = Prog(stack, name)
    nc = p.nc
    x_hbm = p.din("x", [E, Tc])
    w1_hbm = p.din("w1", [E, 4096])
    b1_hbm = p.din("b1", [128, 32])
    w2_hbm = p.din("w2", [4096, E])
    b2_hbm = p.din("b2", [128, NF])
    gam_hbm = p.din("gamma", [128, NF])
    bet_hbm = p.din("beta", [128, NF])
    y_hbm = p.dout("y", [E, Tc])

    DF = 4096
    W = 256
    NQ = Tc // W

    x_q = p.sb("x_q", [128, NF * W], F32)
    xnf_q = p.sb("xnf_q", [128, NF * W], F32)
    xn_q = p.sb("xn_q", [128, NF * W], BF16)
    u_q = p.sb("u_q", [128, NF * W], F32)
    h_all = p.sb("h_all", [128, 32 * W], BF16)
    w1_sb = p.sb("w1_sb", [128, NF * DF], BF16)
    w2_sb = p.sb("w2_sb", [128, 32 * E], BF16)
    b1_sb = p.sb("b1_sb", [128, 32], F32)
    b2_sb = p.sb("b2_sb", [128, NF], F32)
    gam_sb = p.sb("gam_sb", [128, NF], F32)
    bet_sb = p.sb("bet_sb", [128, NF], F32)
    ln = LNTiles(p, W)
    pu = [p.ps(f"pu{i}", [128, W], F32)[:, 0:W] for i in range(2)]
    pd = [p.ps(f"pd{i}", [128, W], F32)[:, 0:W] for i in range(2)]

    sem_w = p.sem("sem_w")
    sem_b = p.sem("sem_b")
    sem_x = p.sem("sem_x")
    sem_mm = p.sem("sem_mm")
    sem_v = p.sem("sem_v")
    sem_a = p.sem("sem_a")
    sem_g = p.sem("sem_g")
    sem_st = p.sem("sem_st")

    def w1v(f, oc):
        return w1_sb[:, f * DF + oc * 128: f * DF + (oc + 1) * 128]

    def w2v(oc, f):
        return w2_sb[:, oc * E + f * 128: oc * E + (f + 1) * 128]

    with nc.Block() as blk:
        @blk.gpsimd
        def _(g):
            for f in range(NF):
                sem_w.inc(g.dma_start(out=w1_sb[:, f * DF:(f + 1) * DF],
                                      in_=w1_hbm[f * 128:(f + 1) * 128, :]))
            for c in range(32):
                sem_w.inc(g.dma_start(out=w2_sb[:, c * E:(c + 1) * E],
                                      in_=w2_hbm[c * 128:(c + 1) * 128, :]))
            sem_g.inc(ln.init(g), 1)

        @blk.sync
        def _(s):
            sem_b.inc(s.dma_start(out=b1_sb[:], in_=b1_hbm[:]))
            sem_b.inc(s.dma_start(out=b2_sb[:], in_=b2_hbm[:]))
            sem_b.inc(s.dma_start(out=gam_sb[:], in_=gam_hbm[:]))
            sem_b.inc(s.dma_start(out=bet_sb[:], in_=bet_hbm[:]))

        for q in range(NQ):
            tsl = slice(q * W, (q + 1) * W)

            def xq(f):
                return x_q[:, f * W:(f + 1) * W]

            def xnfq(f):
                return xnf_q[:, f * W:(f + 1) * W]

            def xnq(f):
                return xn_q[:, f * W:(f + 1) * W]

            def uq(f):
                return u_q[:, f * W:(f + 1) * W]

            @blk.sync
            def _(s, tsl=tsl, q=q):
                if q > 0:
                    sem_st.wait(s)
                for f in range(NF):
                    sem_x.inc(s.dma_start(out=xq(f), in_=x_hbm[f * 128:(f + 1) * 128, tsl]))

            @blk.scalar
            def _(a, q=q):
                sem_x.wait(a)
                sem_mm.wait(a)
                sem_v.wait(a)
                sem_a.inc(ln_squares(a, ln, [xq(f) for f in range(NF)], W), 1)

            @blk.tensor
            def _(t, q=q):
                sem_a.wait(t)
                if q == 0:
                    sem_g.wait(t)
                    sem_w.wait(t)
                    sem_b.wait(t)
                sem_mm.inc(ln_stats(t, ln, [xq(f) for f in range(NF)], W), 1)

            @blk.vector
            def _(v, q=q):
                sem_mm.wait(v)
                sem_v.inc(ln_rows(v, ln, W), 1)

            @blk.scalar
            def _(a, q=q):
                sem_v.wait(a)
                sem_a.inc(ln_rstd(a, ln, W), 1)

            @blk.tensor
            def _(t, q=q):
                sem_a.wait(t)
                sem_mm.inc(ln_bcast(t, ln, W), 1)

            @blk.vector
            def _(v, q=q):
                sem_mm.wait(v)
                for f in range(NF):
                    v.tensor_tensor(out=xnfq(f), in0=xq(f), in1=ln.mb[:, 0:W], op=OP.subtract)
                v.drain()
                for f in range(NF):
                    v.tensor_tensor(out=xnfq(f), in0=xnfq(f), in1=ln.rb[:, 0:W], op=OP.mult)
                v.drain()
                for f in range(NF):
                    i = v.tensor_copy(out=xnq(f), in_=xnfq(f))
                sem_v.inc(i, 1)

            a_base = sem_a.n
            mm_base = sem_mm.n

            @blk.tensor
            def _(t, q=q, a_base=a_base):
                sem_v.wait(t)
                for oc in range(32):
                    if oc >= 2:
                        t.wait_ge(sem_a.h, a_base + (oc - 1))
                    for f in range(NF):
                        i = t.matmul(pu[oc % 2], w1v(f, oc), xnq(f),
                                     start=(f == 0), stop=(f == NF - 1))
                    sem_mm.inc(i, 1)

            @blk.scalar
            def _(a, q=q, a_base=a_base, mm_base=mm_base):
                for oc in range(32):
                    a.wait_ge(sem_mm.h, mm_base + oc + 1)
                    sem_a.inc(a.activation(out=h_all[:, oc * W:(oc + 1) * W],
                                           in_=pu[oc % 2], func=AF.Relu,
                                           bias=b1_sb[:, oc:oc + 1]), 1)

            mm_base2 = sem_mm.n
            v_base = sem_v.n

            @blk.tensor
            def _(t, q=q, a_base=a_base, v_base=v_base):
                t.wait_ge(sem_a.h, a_base + 32)
                for f in range(NF):
                    if f >= 2:
                        t.wait_ge(sem_v.h, v_base + (f - 1))
                    for oc in range(32):
                        i = t.matmul(pd[f % 2], w2v(oc, f),
                                     h_all[:, oc * W:(oc + 1) * W],
                                     start=(oc == 0), stop=(oc == 31))
                    sem_mm.inc(i, 1)

            @blk.vector
            def _(v, q=q, v_base=v_base, mm_base2=mm_base2):
                for f in range(NF):
                    v.wait_ge(sem_mm.h, mm_base2 + f + 1)
                    sem_v.inc(v.tensor_scalar(out=uq(f), in0=pd[f % 2],
                                              scalar1=b2_sb[:, f:f + 1], scalar2=None,
                                              op0=OP.add), 1)
                v.drain()
                for f in range(NF):
                    i = v.tensor_tensor(out=uq(f), in0=uq(f), in1=xnfq(f), op=OP.add)
                sem_v.inc(i, 1)

            @blk.scalar
            def _(a, q=q):
                sem_v.wait(a)
                sem_a.inc(ln_squares(a, ln, [uq(f) for f in range(NF)], W), 1)

            @blk.tensor
            def _(t, q=q):
                sem_a.wait(t)
                sem_mm.inc(ln_stats(t, ln, [uq(f) for f in range(NF)], W), 1)

            @blk.vector
            def _(v, q=q):
                sem_mm.wait(v)
                sem_v.inc(ln_rows(v, ln, W), 1)

            @blk.scalar
            def _(a, q=q):
                sem_v.wait(a)
                sem_a.inc(ln_rstd(a, ln, W), 1)

            @blk.tensor
            def _(t, q=q):
                sem_a.wait(t)
                sem_mm.inc(ln_bcast(t, ln, W), 1)

            @blk.vector
            def _(v, q=q):
                sem_mm.wait(v)
                for f in range(NF):
                    v.tensor_tensor(out=ln.sq[:, f * W:(f + 1) * W], in0=uq(f),
                                    in1=ln.mb[:, 0:W], op=OP.subtract)
                v.drain()
                for f in range(NF):
                    v.tensor_tensor(out=ln.sq[:, f * W:(f + 1) * W],
                                    in0=ln.sq[:, f * W:(f + 1) * W],
                                    in1=ln.rb[:, 0:W], op=OP.mult)
                v.drain()
                for f in range(NF):
                    v.tensor_scalar(out=ln.sq[:, f * W:(f + 1) * W],
                                    in0=ln.sq[:, f * W:(f + 1) * W],
                                    scalar1=gam_sb[:, f:f + 1],
                                    scalar2=bet_sb[:, f:f + 1], op0=OP.mult, op1=OP.add)
                v.drain()
                for f in range(NF):
                    i = v.tensor_tensor(out=xq(f), in0=xq(f),
                                        in1=ln.sq[:, f * W:(f + 1) * W], op=OP.add)
                sem_v.inc(i, 1)

            @blk.sync
            def _(s, tsl=tsl, q=q):
                sem_v.wait(s)
                for f in range(NF):
                    sem_st.inc(s.dma_start(out=y_hbm[f * 128:(f + 1) * 128, tsl], in_=xq(f)))

        @blk.sync
        def _(s):
            sem_st.wait(s)

    return p



def build_l1_ffn(stack, Tc=TC, name="l1ffn"):
    p = Prog(stack, name)
    nc = p.nc
    x_hbm = p.din("x", [E, Tc])
    wu_hbm = p.din("wu", [E, 4096])
    wd_hbm = p.din("wd", [4096, E])
    y_hbm = p.dout("y", [E, Tc])

    DF = 4096
    W = 256
    NQ = Tc // W
    EPS2 = 1e-8

    x_q = p.sb("x_q", [128, NF * W], F32)
    xn_q = p.sb("xn_q", [128, NF * W], BF16)
    hg = p.sb("hg", [128, 32 * W], BF16)
    hsq = p.sb("hsq", [128, W], F32)
    wu_sb = p.sb("wu_sb", [128, NF * DF], BF16)
    wd_sb = p.sb("wd_sb", [128, 32 * E], BF16)
    wtmp = p.sb("wtmp", [128, DF], F32)
    acc = p.sb("acc", [128, 48], F32)
    arow = p.sb("arow", [1, 64], F32)
    thr_col = p.sb("thr_col", [128, 2], F32)
    ln = LNTiles(p, W)
    pu = [p.ps(f"pu{i}", [128, W], F32)[:, 0:W] for i in range(2)]
    pd = [p.ps(f"pd{i}", [128, W], F32)[:, 0:W] for i in range(2)]
    hs_ps = p.ps("hs_ps", [128, W], F32)
    ar_ps = p.ps("ar_ps", [1, 64], F32)

    sem_w = p.sem("sem_w")
    sem_x = p.sem("sem_x")
    sem_mm = p.sem("sem_mm")
    sem_v = p.sem("sem_v")
    sem_a = p.sem("sem_a")
    sem_g = p.sem("sem_g")
    sem_st = p.sem("sem_st")

    NCH_U, NCH_D = NF, 32

    def wchunk_hbm(i):
        if i < NCH_U:
            return wu_hbm[i * 128:(i + 1) * 128, :]
        return wd_hbm[(i - NCH_U) * 128:(i - NCH_U + 1) * 128, 0:E]

    with nc.Block() as blk:
        for i in range(40):
            w_cols = DF if i < NCH_U else E

            @blk.sync
            def _(s, i=i, w_cols=w_cols):
                if i >= 1:
                    sem_v.wait(s)
                sem_x.inc(s.dma_start(out=wtmp[:, 0:w_cols], in_=wchunk_hbm(i)))

            @blk.vector
            def _(v, i=i, w_cols=w_cols):
                sem_x.wait(v)
                sem_v.inc(v.reduce_sum(out=acc[:, i:i + 1], in_=wtmp[:, 0:w_cols],
                                       axis=mybir.AxisListType.X,
                                       apply_absolute_value=True), 1)

        @blk.gpsimd
        def _(g):
            sem_g.inc(LNTiles.init(ln, g), 1)

        @blk.tensor
        def _(t):
            sem_v.wait(t)
            sem_g.wait(t)
            t.matmul(ar_ps[:, 0:8], ln.ones_c[:], acc[:, 0:8], start=True, stop=True)
            sem_mm.inc(t.matmul(ar_ps[:, 8:40], ln.ones_c[:], acc[:, 8:40],
                                start=True, stop=True), 1)

        @blk.vector
        def _(v):
            sem_mm.wait(v)
            v.reduce_sum(out=arow[0:1, 0:1], in_=ar_ps[0:1, 0:8], axis=mybir.AxisListType.X)
            v.reduce_sum(out=arow[0:1, 1:2], in_=ar_ps[0:1, 8:40], axis=mybir.AxisListType.X)
            v.drain()
            sem_v.inc(v.tensor_scalar(out=arow[0:1, 2:4], in0=arow[0:1, 0:2],
                                      scalar1=0.5 / (E * DF), scalar2=None,
                                      op0=OP.mult), 1)

        @blk.tensor
        def _(t):
            sem_v.wait(t)
            sem_mm.inc(t.matmul(hs_ps[:, 0:2], ln.ones_r[:], arow[0:1, 2:4],
                                start=True, stop=True), 1)

        @blk.vector
        def _(v):
            sem_mm.wait(v)
            sem_v.inc(v.tensor_copy(out=thr_col[:], in_=hs_ps[:, 0:2]), 1)

        for i in range(40):
            w_cols = DF if i < NCH_U else E

            @blk.sync
            def _(s, i=i, w_cols=w_cols):
                sem_v.wait(s)
                sem_x.inc(s.dma_start(out=wtmp[:, 0:w_cols], in_=wchunk_hbm(i)))

            @blk.scalar
            def _(a, i=i, w_cols=w_cols):
                sem_x.wait(a)
                dst = (wu_sb[:, i * DF:(i + 1) * DF] if i < NCH_U
                       else wd_sb[:, (i - NCH_U) * E:(i - NCH_U + 1) * E])
                sem_a.inc(a.activation(out=dst, in_=wtmp[:, 0:w_cols], func=AF.Sign), 1)

            @blk.vector
            def _(v, i=i, w_cols=w_cols):
                sem_a.wait(v)
                tc = 0 if i < NCH_U else 1
                dst = (wu_sb[:, i * DF:(i + 1) * DF] if i < NCH_U
                       else wd_sb[:, (i - NCH_U) * E:(i - NCH_U + 1) * E])
                v.tensor_tensor(out=wtmp[:, 0:w_cols], in0=wtmp[:, 0:w_cols],
                                in1=dst, op=OP.mult)
                v.drain()
                v.tensor_scalar(out=wtmp[:, 0:w_cols], in0=wtmp[:, 0:w_cols],
                                scalar1=thr_col[:, tc:tc + 1], scalar2=None,
                                op0=OP.is_gt)
                v.drain()
                sem_v.inc(v.tensor_tensor(out=dst, in0=dst,
                                          in1=wtmp[:, 0:w_cols], op=OP.mult), 1)

        for q in range(NQ):
            tsl = slice(q * W, (q + 1) * W)

            def xq(f):
                return x_q[:, f * W:(f + 1) * W]

            def xnq(f):
                return xn_q[:, f * W:(f + 1) * W]

            @blk.sync
            def _(s, tsl=tsl, q=q):
                if q > 0:
                    sem_st.wait(s)
                else:
                    sem_v.wait(s)
                for f in range(NF):
                    sem_x.inc(s.dma_start(out=xq(f), in_=x_hbm[f * 128:(f + 1) * 128, tsl]))

            @blk.scalar
            def _(a, q=q):
                sem_x.wait(a)
                sem_mm.wait(a)
                sem_v.wait(a)
                sem_a.inc(ln_squares(a, ln, [xq(f) for f in range(NF)], W), 1)

            @blk.tensor
            def _(t, q=q):
                sem_a.wait(t)
                sem_mm.inc(ln_stats(t, ln, [xq(f) for f in range(NF)], W), 1)

            @blk.vector
            def _(v, q=q):
                sem_mm.wait(v)
                sem_v.inc(ln_rows(v, ln, W), 1)

            @blk.scalar
            def _(a, q=q):
                sem_v.wait(a)
                sem_a.inc(ln_rstd(a, ln, W), 1)

            @blk.scalar
            def _(a, q=q):
                sem_a.wait(a)
                sem_a.inc(a.activation(out=ln.row[0:1, 3 * W:3 * W + W],
                                       in_=ln.row[0:1, W:2 * W], func=AF.Sqrt), 1)

            @blk.vector
            def _(v, q=q):
                sem_a.wait(v)
                rstd = ln.row[0:1, 2 * W:3 * W]
                sq_v = ln.row[0:1, 3 * W:4 * W]
                v.tensor_tensor(out=sq_v, in0=sq_v, in1=rstd, op=OP.mult)
                v.drain()
                v.tensor_scalar(out=sq_v, in0=sq_v, scalar1=EPS2, scalar2=None, op0=OP.add)
                v.drain()
                v.reciprocal(out=sq_v, in_=sq_v)
                v.drain()
                sem_v.inc(v.tensor_tensor(out=ln.row[0:1, 2 * W:3 * W], in0=rstd,
                                          in1=sq_v, op=OP.mult), 1)

            @blk.tensor
            def _(t, q=q):
                sem_v.wait(t)
                sem_mm.inc(ln_bcast(t, ln, W), 1)

            @blk.vector
            def _(v, q=q):
                sem_mm.wait(v)
                for f in range(NF):
                    v.tensor_tensor(out=ln.sq[:, f * W:(f + 1) * W], in0=xq(f),
                                    in1=ln.mb[:, 0:W], op=OP.subtract)
                v.drain()
                for f in range(NF):
                    i = v.tensor_tensor(out=xnq(f), in0=ln.sq[:, f * W:(f + 1) * W],
                                        in1=ln.rb[:, 0:W], op=OP.mult)
                sem_v.inc(i, 1)

            a_base = sem_a.n
            mm_base = sem_mm.n

            @blk.tensor
            def _(t, q=q, a_base=a_base):
                sem_v.wait(t)
                for oc in range(32):
                    if oc >= 2:
                        t.wait_ge(sem_a.h, a_base + (oc - 1))
                    for f in range(NF):
                        i = t.matmul(pu[oc % 2], wu_sb[:, f * DF + oc * 128:f * DF + (oc + 1) * 128],
                                     xnq(f), start=(f == 0), stop=(f == NF - 1))
                    sem_mm.inc(i, 1)

            @blk.scalar
            def _(a, q=q, a_base=a_base, mm_base=mm_base):
                for oc in range(32):
                    a.wait_ge(sem_mm.h, mm_base + oc + 1)
                    sem_a.inc(a.activation(out=hg[:, oc * W:(oc + 1) * W],
                                           in_=pu[oc % 2], func=AF.Gelu), 1)

            a2_base = sem_a.n
            mm2_base = sem_mm.n

            @blk.scalar
            def _(a, q=q, a2_base=a2_base, mm2_base=mm2_base):
                a.drain()
                for oc in range(32):
                    a.wait_ge(sem_mm.h, mm2_base + max(0, oc))
                    sem_a.inc(a.activation(out=hsq[:], in_=hg[:, oc * W:(oc + 1) * W],
                                           func=AF.Square), 1)

            @blk.tensor
            def _(t, q=q, a2_base=a2_base):
                for oc in range(32):
                    t.wait_ge(sem_a.h, a2_base + oc + 1)
                    i = t.matmul(ln.s2[:, 0:W], ln.ones_c[:], hsq[:],
                                 start=(oc == 0), stop=(oc == 31))
                    sem_mm.inc(i, 1)

            @blk.scalar
            def _(a, q=q):
                sem_mm.wait(a)
                sem_a.inc(a.activation(out=ln.row[0:1, 3 * W:4 * W], in_=ln.s2[:, 0:W],
                                       func=AF.Sqrt, scale=1.0 / DF), 1)

            @blk.vector
            def _(v, q=q):
                sem_a.wait(v)
                r = ln.row[0:1, 3 * W:4 * W]
                v.tensor_scalar(out=r, in0=r, scalar1=EPS2, scalar2=None, op0=OP.add)
                v.drain()
                sem_v.inc(v.reciprocal(out=r, in_=r), 1)

            @blk.tensor
            def _(t, q=q):
                sem_v.wait(t)
                sem_mm.inc(t.matmul(hs_ps[:, 0:W], ln.ones_r[:], ln.row[0:1, 3 * W:4 * W],
                                    start=True, stop=True), 1)

            @blk.vector
            def _(v, q=q):
                sem_mm.wait(v)
                for oc in range(32):
                    i = v.tensor_tensor(out=hg[:, oc * W:(oc + 1) * W],
                                        in0=hg[:, oc * W:(oc + 1) * W],
                                        in1=hs_ps[:, 0:W], op=OP.mult)
                sem_v.inc(i, 1)

            mm3_base = sem_mm.n
            v3_base = sem_v.n

            @blk.tensor
            def _(t, q=q, v3_base=v3_base):
                sem_v.wait(t)
                for f in range(NF):
                    if f >= 2:
                        t.wait_ge(sem_v.h, v3_base + (f - 1))
                    for oc in range(32):
                        i = t.matmul(pd[f % 2], wd_sb[:, oc * E + f * 128:oc * E + (f + 1) * 128],
                                     hg[:, oc * W:(oc + 1) * W],
                                     start=(oc == 0), stop=(oc == 31))
                    sem_mm.inc(i, 1)

            @blk.vector
            def _(v, q=q, mm3_base=mm3_base):
                for f in range(NF):
                    v.wait_ge(sem_mm.h, mm3_base + f + 1)
                    sem_v.inc(v.tensor_tensor(out=xq(f), in0=xq(f), in1=pd[f % 2],
                                              op=OP.add), 1)

            @blk.sync
            def _(s, tsl=tsl, q=q):
                sem_v.wait(s)
                for f in range(NF):
                    sem_st.inc(s.dma_start(out=y_hbm[f * 128:(f + 1) * 128, tsl], in_=xq(f)))

        @blk.sync
        def _(s):
            sem_st.wait(s)

    return p


def sub_l1_ffn(x, inp):
    wu = np.ascontiguousarray(np.asarray(inp["L1_ffn_w_up"]).T)
    wd = np.ascontiguousarray(np.asarray(inp["L1_ffn_w_down"]).T)
    in_maps = []
    for c in range(8):
        b, s = c // 2, c % 2
        xs = np.ascontiguousarray(x[b][:, s * TC:(s + 1) * TC])
        in_maps.append({"x": xs, "wu": wu, "wd": wd})
    outs = _run("l1ffn", build_l1_ffn, in_maps, ["y"])
    xo = np.empty_like(x)
    for c in range(8):
        b, s = c // 2, c % 2
        xo[b][:, s * TC:(s + 1) * TC] = outs[c][0]
    return xo



def build_l0_attn_a_p1(stack, name="l0aap1"):
    p = Prog(stack, name)
    nc = p.nc
    W = 512
    NQ = T // W
    x_hbm = p.din("x", [E, T])
    wq_hbm = p.din("wq", [E, 512])
    wk_hbm = p.din("wk", [E, 512])
    wv_hbm = p.din("wv", [E, 512])
    bq_hbm = p.din("bq", [128, 4])
    bk_hbm = p.din("bk", [128, 4])
    bv_hbm = p.din("bv", [128, 4])
    ident_hbm = p.din("ident", [128, 128])
    trimask_hbm = p.din("trimask", [128, 128])
    ho_hbm = p.dout("ho", [512, T], BF16)

    x_q = p.sb("x_q", [128, NF * W], F32)
    xn_q = p.sb("xn_q", [128, NF * W], BF16)
    fq = p.sb("fq", [128, 4 * W], BF16)
    fk = p.sb("fk", [128, 4 * W], BF16)
    fv = p.sb("fv", [128, 4 * W], BF16)
    ho_sb = p.sb("ho_sb", [128, 4 * W], BF16)
    wq_sb = p.sb("wq_sb", [128, NF * 512], BF16)
    wk_sb = p.sb("wk_sb", [128, NF * 512], BF16)
    wv_sb = p.sb("wv_sb", [128, NF * 512], BF16)
    bq_sb = p.sb("bq_sb", [128, 4], F32)
    bk_sb = p.sb("bk_sb", [128, 4], F32)
    bv_sb = p.sb("bv_sb", [128, 4], F32)
    idn_sb = p.sb("idn_sb", [128, 128], BF16)
    msk_sb = p.sb("msk_sb", [128, 128], F32)
    scr = p.sb("scr", [128, W], F32)
    escr = p.sb("escr", [128, W], F32)
    mscr = p.sb("mscr", [128, W], mybir.dt.uint8)
    at_sb = p.sb("at_sb", [128, 256], BF16)
    tm_sb = p.sb("tm_sb", [128, 256], BF16)
    S_fp = p.sb("S_fp", [128, 256], F32)
    S_bf = p.sb("S_bf", [128, 256], BF16)
    z_fp = p.sb("z_fp", [128, 4], F32)
    z_bf = p.sb("z_bf", [128, 4], BF16)
    zt_sb = p.sb("zt_sb", [128, 1], F32)
    o_fp = p.sb("o_fp", [128, 128], F32)
    drow = p.sb("drow", [1, 256], F32)
    ln = LNTiles(p, W)

    bA = p.ps("bA", [128, W], F32)
    bB = p.ps("bB", [128, W], F32)
    bC = p.ps("bC", [128, W], F32)
    bT = p.ps("bT", [128, 256], BF16)

    pu = [bA[:, 0:W], bB[:, 0:W]]
    intra_ps = bA[:, 0:128]
    outp_ps = bB[:, 0:128]
    den_ps = bC[0:1, 0:256]
    schunk_ps = bC[:, 256:320]
    rdb_ps = bC[:, 384:512]

    sem_w = p.sem("sem_w")
    sem_b = p.sem("sem_b")
    sem_x = p.sem("sem_x")
    sem_mm = p.sem("sem_mm")
    sem_v = p.sem("sem_v")
    sem_a = p.sem("sem_a")
    sem_g = p.sem("sem_g")
    sem_st = p.sem("sem_st")

    with nc.Block() as blk:
        @blk.gpsimd
        def _(g):
            for f in range(NF):
                sem_w.inc(g.dma_start(out=wq_sb[:, f * 512:(f + 1) * 512],
                                      in_=wq_hbm[f * 128:(f + 1) * 128, :]))
                sem_w.inc(g.dma_start(out=wk_sb[:, f * 512:(f + 1) * 512],
                                      in_=wk_hbm[f * 128:(f + 1) * 128, :]))
                sem_w.inc(g.dma_start(out=wv_sb[:, f * 512:(f + 1) * 512],
                                      in_=wv_hbm[f * 128:(f + 1) * 128, :]))
            sem_w.inc(g.dma_start(out=idn_sb[:], in_=ident_hbm[:]))
            g.memset(z_fp[:], 0.0)
            g.memset(z_bf[:], 0.0)
            g.memset(S_fp[:], 0.0)
            g.memset(S_bf[:], 0.0)
            sem_g.inc(ln.init(g), 1)

        @blk.sync
        def _(s):
            sem_b.inc(s.dma_start(out=bq_sb[:], in_=bq_hbm[:]))
            sem_b.inc(s.dma_start(out=bk_sb[:], in_=bk_hbm[:]))
            sem_b.inc(s.dma_start(out=bv_sb[:], in_=bv_hbm[:]))
            sem_b.inc(s.dma_start(out=msk_sb[:], in_=trimask_hbm[:]))

        for q in range(NQ):
            tsl = slice(q * W, (q + 1) * W)

            def xq(f):
                return x_q[:, f * W:(f + 1) * W]

            def xnq(f):
                return xn_q[:, f * W:(f + 1) * W]

            @blk.sync
            def _(s, tsl=tsl, q=q):
                if q > 0:
                    sem_st.wait(s)
                for f in range(NF):
                    sem_x.inc(s.dma_start(out=xq(f), in_=x_hbm[f * 128:(f + 1) * 128, tsl]))

            @blk.scalar
            def _(a, q=q):
                sem_x.wait(a)
                sem_mm.wait(a)
                sem_v.wait(a)
                sem_a.inc(ln_squares(a, ln, [xq(f) for f in range(NF)], W), 1)

            @blk.tensor
            def _(t, q=q):
                sem_a.wait(t)
                if q == 0:
                    sem_g.wait(t)
                    sem_w.wait(t)
                    sem_b.wait(t)
                sem_mm.inc(ln_stats(t, ln, [xq(f) for f in range(NF)], W), 1)

            @blk.vector
            def _(v, q=q):
                sem_mm.wait(v)
                sem_v.inc(ln_rows(v, ln, W), 1)

            @blk.scalar
            def _(a, q=q):
                sem_v.wait(a)
                sem_a.inc(ln_rstd(a, ln, W), 1)

            @blk.tensor
            def _(t, q=q):
                sem_a.wait(t)
                sem_mm.inc(ln_bcast(t, ln, W), 1)

            @blk.vector
            def _(v, q=q):
                sem_mm.wait(v)
                for f in range(NF):
                    v.tensor_tensor(out=ln.sq[:, f * W:(f + 1) * W], in0=xq(f),
                                    in1=ln.mb[:, 0:W], op=OP.subtract)
                v.drain()
                for f in range(NF):
                    i = v.tensor_tensor(out=xnq(f), in0=ln.sq[:, f * W:(f + 1) * W],
                                        in1=ln.rb[:, 0:W], op=OP.mult)
                sem_v.inc(i, 1)

            tiles = ([("q", wq_sb, bq_sb, fq, r) for r in range(4)]
                     + [("k", wk_sb, bk_sb, fk, r) for r in range(4)]
                     + [("v", wv_sb, bv_sb, fv, r) for r in range(4)])
            vcnt = [2 if k_ != "v" else 1 for (k_, _, _, _, _) in tiles]
            vcum = [0]
            for c in vcnt:
                vcum.append(vcum[-1] + c)
            acum = [0]
            for (k_, _, _, _, _) in tiles:
                acum.append(acum[-1] + (1 if k_ != "v" else 0))
            v_base = sem_v.n
            a_base = sem_a.n
            mm_base = sem_mm.n

            @blk.tensor
            def _(t, q=q, tiles=tiles, v_base=v_base, vcum=vcum):
                sem_v.wait(t)
                for j, (kind, wsb, bsb, dst, r) in enumerate(tiles):
                    if j >= 2:
                        t.wait_ge(sem_v.h, v_base + vcum[j - 2] + 1)
                    for f in range(NF):
                        i = t.matmul(pu[j % 2], wsb[:, f * 512 + r * 128:f * 512 + (r + 1) * 128],
                                     xnq(f), start=(f == 0), stop=(f == NF - 1))
                    sem_mm.inc(i, 1)

            @blk.vector
            def _(v, q=q, tiles=tiles, mm_base=mm_base, a_base=a_base, acum=acum):
                for j, (kind, wsb, bsb, dst, r) in enumerate(tiles):
                    v.wait_ge(sem_mm.h, mm_base + j + 1)
                    if kind == "v":
                        sem_v.inc(v.tensor_scalar(out=dst[:, r * W:(r + 1) * W],
                                                  in0=pu[j % 2], scalar1=bsb[:, r:r + 1],
                                                  scalar2=None, op0=OP.add), 1)
                    else:
                        if acum[j] > 0:
                            v.wait_ge(sem_a.h, a_base + acum[j])
                        sem_v.inc(v.tensor_scalar(out=scr[:], in0=pu[j % 2],
                                                  scalar1=bsb[:, r:r + 1],
                                                  scalar2=None, op0=OP.add), 1)
                        v.drain()
                        v.tensor_scalar(out=mscr[:], in0=scr[:], scalar1=0.0,
                                        scalar2=None, op0=OP.is_gt)
                        v.wait_ge(sem_a.h, a_base + acum[j] + 1)
                        v.drain()
                        v.tensor_scalar(out=scr[:], in0=scr[:], scalar1=1.0,
                                        scalar2=None, op0=OP.add)
                        v.drain()
                        sem_v.inc(v.select(out=dst[:, r * W:(r + 1) * W], mask=mscr[:],
                                           on_true=scr[:], on_false=escr[:],
                                           add_drain=True), 1)
                        v.drain()

            @blk.scalar
            def _(a, q=q, tiles=tiles, v_base=v_base, vcum=vcum):
                for j, (kind, wsb, bsb, dst, r) in enumerate(tiles):
                    if kind == "v":
                        continue
                    a.wait_ge(sem_v.h, v_base + vcum[j] + 1)
                    sem_a.inc(a.activation(out=escr[:], in_=scr[:], func=AF.Exp), 1)

            for j in range(4):
                for pair in range(4):
                    mmb = sem_mm.n
                    vb = sem_v.n
                    csl = slice(j * 128, (j + 1) * 128)

                    def fqp():
                        return fq[:, pair * W + j * 128: pair * W + (j + 1) * 128]

                    def fkp():
                        return fk[:, pair * W + j * 128: pair * W + (j + 1) * 128]

                    def fvp():
                        return fv[:, pair * W + j * 128: pair * W + (j + 1) * 128]

                    @blk.tensor
                    def _(t, q=q, j=j, pair=pair, mmb=mmb, vb=vb, fqp=fqp, fkp=fkp, fvp=fvp):
                        sem_v.wait(t)
                        t.transpose(bT[:, 0:128], fkp(), idn_sb[:])
                        i = t.transpose(bT[:, 128:256], fvp(), idn_sb[:])
                        sem_mm.inc(i, 1)
                        t.wait_ge(sem_v.h, vb + 1)
                        sem_mm.inc(t.matmul(intra_ps[:], fkp()[0:64, :], fqp()[0:64, :],
                                            start=True, stop=True), 1)
                        t.wait_ge(sem_v.h, vb + 2)
                        i = t.matmul(intra_ps[:], fkp()[64:128, :], fqp()[64:128, :],
                                     start=True, stop=True)
                        sem_mm.inc(i, 1)
                        t.wait_ge(sem_v.h, vb + 3)
                        for par in (0, 64):
                            hv = slice(par, par + 64)
                            hb = (par // 64) * 128
                            tp = (0, par)
                            t.matmul(outp_ps[hv, :], S_bf[hv, pair * 64:(pair + 1) * 64],
                                     fqp()[hv, :], start=True, stop=False,
                                     tile_position=(par, par))
                            t.matmul(outp_ps[hv, :], tm_sb[:, 128 + par:128 + par + 64],
                                     at_sb[:, hb:hb + 128], start=False, stop=True,
                                     tile_position=tp)
                            t.matmul(den_ps[:, hb:hb + 128], z_bf[hv, pair:pair + 1],
                                     fqp()[hv, :], start=True, stop=False)
                            t.matmul(den_ps[:, hb:hb + 128], ln.ones_cb[:],
                                     at_sb[:, hb:hb + 128], start=False, stop=True)
                            i = t.matmul(schunk_ps[hv, :], tm_sb[:, par:par + 64],
                                         tm_sb[:, 128 + par:128 + par + 64],
                                         start=True, stop=True, tile_position=tp)
                        sem_mm.inc(i, 1)

                    @blk.vector
                    def _(v, q=q, j=j, pair=pair, mmb=mmb, fkp=fkp):
                        v.wait_ge(sem_mm.h, mmb + 1)
                        sem_v.inc(v.tensor_copy(out=tm_sb[:], in_=bT[:]), 1)
                        v.wait_ge(sem_mm.h, mmb + 2)
                        sem_v.inc(v.tensor_tensor(out=at_sb[:, 0:128], in0=intra_ps[:],
                                                  in1=msk_sb[:], op=OP.mult), 1)
                        v.wait_ge(sem_mm.h, mmb + 3)
                        sem_v.inc(v.tensor_tensor(out=at_sb[:, 128:256], in0=intra_ps[:],
                                                  in1=msk_sb[:], op=OP.mult), 1)
                        v.wait_ge(sem_mm.h, mmb + 4)
                        v.tensor_scalar(out=drow[0:1, 0:256], in0=den_ps[0:1, 0:256],
                                        scalar1=1e-6, scalar2=None, op0=OP.add)
                        v.drain()
                        v.reciprocal(out=drow[0:1, 0:256], in_=drow[0:1, 0:256])
                        v.tensor_tensor(out=S_fp[:, pair * 64:(pair + 1) * 64],
                                        in0=S_fp[:, pair * 64:(pair + 1) * 64],
                                        in1=schunk_ps[:], op=OP.add)
                        v.reduce_sum(out=zt_sb[:], in_=fkp(), axis=mybir.AxisListType.X)
                        v.drain()
                        v.tensor_tensor(out=z_fp[:, pair:pair + 1], in0=z_fp[:, pair:pair + 1],
                                        in1=zt_sb[:], op=OP.add)
                        v.drain()
                        v.tensor_copy(out=S_bf[:, pair * 64:(pair + 1) * 64],
                                      in_=S_fp[:, pair * 64:(pair + 1) * 64])
                        v.tensor_copy(out=z_bf[:, pair:pair + 1], in_=z_fp[:, pair:pair + 1])
                        sem_v.inc(v.tensor_copy(out=o_fp[:], in_=outp_ps[:]), 1)

                    mmb3 = sem_mm.n
                    vb3 = sem_v.n

                    @blk.tensor
                    def _(t, q=q, j=j, pair=pair, vb3=vb3):
                        t.wait_ge(sem_v.h, vb3)
                        t.matmul(rdb_ps[0:64, :], ln.ones_r[0:1, 0:64], drow[0:1, 0:128],
                                 start=True, stop=True)
                        i = t.matmul(rdb_ps[64:128, :], ln.ones_r[0:1, 0:64],
                                     drow[0:1, 128:256], start=True, stop=True,
                                     tile_position=(0, 64))
                        sem_mm.inc(i, 1)

                    mmb4 = sem_mm.n

                    @blk.vector
                    def _(v, q=q, j=j, pair=pair, mmb4=mmb4):
                        v.wait_ge(sem_mm.h, mmb4)
                        sem_v.inc(v.tensor_tensor(
                            out=ho_sb[:, pair * W + j * 128:pair * W + (j + 1) * 128],
                            in0=o_fp[:], in1=rdb_ps[:], op=OP.mult), 1)

            @blk.sync
            def _(s, tsl=tsl, q=q):
                sem_v.wait(s)
                for r in range(4):
                    sem_st.inc(s.dma_start(out=ho_hbm[r * 128:(r + 1) * 128, tsl],
                                           in_=ho_sb[:, r * W:(r + 1) * W]))

        @blk.sync
        def _(s):
            sem_st.wait(s)

    return p



def build_combine(stack, with_bias, Tc=TC, name="comb", ein=E):
    p = Prog(stack, name)
    nc = p.nc
    NKF = ein // 128
    ho_hbm = p.din("ho", [ein, Tc], BF16)
    w_hbm = p.din("w", [ein, E])
    x_hbm = p.din("x", [E, Tc])
    if with_bias:
        bo_hbm = p.din("bo", [128, NF])
    y_hbm = p.dout("y", [E, Tc])

    W = 512
    NQ = Tc // W

    ho_sb = p.sb("ho_sb", [128, NKF * W], BF16)
    x_sb = p.sb("x_sb", [128, NF * W], F32)
    w_sb = p.sb("w_sb", [128, NKF * E], BF16)
    if with_bias:
        bo_sb = p.sb("bo_sb", [128, NF], F32)
    pp = [p.ps(f"pp{i}", [128, W], F32)[:, 0:W] for i in range(2)]

    sem_w = p.sem("sem_w")
    sem_b = p.sem("sem_b")
    sem_x = p.sem("sem_x")
    sem_mm = p.sem("sem_mm")
    sem_v = p.sem("sem_v")
    sem_st = p.sem("sem_st")

    with nc.Block() as blk:
        @blk.gpsimd
        def _(g):
            for f in range(NKF):
                sem_w.inc(g.dma_start(out=w_sb[:, f * E:(f + 1) * E],
                                      in_=w_hbm[f * 128:(f + 1) * 128, :]))

        if with_bias:
            @blk.sync
            def _(s):
                sem_b.inc(s.dma_start(out=bo_sb[:], in_=bo_hbm[:]))

        for q in range(NQ):
            tsl = slice(q * W, (q + 1) * W)

            @blk.sync
            def _(s, tsl=tsl, q=q):
                if q > 0:
                    sem_st.wait(s)
                for f in range(NKF):
                    sem_x.inc(s.dma_start(out=ho_sb[:, f * W:(f + 1) * W],
                                          in_=ho_hbm[f * 128:(f + 1) * 128, tsl]))
                for f in range(NF):
                    sem_x.inc(s.dma_start(out=x_sb[:, f * W:(f + 1) * W],
                                          in_=x_hbm[f * 128:(f + 1) * 128, tsl]))

            v_base = sem_v.n
            mm_base = sem_mm.n

            @blk.tensor
            def _(t, q=q, v_base=v_base):
                sem_x.wait(t)
                if q == 0:
                    sem_w.wait(t)
                    if with_bias:
                        sem_b.wait(t)
                for oc in range(NF):
                    if oc >= 2:
                        t.wait_ge(sem_v.h, v_base + (oc - 1))
                    for f in range(NKF):
                        i = t.matmul(pp[oc % 2], w_sb[:, f * E + oc * 128:f * E + (oc + 1) * 128],
                                     ho_sb[:, f * W:(f + 1) * W],
                                     start=(f == 0), stop=(f == NKF - 1))
                    sem_mm.inc(i, 1)

            @blk.vector
            def _(v, q=q, mm_base=mm_base):
                for oc in range(NF):
                    v.wait_ge(sem_mm.h, mm_base + oc + 1)
                    if with_bias:
                        v.tensor_scalar(out=x_sb[:, oc * W:(oc + 1) * W],
                                        in0=x_sb[:, oc * W:(oc + 1) * W],
                                        scalar1=bo_sb[:, oc:oc + 1], scalar2=None,
                                        op0=OP.add)
                        v.drain()
                    sem_v.inc(v.tensor_tensor(out=x_sb[:, oc * W:(oc + 1) * W],
                                              in0=x_sb[:, oc * W:(oc + 1) * W],
                                              in1=pp[oc % 2], op=OP.add), 1)

            @blk.sync
            def _(s, tsl=tsl, q=q):
                sem_v.wait(s)
                for f in range(NF):
                    sem_st.inc(s.dma_start(out=y_hbm[f * 128:(f + 1) * 128, tsl],
                                           in_=x_sb[:, f * W:(f + 1) * W]))

        @blk.sync
        def _(s):
            sem_st.wait(s)

    return p


def sub_l0_attn_a(x, inp):
    wq = np.asarray(inp["L0_attn_a_wq"])
    wk = np.asarray(inp["L0_attn_a_wk"])
    wv = np.asarray(inp["L0_attn_a_wv"])
    wo = np.ascontiguousarray(np.asarray(inp["L0_attn_a_wo"]))
    bq = np.asarray(inp["L0_attn_a_bq"])
    bk = np.asarray(inp["L0_attn_a_bk"])
    bv = np.asarray(inp["L0_attn_a_bv"])
    bo = _vec128(np.asarray(inp["L0_attn_a_bo"]), NF)
    ident = np.eye(128, dtype=np.float32)
    trimask = np.triu(np.ones((128, 128), np.float32))
    in_maps = []
    for c in range(8):
        b, s = c // 2, c % 2
        cols = slice(s * 512, (s + 1) * 512)
        in_maps.append({
            "x": np.ascontiguousarray(x[b]),
            "wq": np.ascontiguousarray(wq[:, cols]),
            "wk": np.ascontiguousarray(wk[:, cols]),
            "wv": np.ascontiguousarray(wv[:, cols]),
            "bq": _vec128(bq[cols], 4), "bk": _vec128(bk[cols], 4),
            "bv": _vec128(bv[cols], 4),
            "ident": ident, "trimask": trimask,
        })
    outs = _run("l0aap1", build_l0_attn_a_p1, in_maps, ["ho"])
    ho_full = [np.concatenate([outs[2 * b][0], outs[2 * b + 1][0]], axis=0)
               for b in range(B)]
    in_maps = []
    for c in range(8):
        b, s = c // 2, c % 2
        tsl = slice(s * TC, (s + 1) * TC)
        in_maps.append({"ho": np.ascontiguousarray(ho_full[b][:, tsl]),
                        "w": wo, "x": np.ascontiguousarray(x[b][:, tsl]),
                        "bo": bo})
    outs = _run("comb_b", lambda st: build_combine(st, True, name="comb_b"),
                in_maps, ["y"])
    xo = np.empty_like(x)
    for c in range(8):
        b, s = c // 2, c % 2
        xo[b][:, s * TC:(s + 1) * TC] = outs[c][0]
    return xo



def build_l1_attn_a_p1(stack, name="l1aap1"):
    p = Prog(stack, name)
    nc = p.nc
    W = 512
    NQ = T // W
    x_hbm = p.din("x", [E, T])
    lnw_hbm = p.din("lnw", [128, NF])
    lnb_hbm = p.din("lnb", [128, NF])
    wq_hbm = p.din("wq", [E, 512])
    wk_hbm = p.din("wk", [E, 512])
    wv_hbm = p.din("wv", [E, 512])
    th_hbm = p.din("theta128", [128, 1])
    frep_hbm = p.din("frep", [128, 1])
    sgn_hbm = p.din("sgn", [128, 1])
    pswap_hbm = p.din("pswap", [128, 128])
    ho_hbm = p.dout("ho", [512, T], BF16)

    x_q = p.sb("x_q", [128, NF * W], F32)
    xn_q = p.sb("xn_q", [128, NF * W], BF16)
    fq = p.sb("fq", [128, 4 * T], BF16)
    fk = p.sb("fk", [128, 4 * T], BF16)
    fv = p.sb("fv", [128, 4 * T], BF16)
    ho_sb = p.sb("ho_sb", [128, 4 * W], BF16)
    wq_sb = p.sb("wq_sb", [128, NF * 512], BF16)
    wk_sb = p.sb("wk_sb", [128, NF * 512], BF16)
    wv_sb = p.sb("wv_sb", [128, NF * 512], BF16)
    lnw_sb = p.sb("lnw_sb", [128, NF], F32)
    lnb_sb = p.sb("lnb_sb", [128, NF], F32)
    th_sb = p.sb("th_sb", [128, 1], F32)
    frep_sb = p.sb("frep_sb", [128, 1], F32)
    sgn_sb = p.sb("sgn_sb", [128, 1], F32)
    psw_sb = p.sb("psw_sb", [128, 128], F32)
    frq_sb = p.sb("frq_sb", [128, 2], F32)
    c220_sb = p.sb("c220_sb", [128, 1], F32)
    trow = p.sb("trow", [128, W], F32)
    ang = p.sb("ang", [128, W], F32)
    krnd = p.sb("krnd", [128, W], F32)
    red = p.sb("red", [128, W], F32)
    cosb = p.sb("cosb", [128, W], F32)
    sinb = p.sb("sinb", [128, W], F32)
    cs2 = p.sb("cs2", [128, W], F32)
    qf = p.sb("qf", [128, W], F32)
    r1 = p.sb("r1", [128, W], F32)
    r2 = p.sb("r2", [128, W], F32)
    wt = p.sb("wt", [128, 2 * W], BF16)
    o_fp = p.sb("o_fp", [64, W], F32)
    csum = p.sb("csum", [1, 512], F32)
    ones_r512 = p.sb("ones_r512", [1, 512], F32)
    bsq_sb = p.sb("bsq_sb", [128, 1], F32)
    drow = p.sb("drow", [1, W], F32)
    ln = LNTiles(p, W)

    bA = p.ps("bA", [128, W], F32)
    bB = p.ps("bB", [128, W], F32)
    bC = p.ps("bC", [128, W], F32)
    bQ = p.ps("bQ", [128, W], F32)
    pu = [bA[:, 0:W], bB[:, 0:W]]
    sc_ps = pu
    outp_ps = bC[0:64, 0:W]
    den_ps = ln.s1[:, 0:W]
    rdb_ps = ln.mb[0:64, 0:W]

    sem_w = p.sem("sem_w")
    sem_b = p.sem("sem_b")
    sem_x = p.sem("sem_x")
    sem_mm = p.sem("sem_mm")
    sem_v = p.sem("sem_v")
    sem_a = p.sem("sem_a")
    sem_g = p.sem("sem_g")
    sem_st = p.sem("sem_st")

    TWO_PI = 6.283185307179586
    import struct

    def trunc7(x):
        b = struct.unpack(">I", struct.pack(">f", np.float32(x)))[0]
        b &= 0xFFFF0000 | 0xFFFE0000
        b &= ~((1 << 16) - 1)
        return struct.unpack(">f", struct.pack(">I", b))[0]

    c1 = trunc7(TWO_PI)
    c2 = trunc7(TWO_PI - c1)
    c3 = float(np.float32(TWO_PI - c1 - np.float64(c2)))
    MAGIC = 12582912.0

    with nc.Block() as blk:
        @blk.gpsimd
        def _(g):
            for f in range(NF):
                sem_w.inc(g.dma_start(out=wq_sb[:, f * 512:(f + 1) * 512],
                                      in_=wq_hbm[f * 128:(f + 1) * 128, :]))
                sem_w.inc(g.dma_start(out=wk_sb[:, f * 512:(f + 1) * 512],
                                      in_=wk_hbm[f * 128:(f + 1) * 128, :]))
                sem_w.inc(g.dma_start(out=wv_sb[:, f * 512:(f + 1) * 512],
                                      in_=wv_hbm[f * 128:(f + 1) * 128, :]))
            ln.init(g)
            g.memset(ones_r512[:], 1.0)
            g.memset(c220_sb[:], 220.0)
            sem_g.inc(g.memset(bsq_sb[:], 0.7071067811865476), 1)

        @blk.sync
        def _(s):
            sem_b.inc(s.dma_start(out=lnw_sb[:], in_=lnw_hbm[:]))
            sem_b.inc(s.dma_start(out=lnb_sb[:], in_=lnb_hbm[:]))
            sem_b.inc(s.dma_start(out=th_sb[:], in_=th_hbm[:]))
            sem_b.inc(s.dma_start(out=frep_sb[:], in_=frep_hbm[:]))
            sem_b.inc(s.dma_start(out=sgn_sb[:], in_=sgn_hbm[:]))
            sem_b.inc(s.dma_start(out=psw_sb[:], in_=pswap_hbm[:]))

        @blk.vector
        def _(v):
            sem_b.wait(v)
            sem_g.wait(v)
            rcol = drow[0:1, 0:1]
            v.reciprocal(out=c220_sb[:], in_=c220_sb[:])
            v.drain()
            v.tensor_tensor(out=frq_sb[:, 0:1], in0=th_sb[:], in1=c220_sb[:], op=OP.mult)
            v.drain()
            v.tensor_scalar(out=frq_sb[:, 1:2], in0=frq_sb[:, 0:1], scalar1=220.0,
                            scalar2=None, op0=OP.mult)
            v.drain()
            v.tensor_tensor(out=frq_sb[:, 1:2], in0=th_sb[:], in1=frq_sb[:, 1:2],
                            op=OP.subtract)
            v.drain()
            v.tensor_tensor(out=frq_sb[:, 1:2], in0=frq_sb[:, 1:2], in1=c220_sb[:],
                            op=OP.mult)
            v.drain()
            v.tensor_tensor(out=frq_sb[:, 0:1], in0=frq_sb[:, 0:1], in1=frq_sb[:, 1:2],
                            op=OP.add)
            v.drain()
            sem_v.inc(v.tensor_tensor(out=frq_sb[:, 1:2], in0=frq_sb[:, 0:1],
                                      in1=frep_sb[:], op=OP.mult), 1)

        for q in range(NQ):
            tsl = slice(q * W, (q + 1) * W)

            def xq(f):
                return x_q[:, f * W:(f + 1) * W]

            def xnq(f):
                return xn_q[:, f * W:(f + 1) * W]

            @blk.sync
            def _(s, tsl=tsl, q=q):
                sem_v.wait(s)
                for f in range(NF):
                    sem_x.inc(s.dma_start(out=xq(f), in_=x_hbm[f * 128:(f + 1) * 128, tsl]))

            @blk.gpsimd
            def _(g, q=q):
                sem_v.wait(g)
                sem_g.inc(g.iota(trow[:], pattern=[[1, W]], base=q * W,
                                 channel_multiplier=0,
                                 allow_small_or_imprecise_dtypes=True), 1)

            @blk.vector
            def _(v, q=q):
                sem_g.wait(v)
                sem_a.wait(v)
                v.tensor_scalar(out=ang[:], in0=trow[:], scalar1=frq_sb[:, 1:2],
                                scalar2=None, op0=OP.mult)
                v.drain()
                v.tensor_scalar(out=krnd[:], in0=ang[:], scalar1=1.0 / TWO_PI,
                                scalar2=MAGIC, op0=OP.mult, op1=OP.add)
                v.drain()
                v.tensor_scalar(out=krnd[:], in0=krnd[:], scalar1=MAGIC,
                                scalar2=None, op0=OP.subtract)
                v.drain()
                v.tensor_scalar(out=red[:], in0=krnd[:], scalar1=c1, scalar2=None, op0=OP.mult)
                v.drain()
                v.tensor_tensor(out=red[:], in0=ang[:], in1=red[:], op=OP.subtract)
                v.drain()
                v.tensor_scalar(out=r1[:], in0=krnd[:], scalar1=c2, scalar2=None, op0=OP.mult)
                v.drain()
                v.tensor_tensor(out=red[:], in0=red[:], in1=r1[:], op=OP.subtract)
                v.drain()
                v.tensor_scalar(out=r1[:], in0=krnd[:], scalar1=c3, scalar2=None, op0=OP.mult)
                v.drain()
                v.tensor_tensor(out=red[:], in0=red[:], in1=r1[:], op=OP.subtract)
                v.drain()
                PI = 3.14159265358979
                v.tensor_scalar(out=r1[:], in0=red[:], scalar1=PI, scalar2=None, op0=OP.is_gt)
                v.tensor_scalar(out=r2[:], in0=red[:], scalar1=-PI, scalar2=None, op0=OP.is_lt)
                v.drain()
                v.tensor_tensor(out=r1[:], in0=r2[:], in1=r1[:], op=OP.subtract)
                v.drain()
                v.scalar_tensor_tensor(out=r1[:], in0=r1[:], scalar=TWO_PI, in1=red[:],
                                       op0=OP.mult, op1=OP.add)
                v.tensor_scalar(out=r2[:], in0=red[:], scalar1=1.5707963267948966,
                                scalar2=None, op0=OP.add)
                v.drain()
                v.tensor_scalar(out=red[:], in0=r2[:], scalar1=PI, scalar2=None, op0=OP.is_gt)
                v.tensor_scalar(out=krnd[:], in0=r2[:], scalar1=-PI, scalar2=None, op0=OP.is_lt)
                v.drain()
                v.tensor_tensor(out=red[:], in0=krnd[:], in1=red[:], op=OP.subtract)
                v.drain()
                i = v.scalar_tensor_tensor(out=r2[:], in0=red[:], scalar=TWO_PI, in1=r2[:],
                                           op0=OP.mult, op1=OP.add)
                sem_v.inc(i, 1)

            @blk.scalar
            def _(a, q=q):
                sem_v.wait(a)
                a.activation(out=sinb[:], in_=r1[:], func=AF.Sin)
                sem_a.inc(a.activation(out=cosb[:], in_=r2[:], func=AF.Sin), 1)

            @blk.vector
            def _(v, q=q):
                sem_a.wait(v)
                sem_v.inc(v.tensor_scalar(out=cs2[:], in0=sinb[:], scalar1=sgn_sb[:, 0:1],
                                          scalar2=None, op0=OP.mult), 1)

            @blk.scalar
            def _(a, q=q):
                sem_x.wait(a)
                sem_mm.wait(a)
                sem_v.wait(a)
                sem_a.inc(ln_squares(a, ln, [xq(f) for f in range(NF)], W), 1)

            @blk.tensor
            def _(t, q=q):
                sem_a.wait(t)
                if q == 0:
                    sem_g.wait(t)
                    sem_w.wait(t)
                sem_mm.inc(ln_stats(t, ln, [xq(f) for f in range(NF)], W), 1)

            @blk.vector
            def _(v, q=q):
                sem_mm.wait(v)
                sem_v.inc(ln_rows(v, ln, W), 1)

            @blk.scalar
            def _(a, q=q):
                sem_v.wait(a)
                sem_a.inc(ln_rstd(a, ln, W), 1)

            @blk.tensor
            def _(t, q=q):
                sem_a.wait(t)
                sem_mm.inc(ln_bcast(t, ln, W), 1)

            @blk.vector
            def _(v, q=q):
                sem_mm.wait(v)
                for f in range(NF):
                    v.tensor_tensor(out=ln.sq[:, f * W:(f + 1) * W], in0=xq(f),
                                    in1=ln.mb[:, 0:W], op=OP.subtract)
                v.drain()
                for f in range(NF):
                    v.tensor_tensor(out=ln.sq[:, f * W:(f + 1) * W],
                                    in0=ln.sq[:, f * W:(f + 1) * W],
                                    in1=ln.rb[:, 0:W], op=OP.mult)
                v.drain()
                for f in range(NF):
                    i = v.tensor_scalar(out=xnq(f), in0=ln.sq[:, f * W:(f + 1) * W],
                                        scalar1=lnw_sb[:, f:f + 1],
                                        scalar2=lnb_sb[:, f:f + 1],
                                        op0=OP.mult, op1=OP.add)
                sem_v.inc(i, 1)

            for which, wsb, dst in (("q", wq_sb, fq), ("k", wk_sb, fk)):
                for r in range(4):
                    mmb = sem_mm.n
                    vb = sem_v.n

                    @blk.tensor
                    def _(t, q=q, which=which, wsb=wsb, r=r, vb=vb):
                        sem_v.wait(t)
                        for f in range(NF):
                            i = t.matmul(pu[0], wsb[:, f * 512 + r * 128:f * 512 + (r + 1) * 128],
                                         xnq(f), start=(f == 0), stop=(f == NF - 1))
                        sem_mm.inc(i, 1)

                    @blk.vector
                    def _(v, q=q, r=r, mmb=mmb):
                        v.wait_ge(sem_mm.h, mmb + 1)
                        sem_v.inc(v.tensor_copy(out=qf[:], in_=pu[0]), 1)

                    mmb2 = sem_mm.n
                    vb2 = sem_v.n

                    @blk.tensor
                    def _(t, q=q, r=r, vb2=vb2):
                        t.wait_ge(sem_v.h, vb2)
                        sem_mm.inc(t.matmul(bQ[:, 0:W], psw_sb[:], qf[:],
                                            start=True, stop=True), 1)

                    mmb3 = sem_mm.n

                    @blk.vector
                    def _(v, q=q, which=which, dst=dst, r=r, mmb3=mmb3, tsl=tsl):
                        v.wait_ge(sem_mm.h, mmb3)
                        v.tensor_tensor(out=r1[:], in0=qf[:], in1=cosb[:], op=OP.mult)
                        v.tensor_tensor(out=r2[:], in0=bQ[:, 0:W], in1=cs2[:], op=OP.mult)
                        v.drain()
                        i = v.tensor_tensor(out=dst[:, r * T + q * W:r * T + (q + 1) * W],
                                            in0=r1[:], in1=r2[:], op=OP.add)
                        sem_v.inc(i, 1)

            for j in range(4):
                mmb = sem_mm.n
                vb = sem_v.n

                @blk.tensor
                def _(t, q=q, j=j, vb=vb):
                    sem_v.wait(t)
                    for f in range(NF):
                        i = t.matmul(pu[0], xnq(f)[:, j * 128:(j + 1) * 128],
                                     wv_sb[:, f * 512:(f + 1) * 512],
                                     start=(f == 0), stop=(f == NF - 1))
                    sem_mm.inc(i, 1)

                @blk.vector
                def _(v, q=q, j=j, mmb=mmb, tsl=tsl):
                    v.wait_ge(sem_mm.h, mmb + 1)
                    jj = q * 4 + j
                    sem_v.inc(v.tensor_copy(out=fv[:, jj * 512:(jj + 1) * 512],
                                            in_=pu[0]), 1)

        mmb = sem_mm.n
        vb = sem_v.n

        @blk.tensor
        def _(t, vb=vb):
            t.wait_ge(sem_v.h, vb)
            for jj in range(32):
                i = t.matmul(ln.s2[:, 0:512], ln.ones_cb[:], fv[:, jj * 512:(jj + 1) * 512],
                             start=(jj == 0), stop=(jj == 31))
            sem_mm.inc(i, 1)

        @blk.vector
        def _(v, mmb=mmb):
            v.wait_ge(sem_mm.h, mmb + 1)
            sem_v.inc(v.tensor_scalar(out=csum[:], in0=ln.s2[0:1, 0:512], scalar1=0.5,
                                      scalar2=None, op0=OP.mult), 1)

        SC = 0.125
        ASC = SC / 1.4142135623730951

        for qc in range(NQ):
            qsl = slice(qc * W, (qc + 1) * W)
            for h in range(8):
                pr, par = h // 2, (h % 2) * 64
                hv = slice(par, par + 64)
                a_base = sem_a.n
                sc_marks = {}

                @blk.tensor
                def _(t, h=h, qc=qc, pr=pr, par=par, hv=hv, a_base=a_base,
                      sc_marks=sc_marks):
                    sem_v.wait(t)
                    t.matmul(outp_ps[:], csum[0:1, h * 64:(h + 1) * 64], ones_r512[:],
                             start=True, stop=False)
                    for kt in range(33):
                        if kt < 32:
                            if kt >= 2:
                                t.wait_ge(sem_a.h, a_base + kt - 1)
                            i = t.matmul(sc_ps[kt % 2],
                                         fk[:, pr * T + kt * 128:pr * T + (kt + 1) * 128][hv, :],
                                         fq[:, pr * T + qc * W:pr * T + (qc + 1) * W][hv, :],
                                         start=True, stop=True)
                            sem_mm.inc(i, 1)
                            sc_marks[kt] = sem_mm.n
                        kd = kt - 1
                        if kd >= 0:
                            t.wait_ge(sem_a.h, a_base + kd + 1)
                            t.matmul(outp_ps[:],
                                     fv[:, kd * 512 + h * 64:kd * 512 + (h + 1) * 64],
                                     wt[:, kd % 2 * W:(kd % 2 + 1) * W],
                                     start=False, stop=(kd == 31))
                            i = t.matmul(den_ps, ln.ones_cb[:],
                                         wt[:, kd % 2 * W:(kd % 2 + 1) * W],
                                         start=(kd == 0), stop=(kd == 31))
                            sem_mm.inc(i, 1)

                @blk.scalar
                def _(a, h=h, qc=qc, a_base=a_base, sc_marks=sc_marks):
                    for kt in range(32):
                        a.wait_ge(sem_mm.h, sc_marks[kt])
                        sem_a.inc(a.activation(out=wt[:, kt % 2 * W:(kt % 2 + 1) * W],
                                               in_=sc_ps[kt % 2], func=AF.Square,
                                               scale=ASC, bias=bsq_sb[:, 0:1]), 1)

                mmb4 = sem_mm.n

                @blk.vector
                def _(v, h=h, qc=qc, mmb4=mmb4):
                    v.wait_ge(sem_mm.h, mmb4)
                    v.tensor_scalar(out=drow[:], in0=den_ps[0:1, 0:W], scalar1=0.5 * T,
                                    scalar2=None, op0=OP.add)
                    v.drain()
                    v.reciprocal(out=drow[:], in_=drow[:])
                    v.drain()
                    sem_v.inc(v.tensor_copy(out=o_fp[:], in_=outp_ps[:]), 1)

                vb5 = sem_v.n

                @blk.tensor
                def _(t, h=h, qc=qc, vb5=vb5):
                    t.wait_ge(sem_v.h, vb5)
                    sem_mm.inc(t.matmul(rdb_ps, ln.ones_r[0:1, 0:64], drow[:],
                                        start=True, stop=True), 1)

                mmb6 = sem_mm.n

                @blk.vector
                def _(v, h=h, qc=qc, pr=pr, par=par, mmb6=mmb6):
                    v.wait_ge(sem_mm.h, mmb6)
                    if h == 0:
                        sem_st.wait(v)
                    sem_v.inc(v.tensor_tensor(
                        out=ho_sb[par:par + 64, pr * W:(pr + 1) * W],
                        in0=o_fp[:], in1=rdb_ps, op=OP.mult), 1)

            @blk.sync
            def _(s, qsl=qsl, qc=qc):
                sem_v.wait(s)
                for r in range(4):
                    sem_st.inc(s.dma_start(out=ho_hbm[r * 128:(r + 1) * 128, qsl],
                                           in_=ho_sb[:, r * W:(r + 1) * W]))

        @blk.sync
        def _(s):
            sem_st.wait(s)

    return p


def _freqs_base_np():
    import jax
    import jax.numpy as jnp
    cpu = jax.devices("cpu")[0]
    with jax.default_device(cpu):
        top = 2595.0 * jnp.log10(jnp.asarray(1.0 + 4000.0 / 200.0, dtype=jnp.float32))
        mel = jnp.power(10.0, jnp.linspace(0.0, top, 32, dtype=jnp.float32) / 2595.0) - 1.0
        fb = 200.0 * mel / 1000.0
    return np.asarray(fb, np.float32)


def sub_l1_attn_a(x, inp):
    w_q = np.asarray(inp["L1_attn_a_w_q"])
    w_kv = np.asarray(inp["L1_attn_a_w_kv"])
    w_o = np.ascontiguousarray(np.asarray(inp["L1_attn_a_w_o"]))
    lnw = _vec128(np.asarray(inp["L1_attn_a_ln_w"]), NF)
    lnb = _vec128(np.asarray(inp["L1_attn_a_ln_b"]), NF)
    theta = np.full((128, 1), np.asarray(inp["L1_attn_a_theta"]), np.float32)
    fb = _freqs_base_np()
    frep = np.ascontiguousarray(np.tile(fb, 4)[:, None])
    sgn = np.ascontiguousarray(
        np.concatenate([np.full(32, -1.0), np.full(32, 1.0)] * 2).astype(np.float32)[:, None])
    psw = np.zeros((128, 128), np.float32)
    for i in range(128):
        blk64, r = i // 64, i % 64
        psw[i, blk64 * 64 + (r + 32) % 64] = 1.0
    perm = np.concatenate([np.concatenate([h * 64 + 2 * np.arange(32),
                                           h * 64 + 2 * np.arange(32) + 1])
                           for h in range(8)])
    in_maps = []
    for c in range(8):
        b, s = c // 2, c % 2
        cols = slice(s * 512, (s + 1) * 512)
        wq_sl = np.ascontiguousarray(w_q[:, cols][:, perm])
        wk_sl = np.ascontiguousarray(w_kv[:, 0:E][:, cols][:, perm])
        wv_sl = np.ascontiguousarray(w_kv[:, E:2 * E][:, cols])
        in_maps.append({"x": np.ascontiguousarray(x[b]), "lnw": lnw, "lnb": lnb,
                        "wq": wq_sl, "wk": wk_sl, "wv": wv_sl,
                        "theta128": theta, "frep": frep, "sgn": sgn, "pswap": psw})
    outs = _run("l1aap1", build_l1_attn_a_p1, in_maps, ["ho"])
    ho_full = [np.concatenate([outs[2 * b][0], outs[2 * b + 1][0]], axis=0)
               for b in range(B)]
    in_maps = []
    for c in range(8):
        b, s = c // 2, c % 2
        tsl = slice(s * TC, (s + 1) * TC)
        in_maps.append({"ho": np.ascontiguousarray(ho_full[b][:, tsl]),
                        "w": w_o, "x": np.ascontiguousarray(x[b][:, tsl])})
    outs = _run("comb_nb", lambda st: build_combine(st, False, name="comb_nb"),
                in_maps, ["y"])
    xo = np.empty_like(x)
    for c in range(8):
        b, s = c // 2, c % 2
        xo[b][:, s * TC:(s + 1) * TC] = outs[c][0]
    return xo



def build_l0_attn_b_p1(stack, name="l0abp1", nq_main=None, nq_pro=None, nh=8, stage=3):
    p = Prog(stack, name)
    nc = p.nc
    W = 512
    NQ = T // W
    SC = 0.125
    POLY = (-0.037, 0.481, -2.3, 4.917, -1.791)
    x_hbm = p.din("x", [E, T])
    wq_hbm = p.din("wq", [E, 512])
    wk_hbm = p.din("wk", [E, 512])
    wv_hbm = p.din("wv", [E, 512])
    dmask_hbm = p.din("dmask", [128, 4 * W])
    ho_hbm = p.dout("ho", [512, T], BF16)

    x_q = p.sb("x_q", [128, NF * W], F32)
    xn_q = p.sb("xn_q", [128, NF * W], BF16)
    fq = p.sb("fq", [128, 4 * W], BF16)
    fk = p.sb("fk", [128, 4 * T], BF16)
    fv = p.sb("fv", [128, 4 * T], BF16)
    lbuf = p.sb("lbuf", [128, 32 * W], BF16)
    et = p.sb("et", [128, 2 * W], BF16)
    elt = p.sb("elt", [128, W], BF16)
    pt = p.sb("pt", [128, 2 * W], BF16)
    lb2 = p.sb("lb2", [128, 2 * W], BF16)
    ho_sb = p.sb("ho_sb", [128, 4 * W], BF16)
    wq_sb = p.sb("wq_sb", [128, NF * 512], BF16)
    wk_sb = p.sb("wk_sb", [128, NF * 512], BF16)
    wv_sb = p.sb("wv_sb", [128, NF * 512], BF16)
    dm_sb = p.sb("dm_sb", [128, 4 * W], F32)
    o_fp = p.sb("o_fp", [64, W], F32)
    rowa = p.sb("rowa", [1, 4 * W], F32)
    drow = p.sb("drow", [1, W], F32)

    bA = p.ps("bA", [128, W], F32)
    bB = p.ps("bB", [128, W], F32)
    bC = p.ps("bC", [128, W], F32)
    bZ = p.ps("bZ", [1, W], F32)
    bE = p.ps("bE", [1, W], F32)
    bM = p.ps("bM", [128, W], F32)
    bR = p.ps("bR", [128, W], F32)
    l_ps = [bA[:, 0:W], bB[:, 0:W]]
    outp_ps = bC[0:64, 0:W]
    z_ps = bZ[0:1, 0:W]
    e_ps = bE[0:1, 0:W]
    beta_ps = bM[:, 0:W]
    rdb_ps = bR[0:64, 0:W]
    ln = LNTiles(p, W, s1=z_ps, s2=e_ps, mb=bM[:, 0:W], rb=bR[:, 0:W])

    sem_w = p.sem("sem_w")
    sem_x = p.sem("sem_x")
    sem_mm = p.sem("sem_mm")
    sem_v = p.sem("sem_v")
    sem_a = p.sem("sem_a")
    sem_g = p.sem("sem_g")
    sem_st = p.sem("sem_st")
    sem_b = p.sem("sem_b")

    with nc.Block() as blk:
        @blk.gpsimd
        def _(g):
            for f in range(NF):
                sem_w.inc(g.dma_start(out=wq_sb[:, f * 512:(f + 1) * 512],
                                      in_=wq_hbm[f * 128:(f + 1) * 128, :]))
                sem_w.inc(g.dma_start(out=wk_sb[:, f * 512:(f + 1) * 512],
                                      in_=wk_hbm[f * 128:(f + 1) * 128, :]))
                sem_w.inc(g.dma_start(out=wv_sb[:, f * 512:(f + 1) * 512],
                                      in_=wv_hbm[f * 128:(f + 1) * 128, :]))
            sem_g.inc(ln.init(g), 1)

        @blk.sync
        def _(s):
            sem_b.inc(s.dma_start(out=dm_sb[:], in_=dmask_hbm[:]))

        def emit_ln_and_xn(q):
            tsl = slice(q * W, (q + 1) * W)

            def xq(f):
                return x_q[:, f * W:(f + 1) * W]

            def xnq(f):
                return xn_q[:, f * W:(f + 1) * W]

            @blk.sync
            def _(s, tsl=tsl, q=q):
                sem_v.wait(s)
                sem_a.wait(s)
                for f in range(NF):
                    sem_x.inc(s.dma_start(out=xq(f), in_=x_hbm[f * 128:(f + 1) * 128, tsl]))

            @blk.scalar
            def _(a, q=q):
                sem_x.wait(a)
                sem_mm.wait(a)
                sem_v.wait(a)
                sem_a.inc(ln_squares(a, ln, [xq(f) for f in range(NF)], W), 1)

            @blk.tensor
            def _(t, q=q):
                sem_a.wait(t)
                if q == 0:
                    sem_g.wait(t)
                    sem_w.wait(t)
                    sem_b.wait(t)
                sem_mm.inc(ln_stats(t, ln, [xq(f) for f in range(NF)], W), 1)

            @blk.vector
            def _(v, q=q):
                sem_mm.wait(v)
                sem_v.inc(ln_rows(v, ln, W), 1)

            @blk.scalar
            def _(a, q=q):
                sem_v.wait(a)
                sem_a.inc(ln_rstd(a, ln, W), 1)

            @blk.tensor
            def _(t, q=q):
                sem_a.wait(t)
                sem_mm.inc(ln_bcast(t, ln, W), 1)

            @blk.vector
            def _(v, q=q):
                sem_mm.wait(v)
                for f in range(NF):
                    v.tensor_tensor(out=ln.sq[:, f * W:(f + 1) * W], in0=xq(f),
                                    in1=ln.mb[:, 0:W], op=OP.subtract)
                v.drain()
                for f in range(NF):
                    i = v.tensor_tensor(out=xnq(f), in0=ln.sq[:, f * W:(f + 1) * W],
                                        in1=ln.rb[:, 0:W], op=OP.mult)
                sem_v.inc(i, 1)

        for q in range(NQ if nq_pro is None else nq_pro):
            emit_ln_and_xn(q)

            def xnq(f):
                return xn_q[:, f * W:(f + 1) * W]

            for r in range(4):
                mmb = sem_mm.n

                @blk.tensor
                def _(t, q=q, r=r):
                    sem_v.wait(t)
                    for f in range(NF):
                        i = t.matmul(l_ps[0], wk_sb[:, f * 512 + r * 128:f * 512 + (r + 1) * 128],
                                     xnq(f), start=(f == 0), stop=(f == NF - 1))
                    sem_mm.inc(i, 1)

                @blk.vector
                def _(v, q=q, r=r, mmb=mmb):
                    v.wait_ge(sem_mm.h, mmb + 1)
                    sem_v.inc(v.tensor_copy(out=fk[:, r * T + q * W:r * T + (q + 1) * W],
                                            in_=l_ps[0]), 1)

            for j in range(4):
                mmb = sem_mm.n

                @blk.tensor
                def _(t, q=q, j=j):
                    sem_v.wait(t)
                    for f in range(NF):
                        i = t.matmul(l_ps[0], xnq(f)[:, j * 128:(j + 1) * 128],
                                     wv_sb[:, f * 512:(f + 1) * 512],
                                     start=(f == 0), stop=(f == NF - 1))
                    sem_mm.inc(i, 1)

                @blk.vector
                def _(v, q=q, j=j, mmb=mmb):
                    v.wait_ge(sem_mm.h, mmb + 1)
                    jj = q * 4 + j
                    sem_v.inc(v.tensor_copy(out=fv[:, jj * 512:(jj + 1) * 512],
                                            in_=l_ps[0]), 1)

        for qc in range(NQ if nq_main is None else nq_main):
            qsl = slice(qc * W, (qc + 1) * W)
            NKT = qc * 4 + 4
            emit_ln_and_xn(qc)

            def xnq(f):
                return xn_q[:, f * W:(f + 1) * W]

            for r in range(4):
                mmb = sem_mm.n

                @blk.tensor
                def _(t, qc=qc, r=r):
                    sem_v.wait(t)
                    for f in range(NF):
                        i = t.matmul(l_ps[0], wq_sb[:, f * 512 + r * 128:f * 512 + (r + 1) * 128],
                                     xnq(f), start=(f == 0), stop=(f == NF - 1))
                    sem_mm.inc(i, 1)

                @blk.vector
                def _(v, qc=qc, r=r, mmb=mmb):
                    v.wait_ge(sem_mm.h, mmb + 1)
                    sem_v.inc(v.tensor_copy(out=fq[:, r * W:(r + 1) * W], in_=l_ps[0]), 1)

            for h in range(nh):
                pr, par = h // 2, (h % 2) * 64
                hv = slice(par, par + 64)
                a_base = sem_a.n
                v_base = sem_v.n
                sc_marks = {}

                @blk.tensor
                def _(t, qc=qc, h=h, pr=pr, hv=hv, NKT=NKT, a_base=a_base,
                      v_base=v_base, sc_marks=sc_marks):
                    sem_v.wait(t)
                    for kt in range(NKT + 1):
                        if kt < NKT:
                            if kt >= 2:
                                t.wait_ge(sem_a.h, a_base + 2 * (kt - 2) + 2)
                            i = t.matmul(l_ps[kt % 2],
                                         fk[:, pr * T + kt * 128:pr * T + (kt + 1) * 128][hv, :],
                                         fq[:, pr * W:(pr + 1) * W][hv, :],
                                         start=True, stop=True)
                            sem_mm.inc(i, 1)
                            sc_marks[kt] = sem_mm.n
                        kd = kt - 1
                        if kd >= 0:
                            t.wait_ge(sem_v.h, v_base + kd + 1)
                            t.matmul(z_ps, ln.ones_cb[:], et[:, kd % 2 * W:(kd % 2 + 1) * W],
                                     start=(kd == 0), stop=(kd == NKT - 1))
                            i = t.matmul(e_ps, ln.ones_cb[:], elt[:],
                                         start=(kd == 0), stop=(kd == NKT - 1))
                            sem_mm.inc(i, 1)
                            sc_marks[("z", kd)] = sem_mm.n

                @blk.scalar
                def _(a, qc=qc, h=h, NKT=NKT, sc_marks=sc_marks):
                    for kt in range(NKT):
                        a.wait_ge(sem_mm.h, sc_marks[kt])
                        sem_a.inc(a.activation(out=et[:, kt % 2 * W:(kt % 2 + 1) * W],
                                               in_=l_ps[kt % 2], func=AF.Exp,
                                               scale=SC), 1)
                        sem_a.inc(a.activation(out=lbuf[:, kt * W:(kt + 1) * W],
                                               in_=l_ps[kt % 2], func=AF.Copy,
                                               scale=SC), 1)

                @blk.vector
                def _(v, qc=qc, h=h, NKT=NKT, a_base=a_base, sc_marks=sc_marks):
                    for kt in range(NKT):
                        d = kt - qc * 4
                        v.wait_ge(sem_a.h, a_base + 2 * kt + 2)
                        if kt >= 1:
                            v.wait_ge(sem_mm.h, sc_marks[("z", kt - 1)])
                        if d >= 0:
                            v.tensor_tensor(out=et[:, kt % 2 * W:(kt % 2 + 1) * W],
                                            in0=et[:, kt % 2 * W:(kt % 2 + 1) * W],
                                            in1=dm_sb[:, d * W:(d + 1) * W], op=OP.mult)
                            v.drain()
                        i = v.tensor_tensor(out=elt[:], in0=et[:, kt % 2 * W:(kt % 2 + 1) * W],
                                            in1=lbuf[:, kt * W:(kt + 1) * W], op=OP.mult)
                        sem_v.inc(i, 1)

                if stage < 2:
                    continue
                mm3 = sem_mm.n

                @blk.vector
                def _(v, qc=qc, h=h, mm3=mm3):
                    v.wait_ge(sem_mm.h, mm3)
                    t1 = rowa[0:1, 2 * W:3 * W]
                    t2 = rowa[0:1, 3 * W:4 * W]
                    v.reciprocal(out=t1, in_=z_ps)
                    v.drain()
                    v.tensor_tensor(out=t2, in0=e_ps, in1=t1, op=OP.mult)
                    sem_v.inc(v.drain(), 1)

                @blk.scalar
                def _(a, qc=qc, h=h):
                    sem_v.wait(a)
                    sem_a.inc(a.activation(out=rowa[0:1, 0:W], in_=z_ps, func=AF.Ln), 1)

                @blk.vector
                def _(v, qc=qc, h=h):
                    sem_a.wait(v)
                    Hrow = rowa[0:1, 0:W]
                    brow = rowa[0:1, W:2 * W]
                    t2 = rowa[0:1, 3 * W:4 * W]
                    v.tensor_tensor(out=Hrow, in0=Hrow, in1=t2, op=OP.subtract)
                    v.drain()
                    v.tensor_scalar(out=brow, in0=Hrow, scalar1=POLY[0], scalar2=POLY[1],
                                    op0=OP.mult, op1=OP.add)
                    v.drain()
                    for cc in POLY[2:]:
                        v.tensor_tensor(out=brow, in0=brow, in1=Hrow, op=OP.mult)
                        v.drain()
                        v.tensor_scalar(out=brow, in0=brow, scalar1=cc, scalar2=None,
                                        op0=OP.add)
                        v.drain()
                    v.tensor_scalar(out=brow, in0=brow, scalar1=1.0, scalar2=None,
                                    op0=OP.max)
                    v.tensor_scalar(out=t2, in0=Hrow, scalar1=0.5, scalar2=None,
                                    op0=OP.is_gt)
                    v.drain()
                    v.tensor_scalar(out=brow, in0=brow, scalar1=1.0, scalar2=None,
                                    op0=OP.subtract)
                    v.drain()
                    v.tensor_tensor(out=brow, in0=brow, in1=t2, op=OP.mult)
                    v.drain()
                    i = v.tensor_scalar(out=brow, in0=brow, scalar1=1.0, scalar2=None,
                                        op0=OP.add)
                    sem_v.inc(i, 1)

                vb4 = sem_v.n

                @blk.tensor
                def _(t, qc=qc, h=h, vb4=vb4):
                    t.wait_ge(sem_v.h, vb4)
                    sem_mm.inc(t.matmul(beta_ps, ln.ones_r[:], rowa[0:1, W:2 * W],
                                        start=True, stop=True), 1)

                if stage < 3:
                    continue
                mm4 = sem_mm.n
                a2_base = sem_a.n
                v2_base = sem_v.n

                @blk.vector
                def _(v, qc=qc, h=h, NKT=NKT, mm4=mm4, a2_base=a2_base,
                      v2_base=v2_base):
                    v.wait_ge(sem_mm.h, mm4)
                    for kt in range(NKT):
                        if kt >= 2:
                            v.wait_ge(sem_a.h, a2_base + (kt - 2) + 1)
                        i = v.tensor_tensor(out=lb2[:, kt % 2 * W:(kt % 2 + 1) * W],
                                            in0=lbuf[:, kt * W:(kt + 1) * W],
                                            in1=beta_ps, op=OP.mult)
                        sem_v.inc(i, 1)

                @blk.scalar
                def _(a, qc=qc, h=h, NKT=NKT, mm4=mm4, v2_base=v2_base):
                    for kt in range(NKT):
                        a.wait_ge(sem_v.h, v2_base + kt + 1)
                        if kt >= 2:
                            a.wait_ge(sem_mm.h, mm4 + (kt - 2) + 1)
                        sem_a.inc(a.activation(out=pt[:, kt % 2 * W:(kt % 2 + 1) * W],
                                               in_=lb2[:, kt % 2 * W:(kt % 2 + 1) * W],
                                               func=AF.Exp), 1)

                v3_base = sem_v.n

                @blk.vector
                def _(v, qc=qc, h=h, NKT=NKT, a2_base=a2_base, mm4=mm4):
                    for idx, kt in enumerate(range(qc * 4, NKT)):
                        v.wait_ge(sem_a.h, a2_base + kt + 1)
                        i = v.tensor_tensor(out=pt[:, kt % 2 * W:(kt % 2 + 1) * W],
                                            in0=pt[:, kt % 2 * W:(kt % 2 + 1) * W],
                                            in1=dm_sb[:, idx * W:(idx + 1) * W], op=OP.mult)
                        sem_v.inc(i, 1)

                @blk.tensor
                def _(t, qc=qc, h=h, NKT=NKT, a2_base=a2_base, v3_base=v3_base):
                    for kt in range(NKT):
                        d = kt - qc * 4
                        if d >= 0:
                            t.wait_ge(sem_v.h, v3_base + d + 1)
                        else:
                            t.wait_ge(sem_a.h, a2_base + kt + 1)
                        t.matmul(outp_ps[:], fv[:, kt * 512 + h * 64:kt * 512 + (h + 1) * 64],
                                 pt[:, kt % 2 * W:(kt % 2 + 1) * W],
                                 start=(kt == 0), stop=(kt == NKT - 1))
                        i = t.matmul(z_ps, ln.ones_cb[:], pt[:, kt % 2 * W:(kt % 2 + 1) * W],
                                     start=(kt == 0), stop=(kt == NKT - 1))
                        sem_mm.inc(i, 1)

                mm6 = sem_mm.n


                @blk.vector
                def _(v, qc=qc, h=h, mm6=mm6):
                    v.wait_ge(sem_mm.h, mm6)
                    v.reciprocal(out=drow[:], in_=z_ps)
                    v.drain()
                    sem_v.inc(v.tensor_copy(out=o_fp[:], in_=outp_ps[:]), 1)

                vb7 = sem_v.n

                @blk.tensor
                def _(t, qc=qc, h=h, vb7=vb7):
                    t.wait_ge(sem_v.h, vb7)
                    sem_mm.inc(t.matmul(rdb_ps, ln.ones_r[0:1, 0:64], drow[:],
                                        start=True, stop=True), 1)

                mm7 = sem_mm.n

                @blk.vector
                def _(v, qc=qc, h=h, pr=pr, par=par, mm7=mm7):
                    v.wait_ge(sem_mm.h, mm7)
                    if h == 0:
                        sem_st.wait(v)
                    sem_v.inc(v.tensor_tensor(
                        out=ho_sb[par:par + 64, pr * W:(pr + 1) * W],
                        in0=o_fp[:], in1=rdb_ps, op=OP.mult), 1)

            @blk.sync
            def _(s, qsl=qsl, qc=qc):
                sem_v.wait(s)
                for r in range(4):
                    sem_st.inc(s.dma_start(out=ho_hbm[r * 128:(r + 1) * 128, qsl],
                                           in_=ho_sb[:, r * W:(r + 1) * W]))

        @blk.sync
        def _(s):
            sem_st.wait(s)

    return p


def sub_l0_attn_b(x, inp):
    wqT = np.ascontiguousarray(np.asarray(inp["L0_attn_b_Wq"]).T)
    wkT = np.ascontiguousarray(np.asarray(inp["L0_attn_b_Wk"]).T)
    wvT = np.ascontiguousarray(np.asarray(inp["L0_attn_b_Wv"]).T)
    woT = np.ascontiguousarray(np.asarray(inp["L0_attn_b_Wo"]).T)
    bo = _vec128(np.asarray(inp["L0_attn_b_bo"]), NF)
    dm = np.zeros((128, 4 * 512), np.float32)
    for d in range(4):
        s_idx = 128 * d + np.arange(128)[:, None]
        dm[:, d * 512:(d + 1) * 512] = (s_idx <= np.arange(512)[None, :])
    in_maps = []
    for c in range(8):
        b, s = c // 2, c % 2
        cols = slice(s * 512, (s + 1) * 512)
        in_maps.append({"x": np.ascontiguousarray(x[b]),
                        "wq": np.ascontiguousarray(wqT[:, cols]),
                        "wk": np.ascontiguousarray(wkT[:, cols]),
                        "wv": np.ascontiguousarray(wvT[:, cols]),
                        "dmask": dm})
    outs = _run("l0abp1", build_l0_attn_b_p1, in_maps, ["ho"])
    ho_full = [np.concatenate([outs[2 * b][0], outs[2 * b + 1][0]], axis=0)
               for b in range(B)]
    in_maps = []
    for c in range(8):
        b, s = c // 2, c % 2
        tsl = slice(s * TC, (s + 1) * TC)
        in_maps.append({"ho": np.ascontiguousarray(ho_full[b][:, tsl]),
                        "w": woT, "x": np.ascontiguousarray(x[b][:, tsl]),
                        "bo": bo})
    outs = _run("comb_b", lambda st: build_combine(st, True, name="comb_b"),
                in_maps, ["y"])
    xo = np.empty_like(x)
    for c in range(8):
        b, s = c // 2, c % 2
        xo[b][:, s * TC:(s + 1) * TC] = outs[c][0]
    return xo



def build_l1_attn_b_p1(stack, name="mbp1"):
    p = Prog(stack, name)
    nc = p.nc
    W = 512
    NQ = T // W
    x_hbm = p.din("x", [E, T])
    wxin_hbm = p.din("wxin", [E, 1024])
    wz_hbm = p.din("wz", [E, 1024])
    cw_hbm = p.din("cw", [128, 32])
    cb_hbm = p.din("cb", [128, 8])
    hc_hbm = p.dout("hc", [1024, T], BF16)
    sz_hbm = p.dout("sz", [1024, T], BF16)

    x_q = p.sb("x_q", [128, NF * W], F32)
    xn_q = p.sb("xn_q", [128, NF * W], BF16)
    xin_b = p.sb("xin_b", [128, NF * (W + 4)], F32)
    cacc = p.sb("cacc", [128, W], F32)
    hc_sb = p.sb("hc_sb", [128, NF * W], BF16)
    sz_sb = p.sb("sz_sb", [128, NF * W], BF16)
    wxin_sb = p.sb("wxin_sb", [128, NF * 1024], BF16)
    wz_sb = p.sb("wz_sb", [128, NF * 1024], BF16)
    cw_sb = p.sb("cw_sb", [128, 32], F32)
    cb_sb = p.sb("cb_sb", [128, 8], F32)
    ln = LNTiles(p, W)
    pu = [p.ps(f"pu{i}", [128, W], F32)[:, 0:W] for i in range(2)]
    pz = p.ps("pz", [128, W], F32)[:, 0:W]

    sem_w = p.sem("sem_w")
    sem_b = p.sem("sem_b")
    sem_x = p.sem("sem_x")
    sem_mm = p.sem("sem_mm")
    sem_v = p.sem("sem_v")
    sem_a = p.sem("sem_a")
    sem_g = p.sem("sem_g")
    sem_st = p.sem("sem_st")

    PW = W + 4

    with nc.Block() as blk:
        @blk.gpsimd
        def _(g):
            for f in range(NF):
                sem_w.inc(g.dma_start(out=wxin_sb[:, f * 1024:(f + 1) * 1024],
                                      in_=wxin_hbm[f * 128:(f + 1) * 128, :]))
                sem_w.inc(g.dma_start(out=wz_sb[:, f * 1024:(f + 1) * 1024],
                                      in_=wz_hbm[f * 128:(f + 1) * 128, :]))
            g.memset(xin_b[:], 0.0)
            sem_g.inc(ln.init(g), 1)

        @blk.sync
        def _(s):
            sem_b.inc(s.dma_start(out=cw_sb[:], in_=cw_hbm[:]))
            sem_b.inc(s.dma_start(out=cb_sb[:], in_=cb_hbm[:]))

        for q in range(NQ):
            tsl = slice(q * W, (q + 1) * W)

            def xq(f):
                return x_q[:, f * W:(f + 1) * W]

            def xnq(f):
                return xn_q[:, f * W:(f + 1) * W]

            @blk.sync
            def _(s, tsl=tsl, q=q):
                if q > 0:
                    sem_st.wait(s)
                for f in range(NF):
                    sem_x.inc(s.dma_start(out=xq(f), in_=x_hbm[f * 128:(f + 1) * 128, tsl]))

            @blk.scalar
            def _(a, q=q):
                sem_x.wait(a)
                sem_mm.wait(a)
                sem_v.wait(a)
                sem_a.inc(ln_squares(a, ln, [xq(f) for f in range(NF)], W), 1)

            @blk.tensor
            def _(t, q=q):
                sem_a.wait(t)
                if q == 0:
                    sem_g.wait(t)
                    sem_w.wait(t)
                    sem_b.wait(t)
                sem_mm.inc(ln_stats(t, ln, [xq(f) for f in range(NF)], W), 1)

            @blk.vector
            def _(v, q=q):
                sem_mm.wait(v)
                sem_v.inc(ln_rows(v, ln, W), 1)

            @blk.scalar
            def _(a, q=q):
                sem_v.wait(a)
                sem_a.inc(ln_rstd(a, ln, W), 1)

            @blk.tensor
            def _(t, q=q):
                sem_a.wait(t)
                sem_mm.inc(ln_bcast(t, ln, W), 1)

            @blk.vector
            def _(v, q=q):
                sem_mm.wait(v)
                for f in range(NF):
                    v.tensor_tensor(out=ln.sq[:, f * W:(f + 1) * W], in0=xq(f),
                                    in1=ln.mb[:, 0:W], op=OP.subtract)
                v.drain()
                for f in range(NF):
                    i = v.tensor_tensor(out=xnq(f), in0=ln.sq[:, f * W:(f + 1) * W],
                                        in1=ln.rb[:, 0:W], op=OP.mult)
                sem_v.inc(i, 1)

            for f2 in range(NF):
                mmb = sem_mm.n
                vb = sem_v.n

                @blk.tensor
                def _(t, q=q, f2=f2, vb=vb):
                    sem_v.wait(t)
                    for f in range(NF):
                        i = t.matmul(pu[f2 % 2],
                                     wxin_sb[:, f * 1024 + f2 * 128:f * 1024 + (f2 + 1) * 128],
                                     xnq(f), start=(f == 0), stop=(f == NF - 1))
                    sem_mm.inc(i, 1)

                @blk.vector
                def _(v, q=q, f2=f2, mmb=mmb):
                    v.wait_ge(sem_mm.h, mmb + 1)
                    sem_a.wait(v)
                    buf = xin_b[:, f2 * PW:(f2 + 1) * PW]
                    v.tensor_copy(out=ln.sq[:, 0:4], in_=buf[:, W:W + 4])
                    v.drain()
                    v.tensor_copy(out=buf[:, 0:4], in_=ln.sq[:, 0:4])
                    v.drain()
                    sem_v.inc(v.tensor_copy(out=buf[:, 4:4 + W], in_=pu[f2 % 2]), 1)
                    v.drain()
                    v.tensor_scalar(out=cacc[:], in0=buf[:, 1:1 + W],
                                    scalar1=cw_sb[:, f2 * 4:f2 * 4 + 1], scalar2=None,
                                    op0=OP.mult)
                    for j in (1, 2, 3):
                        v.drain()
                        v.scalar_tensor_tensor(out=cacc[:], in0=buf[:, 1 + j:1 + j + W],
                                               scalar=cw_sb[:, f2 * 4 + j:f2 * 4 + j + 1],
                                               in1=cacc[:], op0=OP.mult, op1=OP.add)
                    sem_v.inc(v.drain(), 1)

                vb2 = sem_v.n

                @blk.scalar
                def _(a, q=q, f2=f2, vb2=vb2):
                    a.wait_ge(sem_v.h, vb2)
                    sem_a.inc(a.activation(out=hc_sb[:, f2 * W:(f2 + 1) * W],
                                           in_=cacc[:], func=AF.Silu,
                                           bias=cb_sb[:, f2:f2 + 1]), 1)

            for f2 in range(NF):
                mmb = sem_mm.n

                @blk.tensor
                def _(t, q=q, f2=f2):
                    sem_a.wait(t)
                    for f in range(NF):
                        i = t.matmul(pz, wz_sb[:, f * 1024 + f2 * 128:f * 1024 + (f2 + 1) * 128],
                                     xnq(f), start=(f == 0), stop=(f == NF - 1))
                    sem_mm.inc(i, 1)

                @blk.scalar
                def _(a, q=q, f2=f2, mmb=mmb):
                    a.wait_ge(sem_mm.h, mmb + 1)
                    sem_a.inc(a.activation(out=sz_sb[:, f2 * W:(f2 + 1) * W],
                                           in_=pz, func=AF.Silu), 1)

            @blk.sync
            def _(s, tsl=tsl, q=q):
                sem_a.wait(s)
                sem_v.wait(s)
                for f in range(NF):
                    sem_st.inc(s.dma_start(out=hc_hbm[f * 128:(f + 1) * 128, tsl],
                                           in_=hc_sb[:, f * W:(f + 1) * W]))
                for f in range(NF):
                    sem_st.inc(s.dma_start(out=sz_hbm[f * 128:(f + 1) * 128, tsl],
                                           in_=sz_sb[:, f * W:(f + 1) * W]))

        @blk.sync
        def _(s):
            sem_st.wait(s)

    return p


def build_l1_attn_b_p2(stack, name="mbp2"):
    p = Prog(stack, name)
    nc = p.nc
    W = 512
    NQ = T // W
    NKF = 16
    hc_hbm = p.din("hc", [2048, T], BF16)
    sz_hbm = p.din("sz", [1024, T], BF16)
    wxd_hbm = p.din("wxd", [2048, 1024])
    wxbc_hbm = p.din("wxbc", [128, 16 * 32])
    alog_hbm = p.din("alog", [128, NF * 16])
    dd_hbm = p.din("dd", [128, NF])
    ident_hbm = p.din("ident", [128, 128])
    sel_hbm = p.din("sel", [32, 32 * 128])
    y_hbm = p.dout("y", [1024, T], BF16)

    hc_q = p.sb("hc_q", [128, NKF * W], BF16)
    sz_q = p.sb("sz_q", [128, NF * W], BF16)
    dl_q = p.sb("dl_q", [128, NF * W], F32)
    dsc = p.sb("dsc", [128, W], F32)
    dsc2 = p.sb("dsc2", [128, W], F32)
    bcs = p.sb("bcs", [32, W], BF16)
    Bb = p.sb("Bb", [128, 16 * W], BF16)
    Cb = p.sb("Cb", [128, 16 * W], BF16)
    dh_sb = p.sb("dh_sb", [128, W], F32)
    argb = p.sb("argb", [128, 2 * W], F32)
    abuf = p.sb("abuf", [128, 2 * W], F32)
    bub = p.sb("bub", [128, 2 * W], F32)
    hn_sb = p.sb("hn_sb", [128, W], F32)
    hcn = p.sb("hcn", [128, 2 * W], BF16)
    y_sb = p.sb("y_sb", [128, NF * W], BF16)
    st_sb = p.sb("st_sb", [128, NF * 16], F32)
    wxd_sb = p.sb("wxd_sb", [128, NKF * 1024], BF16)
    wxbc_sb = p.sb("wxbc_sb", [128, NKF * 32], BF16)
    acol = p.sb("acol", [128, NF * 16], F32)
    dcol = p.sb("dcol", [128, NF], F32)
    idb = p.sb("idb", [128, 128], BF16)
    sel_sb = p.sb("sel_sb", [32, 32 * 128], BF16)
    onec = p.sb("onec", [128, 1], F32)

    pu = [p.ps(f"pu{i}", [128, W], F32)[:, 0:W] for i in range(2)]
    pbc = p.ps("pbc", [32, W], F32)[0:32, 0:W]
    pbb = p.ps("pbb", [128, W], F32)[:, 0:W]
    py = p.ps("py", [128, W], F32)[:, 0:W]

    sem_w = p.sem("sem_w")
    sem_b = p.sem("sem_b")
    sem_x = p.sem("sem_x")
    sem_mm = p.sem("sem_mm")
    sem_v = p.sem("sem_v")
    sem_a = p.sem("sem_a")
    sem_g = p.sem("sem_g")
    sem_st = p.sem("sem_st")

    with nc.Block() as blk:
        @blk.gpsimd
        def _(g):
            for f in range(NKF):
                sem_w.inc(g.dma_start(out=wxd_sb[:, f * 1024:(f + 1) * 1024],
                                      in_=wxd_hbm[f * 128:(f + 1) * 128, :]))
            sem_w.inc(g.dma_start(out=wxbc_sb[:], in_=wxbc_hbm[:]))
            sem_w.inc(g.dma_start(out=idb[:], in_=ident_hbm[:]))
            sem_w.inc(g.dma_start(out=sel_sb[:], in_=sel_hbm[:]))
            g.memset(st_sb[:], 0.0)
            sem_g.inc(g.memset(onec[:], 1.0), 1)

        @blk.sync
        def _(s):
            sem_b.inc(s.dma_start(out=acol[:], in_=alog_hbm[:]))
            sem_b.inc(s.dma_start(out=dcol[:], in_=dd_hbm[:]))

        @blk.scalar
        def _(a):
            sem_b.wait(a)
            sem_a.inc(a.activation(out=acol[:], in_=acol[:], func=AF.Exp), 1)

        @blk.vector
        def _(v):
            sem_a.wait(v)
            sem_v.inc(v.tensor_scalar(out=acol[:], in0=acol[:], scalar1=-1.0,
                                      scalar2=None, op0=OP.mult), 1)

        for q in range(NQ):
            tsl = slice(q * W, (q + 1) * W)

            @blk.sync
            def _(s, tsl=tsl, q=q):
                if q > 0:
                    sem_st.wait(s)
                else:
                    sem_v.wait(s)
                for f in range(NKF):
                    sem_x.inc(s.dma_start(out=hc_q[:, f * W:(f + 1) * W],
                                          in_=hc_hbm[f * 128:(f + 1) * 128, tsl]))
                for f in range(NF):
                    sem_x.inc(s.dma_start(out=sz_q[:, f * W:(f + 1) * W],
                                          in_=sz_hbm[f * 128:(f + 1) * 128, tsl]))

            for f2 in range(NF):
                mmb = sem_mm.n

                @blk.tensor
                def _(t, q=q, f2=f2):
                    sem_x.wait(t)
                    sem_a.wait(t)
                    if q == 0 and f2 == 0:
                        sem_w.wait(t)
                        sem_g.wait(t)
                        sem_v.wait(t)
                    for f in range(NKF):
                        i = t.matmul(pu[f2 % 2],
                                     wxd_sb[:, f * 1024 + f2 * 128:f * 1024 + (f2 + 1) * 128],
                                     hc_q[:, f * W:(f + 1) * W],
                                     start=(f == 0), stop=(f == NKF - 1))
                    sem_mm.inc(i, 1)

                @blk.scalar
                def _(a, q=q, f2=f2, mmb=mmb):
                    a.wait_ge(sem_mm.h, mmb + 1)
                    a.wait_ge(sem_v.h, sem_v.n)
                    a.activation(out=dsc[:], in_=pu[f2 % 2], func=AF.Copy)
                    a.drain()
                    a.activation(out=dsc2[:], in_=dsc[:], func=AF.Abs)
                    a.drain()
                    a.activation(out=dsc2[:], in_=dsc2[:], func=AF.Exp, scale=-1.0)
                    a.drain()
                    sem_a.inc(a.activation(out=dsc2[:], in_=dsc2[:], func=AF.Ln,
                                           bias=onec[:, 0:1]), 1)

                @blk.vector
                def _(v, q=q, f2=f2):
                    sem_a.wait(v)
                    v.tensor_scalar(out=dsc[:], in0=dsc[:], scalar1=0.0, scalar2=None,
                                    op0=OP.max)
                    v.drain()
                    sem_v.inc(v.tensor_tensor(out=dl_q[:, f2 * W:(f2 + 1) * W],
                                              in0=dsc[:], in1=dsc2[:], op=OP.add), 1)

            mmb = sem_mm.n
            vb = sem_v.n

            @blk.tensor
            def _(t, q=q, vb=vb):
                t.wait_ge(sem_v.h, vb)
                for f in range(NKF):
                    i = t.matmul(pbc, wxbc_sb[:, f * 32:(f + 1) * 32],
                                 hc_q[:, f * W:(f + 1) * W],
                                 start=(f == 0), stop=(f == NKF - 1))
                sem_mm.inc(i, 1)

            @blk.vector
            def _(v, q=q, mmb=mmb):
                v.wait_ge(sem_mm.h, mmb + 1)
                sem_v.inc(v.tensor_copy(out=bcs[:], in_=pbc), 1)

            vb2 = sem_v.n
            bc_marks = {}

            @blk.tensor
            def _(t, q=q, vb2=vb2, bc_marks=bc_marks):
                t.wait_ge(sem_v.h, vb2)
                for r in range(32):
                    if r >= 1:
                        t.wait_ge(sem_v.h, vb2 + r)
                    i = t.matmul(pbb, sel_sb[:, r * 128:(r + 1) * 128], bcs[:],
                                 start=True, stop=True)
                    sem_mm.inc(i, 1)
                    bc_marks[r] = sem_mm.n

            @blk.vector
            def _(v, q=q, bc_marks=bc_marks):
                for r in range(32):
                    v.wait_ge(sem_mm.h, bc_marks[r])
                    dst = (Bb if r < 16 else Cb)
                    n = r % 16
                    sem_v.inc(v.tensor_copy(out=dst[:, n * W:(n + 1) * W], in_=pbb), 1)

            for f2 in range(NF):
                a_base2 = sem_a.n
                y_base = sem_mm.n
                v_marks = {}

                @blk.vector
                def _(v, q=q, f2=f2, a_base2=a_base2, y_base=y_base, v_marks=v_marks):
                    sem_mm.wait(v)
                    v.tensor_tensor(out=dh_sb[:], in0=dl_q[:, f2 * W:(f2 + 1) * W],
                                    in1=hc_q[:, f2 * W:(f2 + 1) * W], op=OP.mult)
                    sem_v.inc(v.drain(), 1)
                    v_marks["dh"] = sem_v.n
                    for n in range(16):
                        if n >= 2:
                            v.wait_ge(sem_a.h, a_base2 + (n - 2) + 1)
                            v.wait_ge(sem_mm.h, y_base + (n - 2) + 1)
                        v.tensor_scalar(out=argb[:, n % 2 * W:(n % 2 + 1) * W],
                                        in0=dl_q[:, f2 * W:(f2 + 1) * W],
                                        scalar1=acol[:, f2 * 16 + n:f2 * 16 + n + 1],
                                        scalar2=None, op0=OP.mult)
                        v.tensor_tensor(out=bub[:, n % 2 * W:(n % 2 + 1) * W],
                                        in0=dh_sb[:], in1=Bb[:, n * W:(n + 1) * W],
                                        op=OP.mult)
                        sem_v.inc(v.drain(), 1)
                        v_marks[("arg", n)] = sem_v.n
                        v.wait_ge(sem_a.h, a_base2 + n + 1)
                        v.tensor_tensor_scan(out=hn_sb[:],
                                             data0=abuf[:, n % 2 * W:(n % 2 + 1) * W],
                                             data1=bub[:, n % 2 * W:(n % 2 + 1) * W],
                                             initial=st_sb[:, f2 * 16 + n:f2 * 16 + n + 1],
                                             op0=OP.mult, op1=OP.add)
                        v.drain()
                        v.tensor_copy(out=st_sb[:, f2 * 16 + n:f2 * 16 + n + 1],
                                      in_=hn_sb[:, W - 1:W])
                        v.drain()
                        i = v.tensor_tensor(out=hcn[:, n % 2 * W:(n % 2 + 1) * W],
                                            in0=hn_sb[:], in1=Cb[:, n * W:(n + 1) * W],
                                            op=OP.mult)
                        sem_v.inc(i, 1)
                        v_marks[("hcn", n)] = sem_v.n

                @blk.scalar
                def _(a, q=q, f2=f2, a_base2=a_base2, v_marks=v_marks):
                    for n in range(16):
                        a.wait_ge(sem_v.h, v_marks[("arg", n)])
                        if n >= 2:
                            a.wait_ge(sem_v.h, v_marks[("hcn", n - 2)])
                        sem_a.inc(a.activation(out=abuf[:, n % 2 * W:(n % 2 + 1) * W],
                                               in_=argb[:, n % 2 * W:(n % 2 + 1) * W],
                                               func=AF.Exp), 1)

                mmy = sem_mm.n

                @blk.tensor
                def _(t, q=q, f2=f2, v_marks=v_marks):
                    for n in range(16):
                        t.wait_ge(sem_v.h, v_marks[("hcn", n)])
                        i = t.matmul(py, idb[:], hcn[:, n % 2 * W:(n % 2 + 1) * W],
                                     start=(n == 0), stop=(n == 15))
                        sem_mm.inc(i, 1)

                mmy2 = sem_mm.n

                @blk.vector
                def _(v, q=q, f2=f2, mmy2=mmy2):
                    v.wait_ge(sem_mm.h, mmy2)
                    v.scalar_tensor_tensor(out=dsc[:], in0=hc_q[:, f2 * W:(f2 + 1) * W],
                                           scalar=dcol[:, f2:f2 + 1], in1=py,
                                           op0=OP.mult, op1=OP.add)
                    v.drain()
                    sem_v.inc(v.tensor_tensor(out=y_sb[:, f2 * W:(f2 + 1) * W],
                                              in0=dsc[:], in1=sz_q[:, f2 * W:(f2 + 1) * W],
                                              op=OP.mult), 1)

            @blk.sync
            def _(s, tsl=tsl, q=q):
                sem_v.wait(s)
                for f in range(NF):
                    sem_st.inc(s.dma_start(out=y_hbm[f * 128:(f + 1) * 128, tsl],
                                           in_=y_sb[:, f * W:(f + 1) * W]))

        @blk.sync
        def _(s):
            sem_st.wait(s)

    return p


def sub_l1_attn_b(x, inp):
    W_in = np.asarray(inp["L1_attn_b_W_in"])
    W_inT = np.ascontiguousarray(W_in.T)
    conv_w = np.asarray(inp["L1_attn_b_conv_w"])[:, 0, :]
    conv_b = np.asarray(inp["L1_attn_b_conv_b"])
    W_x = np.asarray(inp["L1_attn_b_W_x"])
    W_xT = np.ascontiguousarray(W_x.T)
    A_log = np.asarray(inp["L1_attn_b_A_log"])
    Dp = np.asarray(inp["L1_attn_b_D"])
    W_out = np.asarray(inp["L1_attn_b_W_out"])
    W_outT = np.ascontiguousarray(W_out.T)
    ident = np.eye(128, dtype=np.float32)
    sel = np.zeros((32, 32 * 128), np.float32)
    for r in range(32):
        sel[r, r * 128:(r + 1) * 128] = 1.0

    def chunkmaj(vec, cols):
        return np.ascontiguousarray(vec.reshape(cols, 128).T)

    in_maps = []
    for c in range(8):
        b, s = c // 2, c % 2
        esl = slice(s * 1024, (s + 1) * 1024)
        cw = conv_w[esl]
        cwm = np.ascontiguousarray(cw.reshape(8, 128, 4).transpose(1, 0, 2).reshape(128, 32))
        in_maps.append({"x": np.ascontiguousarray(x[b]),
                        "wxin": np.ascontiguousarray(W_inT[:, esl]),
                        "wz": np.ascontiguousarray(W_inT[:, 2048 + s * 1024:2048 + (s + 1) * 1024]),
                        "cw": cwm, "cb": chunkmaj(conv_b[esl], NF)})
    outs = _run("mbp1", build_l1_attn_b_p1, in_maps, ["hc", "sz"])
    in_maps = []
    for c in range(8):
        b, s = c // 2, c % 2
        own, par = outs[c][0], outs[c ^ 1][0]
        hc_full = np.concatenate([own, par], axis=0)
        esl = slice(s * 1024, (s + 1) * 1024)
        osl = slice((1 - s) * 1024, (2 - s) * 1024)
        perm_rows = np.concatenate([np.arange(s * 1024, (s + 1) * 1024),
                                    np.arange((1 - s) * 1024, (2 - s) * 1024)])
        al = A_log[esl]
        alm = np.ascontiguousarray(al.reshape(8, 128, 16).transpose(1, 0, 2).reshape(128, 128))
        in_maps.append({"hc": np.ascontiguousarray(hc_full),
                        "sz": np.ascontiguousarray(outs[c][1]),
                        "wxd": np.ascontiguousarray(W_xT[perm_rows][:, esl]),
                        "wxbc": np.ascontiguousarray(
                            W_xT[perm_rows][:, 2048:2080].reshape(16, 128, 32)
                            .transpose(1, 0, 2).reshape(128, 512)),
                        "alog": alm, "dd": chunkmaj(Dp[esl], NF),
                        "ident": ident, "sel": sel})
    outs2 = _run("mbp2", build_l1_attn_b_p2, in_maps, ["y"])
    in_maps = []
    for c in range(8):
        b, s = c // 2, c % 2
        y_full = np.concatenate([outs2[2 * b][0], outs2[2 * b + 1][0]], axis=0)
        tsl = slice(s * TC, (s + 1) * TC)
        in_maps.append({"ho": np.ascontiguousarray(y_full[:, tsl]),
                        "w": W_outT, "x": np.ascontiguousarray(x[b][:, tsl])})
    outs3 = _run("comb_m", lambda st: build_combine(st, False, name="comb_m", ein=2048),
                 in_maps, ["y"])
    xo = np.empty_like(x)
    for c in range(8):
        b, s = c // 2, c % 2
        xo[b][:, s * TC:(s + 1) * TC] = outs3[c][0]
    return xo



SENT = 305419896


class Arena:
    def __init__(self, tile):
        self.tile = tile
        self.off = 0

    def take(self, cols):
        v = self.tile[:, self.off:self.off + cols]
        self.off += cols
        assert self.off <= self.tile.tensor.shape[1] if hasattr(self.tile, "tensor") else True
        return v

    def reset(self):
        self.off = 0


class FC:
    pass


def _mk_ln(C, sq, W=512):
    ln = LNTiles.__new__(LNTiles)
    ln.W = W
    ln.sq = sq
    ln.row = C.lnrow
    ln.ones_c = C.ones_c
    ln.ones_cb = C.ones_cb
    ln.ones_r = C.ones_r
    ln.s1 = C.pZ[0:1, 0:W]
    ln.s2 = C.pE[0:1, 0:W]
    ln.mb = C.pM[:, 0:W]
    ln.rb = C.pR[:, 0:W]
    return ln


def emit_ln_chunk(C, ln, xs, xnq_views, W, affine=None, q=0, first=False):
    blk = C.blk
    sem_x, sem_mm, sem_v, sem_a = C.sem_x, C.sem_mm, C.sem_v, C.sem_a

    @blk.scalar
    def _(a):
        sem_x.wait(a)
        sem_mm.wait(a)
        sem_v.wait(a)
        C.sem_a.inc(ln_squares(a, ln, xs, W), 1)

    @blk.tensor
    def _(t):
        sem_a.wait(t)
        if first:
            C.sem_g.wait(t)
            C.sem_w.wait(t)
            C.sem_b.wait(t)
        sem_mm.inc(ln_stats(t, ln, xs, W), 1)

    @blk.vector
    def _(v):
        sem_mm.wait(v)
        sem_v.inc(ln_rows(v, ln, W), 1)

    @blk.scalar
    def _(a):
        sem_v.wait(a)
        sem_a.inc(ln_rstd(a, ln, W), 1)

    @blk.tensor
    def _(t):
        sem_a.wait(t)
        sem_mm.inc(ln_bcast(t, ln, W), 1)

    @blk.vector
    def _(v):
        sem_mm.wait(v)
        for f in range(NF):
            v.tensor_tensor(out=ln.sq[:, f * ln.W:f * ln.W + W], in0=xs[f],
                            in1=ln.mb[:, 0:W], op=OP.subtract)
        v.drain()
        if affine is None:
            for f in range(NF):
                i = v.tensor_tensor(out=xnq_views[f], in0=ln.sq[:, f * ln.W:f * ln.W + W],
                                    in1=ln.rb[:, 0:W], op=OP.mult)
        else:
            for f in range(NF):
                v.tensor_tensor(out=ln.sq[:, f * ln.W:f * ln.W + W],
                                in0=ln.sq[:, f * ln.W:f * ln.W + W],
                                in1=ln.rb[:, 0:W], op=OP.mult)
            v.drain()
            gcol, bcol = affine
            for f in range(NF):
                i = v.tensor_scalar(out=xnq_views[f], in0=ln.sq[:, f * ln.W:f * ln.W + W],
                                    scalar1=gcol(f), scalar2=bcol(f),
                                    op0=OP.mult, op1=OP.add)
        sem_v.inc(i, 1)


def emit_barrier(C, k):
    blk = C.blk
    g_waits = [C.sem_st, C.sem_v, C.sem_a]

    @blk.gpsimd
    def _(g):
        for s in g_waits:
            s.wait(g)
        for me in (0, 1):
            with g.If_eq(C.slot_g, me):
                g.dma_start(out=C.shf[me:me + 1, 0:1],
                            in_=C.kv_sb[0:1, k:k + 1]).then_inc(C.sem_gd.h, 16)
            g.end_ifs()
        C.sem_gd.n += 16
        C.sem_gd.wait(g)
        for me in (0, 1):
            other = 1 - me
            with g.If_eq(C.slot_g, me):
                loop = f"bar{k}_{me}_loop"
                done = f"bar{k}_{me}_done"
                C.nc.br(loop, engines=[g.engine])
                with C.nc.body(loop, valid_engines=[g.engine]):
                    g.dma_start(out=C.flg_sb[0:1, 0:1],
                                in_=C.shf[other:other + 1, 0:1]).then_inc(C.sem_poll.h, 16)
                    g.nop(cycle_cnt=2000)
                    g.reg_load(C.val_g, C.flg_sb[0:1, 0:1])
                    g.br_lt(C.val_g, k, on_true=loop, on_false=done)
                C.nc.switch_bb(done)
            g.end_ifs()
        C.sem_bar.inc(g.memset(C.flg_sb[0:1, 1:2], 0.0) if False else g.engine_nop(), 1)


def if_slot_dma(C, eng_section, emit_fn, sem):
    raise NotImplementedError
def emit_aa_p1(C):
    p, nc, blk = C, C.nc, C.blk
    W = 512
    NQ = T // W
    sem_w, sem_b, sem_x = C.sem_w, C.sem_b, C.sem_x
    sem_mm, sem_v, sem_a, sem_g, sem_st = C.sem_mm, C.sem_v, C.sem_a, C.sem_g, C.sem_st

    C.abf.reset(); C.af32.reset()
    fq = C.abf.take(4 * W)
    fk = C.abf.take(4 * W)
    fv = C.abf.take(4 * W)
    ho_sb = C.abf.take(4 * W)
    wq_sb = C.abf.take(NF * 512)
    wk_sb = C.abf.take(NF * 512)
    wv_sb = C.abf.take(NF * 512)
    at_sb = C.abf.take(256)
    tm_sb = C.abf.take(256)
    S_bf = C.abf.take(256)
    z_bf4 = C.abf.take(4)
    idn_sb = C.abf.take(128)
    x_q = C.af32.take(NF * W)
    xn_q_f = C.abf.take(NF * W)
    sq = C.af32.take(NF * W)
    scr = C.af32.take(W)
    escr = C.af32.take(W)
    S_fp = C.af32.take(256)
    z_fp4 = C.af32.take(4)
    zt_sb = C.af32.take(1)
    o_fp = C.af32.take(128)
    msk_sb = C.af32.take(128)
    bq_sb = C.af32.take(4)
    bk_sb = C.af32.take(4)
    bv_sb = C.af32.take(4)
    drow = C.lnrow[0:1, 1024:1280]
    drow2 = C.lnrow[0:1, 1280:1536]
    ln = _mk_ln(C, sq, W)

    pu = [C.pA[:, 0:W], C.pB[:, 0:W]]
    intra_ps = C.pA[:, 0:128]
    outp_ps = C.pB[:, 0:128]
    den_ps = C.pC[0:1, 0:256]
    schunk_ps = C.pC[:, 256:320]
    rdb_ps = C.pC[:, 384:512]
    bT = C.bT

    @blk.gpsimd
    def _(g):
        for f in range(NF):
            sem_w.inc(g.dma_start(out=wq_sb[:, f * 512:(f + 1) * 512],
                                  in_=C.aa_wq[f * 128:(f + 1) * 128, :]))
            sem_w.inc(g.dma_start(out=wk_sb[:, f * 512:(f + 1) * 512],
                                  in_=C.aa_wk[f * 128:(f + 1) * 128, :]))
            sem_w.inc(g.dma_start(out=wv_sb[:, f * 512:(f + 1) * 512],
                                  in_=C.aa_wv[f * 128:(f + 1) * 128, :]))
        sem_w.inc(g.dma_start(out=idn_sb[:], in_=C.identf[:]))
        g.memset(z_fp4[:], 0.0)
        g.memset(z_bf4[:], 0.0)
        g.memset(S_fp[:], 0.0)
        sem_g.inc(g.memset(S_bf[:], 0.0), 1)

    @blk.sync
    def _(s):
        sem_b.inc(s.dma_start(out=bq_sb[:], in_=C.aa_bq[:]))
        sem_b.inc(s.dma_start(out=bk_sb[:], in_=C.aa_bk[:]))
        sem_b.inc(s.dma_start(out=bv_sb[:], in_=C.aa_bv[:]))
        sem_b.inc(s.dma_start(out=msk_sb[:], in_=C.trimask[:]))

    for q in range(NQ):
        tsl = slice(q * W, (q + 1) * W)

        def xq(f):
            return x_q[:, f * W:(f + 1) * W]

        def xnq(f):
            return xn_q_f[:, f * W:(f + 1) * W]

        @blk.sync
        def _(s, tsl=tsl, q=q):
            if q > 0:
                sem_st.wait(s)
            sem_v.wait(s)
            sem_a.wait(s)
            for f in range(NF):
                sem_x.inc(s.dma_start(out=xq(f), in_=C.x0[f * 128:(f + 1) * 128, tsl]))

        emit_ln_chunk(C, ln, [xq(f) for f in range(NF)], [xnq(f) for f in range(NF)],
                      W, first=(q == 0 and C.first_phase))
        if q == 0:
            C.first_phase = False

        tiles = ([("q", wq_sb, bq_sb, fq, r) for r in range(4)]
                 + [("k", wk_sb, bk_sb, fk, r) for r in range(4)]
                 + [("v", wv_sb, bv_sb, fv, r) for r in range(4)])
        vcnt = [2 if k_ != "v" else 1 for (k_, _, _, _, _) in tiles]
        vcum = [0]
        for c in vcnt:
            vcum.append(vcum[-1] + c)
        acum = [0]
        for (k_, _, _, _, _) in tiles:
            acum.append(acum[-1] + (1 if k_ != "v" else 0))
        v_base = sem_v.n
        a_base = sem_a.n
        mm_base = sem_mm.n

        @blk.tensor
        def _(t, q=q, tiles=tiles, v_base=v_base, vcum=vcum):
            sem_v.wait(t)
            for j, (kind, wsb, bsb, dst, r) in enumerate(tiles):
                if j >= 2:
                    t.wait_ge(sem_v.h, v_base + vcum[j - 2] + 1)
                for f in range(NF):
                    i = t.matmul(pu[j % 2], wsb[:, f * 512 + r * 128:f * 512 + (r + 1) * 128],
                                 xnq(f), start=(f == 0), stop=(f == NF - 1))
                sem_mm.inc(i, 1)

        @blk.vector
        def _(v, q=q, tiles=tiles, mm_base=mm_base, a_base=a_base, acum=acum):
            for j, (kind, wsb, bsb, dst, r) in enumerate(tiles):
                v.wait_ge(sem_mm.h, mm_base + j + 1)
                if kind == "v":
                    sem_v.inc(v.tensor_scalar(out=dst[:, r * W:(r + 1) * W],
                                              in0=pu[j % 2], scalar1=bsb[:, r:r + 1],
                                              scalar2=None, op0=OP.add), 1)
                else:
                    if acum[j] > 0:
                        v.wait_ge(sem_a.h, a_base + acum[j])
                    sem_v.inc(v.tensor_scalar(out=scr[:], in0=pu[j % 2],
                                              scalar1=bsb[:, r:r + 1],
                                              scalar2=None, op0=OP.add), 1)
                    v.drain()
                    v.tensor_scalar(out=C.mscr[:, 0:W], in0=scr[:], scalar1=0.0,
                                    scalar2=None, op0=OP.is_gt)
                    v.wait_ge(sem_a.h, a_base + acum[j] + 1)
                    v.drain()
                    v.tensor_scalar(out=scr[:], in0=scr[:], scalar1=1.0,
                                    scalar2=None, op0=OP.add)
                    v.drain()
                    sem_v.inc(v.select(out=dst[:, r * W:(r + 1) * W], mask=C.mscr[:, 0:W],
                                       on_true=scr[:], on_false=escr[:],
                                       add_drain=True), 1)
                    v.drain()

        @blk.scalar
        def _(a, q=q, tiles=tiles, v_base=v_base, vcum=vcum):
            for j, (kind, wsb, bsb, dst, r) in enumerate(tiles):
                if kind == "v":
                    continue
                a.wait_ge(sem_v.h, v_base + vcum[j] + 1)
                sem_a.inc(a.activation(out=escr[:], in_=scr[:], func=AF.Exp), 1)

        for j in range(4):
            for pair in range(4):
                mmb = sem_mm.n
                vb = sem_v.n

                def fqp(pair=pair, j=j):
                    return fq[:, pair * W + j * 128: pair * W + (j + 1) * 128]

                def fkp(pair=pair, j=j):
                    return fk[:, pair * W + j * 128: pair * W + (j + 1) * 128]

                def fvp(pair=pair, j=j):
                    return fv[:, pair * W + j * 128: pair * W + (j + 1) * 128]

                @blk.tensor
                def _(t, q=q, j=j, pair=pair, mmb=mmb, vb=vb, fqp=fqp, fkp=fkp, fvp=fvp):
                    sem_v.wait(t)
                    t.transpose(bT[:, 0:128], fkp(), idn_sb[:])
                    i = t.transpose(bT[:, 128:256], fvp(), idn_sb[:])
                    sem_mm.inc(i, 1)
                    t.wait_ge(sem_v.h, vb + 1)
                    sem_mm.inc(t.matmul(intra_ps[:], fkp()[0:64, :], fqp()[0:64, :],
                                        start=True, stop=True), 1)
                    t.wait_ge(sem_v.h, vb + 2)
                    i = t.matmul(intra_ps[:], fkp()[64:128, :], fqp()[64:128, :],
                                 start=True, stop=True)
                    sem_mm.inc(i, 1)
                    t.wait_ge(sem_v.h, vb + 3)
                    for par in (0, 64):
                        hv = slice(par, par + 64)
                        hb = (par // 64) * 128
                        tp = (0, par)
                        t.matmul(outp_ps[hv, :], S_bf[hv, pair * 64:(pair + 1) * 64],
                                 fqp()[hv, :], start=True, stop=False,
                                 tile_position=(par, par))
                        t.matmul(outp_ps[hv, :], tm_sb[:, 128 + par:128 + par + 64],
                                 at_sb[:, hb:hb + 128], start=False, stop=True,
                                 tile_position=tp)
                        t.matmul(den_ps[:, hb:hb + 128], z_bf4[hv, pair:pair + 1],
                                 fqp()[hv, :], start=True, stop=False)
                        t.matmul(den_ps[:, hb:hb + 128], C.ones_cb[:],
                                 at_sb[:, hb:hb + 128], start=False, stop=True)
                        i = t.matmul(schunk_ps[hv, :], tm_sb[:, par:par + 64],
                                     tm_sb[:, 128 + par:128 + par + 64],
                                     start=True, stop=True, tile_position=tp)
                    sem_mm.inc(i, 1)

                @blk.vector
                def _(v, q=q, j=j, pair=pair, mmb=mmb, fkp=fkp):
                    v.wait_ge(sem_mm.h, mmb + 1)
                    sem_v.inc(v.tensor_copy(out=tm_sb[:], in_=bT[:]), 1)
                    v.wait_ge(sem_mm.h, mmb + 2)
                    sem_v.inc(v.tensor_tensor(out=at_sb[:, 0:128], in0=intra_ps[:],
                                              in1=msk_sb[:], op=OP.mult), 1)
                    v.wait_ge(sem_mm.h, mmb + 3)
                    sem_v.inc(v.tensor_tensor(out=at_sb[:, 128:256], in0=intra_ps[:],
                                              in1=msk_sb[:], op=OP.mult), 1)
                    v.wait_ge(sem_mm.h, mmb + 4)
                    v.tensor_scalar(out=drow[0:1, 0:128], in0=den_ps[0:1, 0:128],
                                    scalar1=1e-6, scalar2=None, op0=OP.add)
                    v.tensor_scalar(out=drow2[0:1, 0:128], in0=den_ps[0:1, 128:256],
                                    scalar1=1e-6, scalar2=None, op0=OP.add)
                    v.drain()
                    v.reciprocal(out=drow[0:1, 0:128], in_=drow[0:1, 0:128])
                    v.reciprocal(out=drow2[0:1, 0:128], in_=drow2[0:1, 0:128])
                    v.tensor_tensor(out=S_fp[:, pair * 64:(pair + 1) * 64],
                                    in0=S_fp[:, pair * 64:(pair + 1) * 64],
                                    in1=schunk_ps[:], op=OP.add)
                    v.reduce_sum(out=zt_sb[:], in_=fkp(), axis=mybir.AxisListType.X)
                    v.drain()
                    v.tensor_tensor(out=z_fp4[:, pair:pair + 1], in0=z_fp4[:, pair:pair + 1],
                                    in1=zt_sb[:], op=OP.add)
                    v.drain()
                    v.tensor_copy(out=S_bf[:, pair * 64:(pair + 1) * 64],
                                  in_=S_fp[:, pair * 64:(pair + 1) * 64])
                    v.tensor_copy(out=z_bf4[:, pair:pair + 1], in_=z_fp4[:, pair:pair + 1])
                    sem_v.inc(v.tensor_copy(out=o_fp[:], in_=outp_ps[:]), 1)

                vb3 = sem_v.n

                @blk.tensor
                def _(t, q=q, j=j, pair=pair, vb3=vb3):
                    t.wait_ge(sem_v.h, vb3)
                    t.matmul(rdb_ps[0:64, :], C.ones_r[0:1, 0:64], drow[0:1, 0:128],
                             start=True, stop=True)
                    i = t.matmul(rdb_ps[64:128, :], C.ones_r[0:1, 0:64],
                                 drow2[0:1, 0:128], start=True, stop=True,
                                 tile_position=(0, 64))
                    sem_mm.inc(i, 1)

                mmb4 = sem_mm.n

                @blk.vector
                def _(v, q=q, j=j, pair=pair, mmb4=mmb4):
                    v.wait_ge(sem_mm.h, mmb4)
                    sem_v.inc(v.tensor_tensor(
                        out=ho_sb[:, pair * W + j * 128:pair * W + (j + 1) * 128],
                        in0=o_fp[:], in1=rdb_ps[:], op=OP.mult), 1)

        @blk.sync
        def _(s, tsl=tsl, q=q):
            sem_v.wait(s)
            for me in (0, 1):
                with s.If_eq(C.slot_s, me):
                    for r in range(4):
                        s.dma_start(out=C.hosh[me * 512 + r * 128:me * 512 + (r + 1) * 128, tsl],
                                    in_=ho_sb[:, r * W:(r + 1) * W]).then_inc(sem_st.h, 16)
                s.end_ifs()
            sem_st.n += 4 * 16


def emit_comb(C, tag, ho_src, w_ap, bo_ap, x_src, x_dst, ein=E):
    p, nc, blk = C, C.nc, C.blk
    W = 512
    NQ = TC // W
    NKF = ein // 128
    sem_w, sem_b, sem_x = C.sem_w, C.sem_b, C.sem_x
    sem_mm, sem_v, sem_a, sem_st = C.sem_mm, C.sem_v, C.sem_a, C.sem_st

    C.abf.reset(); C.af32.reset()
    ho_sb = C.abf.take(NKF * W)
    w_sb = C.abf.take(NKF * E)
    x_sb = C.af32.take(NF * W)
    bo_sb = C.af32.take(NF) if bo_ap is not None else None
    pp = [C.pA[:, 0:W], C.pB[:, 0:W]]

    @blk.gpsimd
    def _(g):
        sem_st.wait(g)
        sem_v.wait(g)
        sem_a.wait(g)
        for f in range(NKF):
            sem_w.inc(g.dma_start(out=w_sb[:, f * E:(f + 1) * E],
                                  in_=w_ap[f * 128:(f + 1) * 128, :]))

    if bo_ap is not None:
        @blk.sync
        def _(s):
            sem_b.inc(s.dma_start(out=bo_sb[:], in_=bo_ap[:]))

    for q in range(NQ):
        @blk.sync
        def _(s, q=q):
            if q > 0:
                sem_st.wait(s)
            sem_w.wait(s)
            C.sem_bar.wait(s)
            for me in (0, 1):
                tb = me * TC + q * W
                with s.If_eq(C.slot_s, me):
                    for f in range(NKF):
                        s.dma_start(out=ho_sb[:, f * W:(f + 1) * W],
                                    in_=ho_src[f * 128:(f + 1) * 128, tb:tb + W]).then_inc(sem_x.h, 16)
                    for f in range(NF):
                        s.dma_start(out=x_sb[:, f * W:(f + 1) * W],
                                    in_=x_src[f * 128:(f + 1) * 128, tb:tb + W]).then_inc(sem_x.h, 16)
                s.end_ifs()
            sem_x.n += (NKF + NF) * 16

        v_base = sem_v.n
        mm_base = sem_mm.n

        @blk.tensor
        def _(t, q=q, v_base=v_base):
            sem_x.wait(t)
            if C.first_phase:
                pass
            sem_b.wait(t)
            for oc in range(NF):
                if oc >= 2:
                    t.wait_ge(sem_v.h, v_base + (oc - 1))
                for f in range(NKF):
                    i = t.matmul(pp[oc % 2], w_sb[:, f * E + oc * 128:f * E + (oc + 1) * 128],
                                 ho_sb[:, f * W:(f + 1) * W],
                                 start=(f == 0), stop=(f == NKF - 1))
                sem_mm.inc(i, 1)

        @blk.vector
        def _(v, q=q, mm_base=mm_base):
            for oc in range(NF):
                v.wait_ge(sem_mm.h, mm_base + oc + 1)
                if bo_sb is not None:
                    v.tensor_scalar(out=x_sb[:, oc * W:(oc + 1) * W],
                                    in0=x_sb[:, oc * W:(oc + 1) * W],
                                    scalar1=bo_sb[:, oc:oc + 1], scalar2=None,
                                    op0=OP.add)
                    v.drain()
                sem_v.inc(v.tensor_tensor(out=x_sb[:, oc * W:(oc + 1) * W],
                                          in0=x_sb[:, oc * W:(oc + 1) * W],
                                          in1=pp[oc % 2], op=OP.add), 1)

        @blk.sync
        def _(s, q=q):
            sem_v.wait(s)
            for me in (0, 1):
                tb = me * TC + q * W
                with s.If_eq(C.slot_s, me):
                    for f in range(NF):
                        s.dma_start(out=x_dst[f * 128:(f + 1) * 128, tb:tb + W],
                                    in_=x_sb[:, f * W:(f + 1) * W]).then_inc(sem_st.h, 16)
                s.end_ifs()
            sem_st.n += NF * 16


def build_fused(stack, nphase=13):
    C = FC()
    p = Prog(stack, "fused")
    C.p = p
    C.nc = p.nc
    nc = p.nc

    C.x0 = p.din("x0", [E, T])
    C.slt = p.din("slt", [1, 2], U32)
    C.kv = p.din("kv", [1, 32], U32)
    C.identf = p.din("identf", [128, 128])
    C.trimask = p.din("trimask", [128, 128])
    C.dmask = p.din("dmask", [128, 2048])
    C.aa_wq = p.din("aa_wq", [E, 512]); C.aa_wk = p.din("aa_wk", [E, 512])
    C.aa_wv = p.din("aa_wv", [E, 512]); C.aa_wo = p.din("aa_wo", [E, E])
    C.aa_bq = p.din("aa_bq", [128, 4]); C.aa_bk = p.din("aa_bk", [128, 4])
    C.aa_bv = p.din("aa_bv", [128, 4]); C.aa_bo = p.din("aa_bo", [128, 8])
    C.ab_wq = p.din("ab_wq", [E, 512]); C.ab_wk = p.din("ab_wk", [E, 512])
    C.ab_wv = p.din("ab_wv", [E, 512]); C.ab_wo = p.din("ab_wo", [E, E])
    C.ab_bo = p.din("ab_bo", [128, 8])
    C.f0_w1 = p.din("f0_w1", [E, 4096]); C.f0_b1 = p.din("f0_b1", [128, 32])
    C.f0_w2 = p.din("f0_w2", [4096, E]); C.f0_b2 = p.din("f0_b2", [128, 8])
    C.f0_g = p.din("f0_g", [128, 8]); C.f0_be = p.din("f0_be", [128, 8])
    C.la_lnw = p.din("la_lnw", [128, 8]); C.la_lnb = p.din("la_lnb", [128, 8])
    C.la_wq = p.din("la_wq", [E, 512]); C.la_wk = p.din("la_wk", [E, 512])
    C.la_wv = p.din("la_wv", [E, 512]); C.la_wo = p.din("la_wo", [E, E])
    C.la_th = p.din("la_th", [128, 1]); C.la_frep = p.din("la_frep", [128, 1])
    C.la_sgn = p.din("la_sgn", [128, 1]); C.la_psw = p.din("la_psw", [128, 128])
    C.f1_wu = p.din("f1_wu", [E, 4096]); C.f1_wd = p.din("f1_wd", [4096, E])
    C.mb_wxin = p.din("mb_wxin", [E, 1024]); C.mb_wz = p.din("mb_wz", [E, 1024])
    C.mb_cw = p.din("mb_cw", [128, 32]); C.mb_cb = p.din("mb_cb", [128, 8])
    C.mb_wxd = p.din("mb_wxd", [2048, 1024]); C.mb_wxbc = p.din("mb_wxbc", [128, 512])
    C.mb_alog = p.din("mb_alog", [128, 128]); C.mb_dd = p.din("mb_dd", [128, 8])
    C.mb_sel = p.din("mb_sel", [32, 4096]); C.mb_wout = p.din("mb_wout", [2048, E])
    C.y = p.dout("y", [E, TC])

    C.xsh = p.dscratch("xsh", [E, T], F32, shared=True)
    C.hosh = p.dscratch("hosh", [E, T], BF16, shared=True)
    C.hcsh = p.dscratch("hcsh", [2048, T], BF16, shared=True)
    C.ysh = p.dscratch("ysh", [2048, T], BF16, shared=True)
    C.shf = p.dscratch("shf", [2, 32], U32, shared=True)
    C.szp = p.dscratch("szp", [1024, T], BF16)
    C.wut = p.dscratch("wut", [E, 4096], BF16)
    C.wdt = p.dscratch("wdt", [4096, E], BF16)

    C.abf = Arena(p.sb("abf", [128, 75800], BF16))
    C.af32 = Arena(p.sb("af32", [128, 12400], F32))
    C.lnrow = p.sb("lnrow", [1, 2048], F32)
    C.ones_c = p.sb("ones_c", [128, 1], F32)
    C.ones_cb = p.sb("ones_cb", [128, 1], BF16)
    C.ones_r = p.sb("ones_r", [1, 128], F32)
    C.ones_r512 = p.sb("ones_r512", [1, 512], F32)
    C.bsq = p.sb("bsq", [128, 1], F32)
    C.onec = p.sb("onec", [128, 1], F32)
    C.c220 = p.sb("c220", [128, 1], F32)
    C.mscr = p.sb("mscr", [128, 512], mybir.dt.uint8)
    C.slt_sb = p.sb("slt_sb", [1, 2], U32)
    C.kv_sb = p.sb("kv_sb", [1, 32], U32)
    C.flg_sb = p.sb("flg_sb", [1, 1], U32)

    C.pA = p.ps("pA", [128, 512], F32)
    C.pB = p.ps("pB", [128, 512], F32)
    C.pC = p.ps("pC", [128, 512], F32)
    C.bT = p.ps("bT", [128, 256], BF16)
    C.pZ = p.ps("pZ", [1, 512], F32)
    C.pE = p.ps("pE", [1, 512], F32)
    C.pM = p.ps("pM", [128, 512], F32)
    C.pR = p.ps("pR", [128, 512], F32)

    for nm in ("sem_w", "sem_b", "sem_x", "sem_mm", "sem_v", "sem_a", "sem_g",
               "sem_st", "sem_gd", "sem_poll", "sem_bar"):
        setattr(C, nm, p.sem(nm))
    C.first_phase = True

    with nc.Block() as blk:
        C.blk = blk

        @blk.sync
        def _(s):
            s.dma_start(out=C.slt_sb[:], in_=C.slt[:]).then_inc(C.sem_b.h, 16)
            s.dma_start(out=C.kv_sb[:], in_=C.kv[:]).then_inc(C.sem_b.h, 16)
            C.sem_b.n += 32
            s.wait_ge(C.sem_b.h, 32)
            C.slot_s = s.alloc_register()
            s.reg_load(C.slot_s, C.slt_sb[0:1, 0:1])

        @blk.gpsimd
        def _(g):
            g.wait_ge(C.sem_b.h, 32)
            C.slot_g = g.alloc_register()
            C.val_g = g.alloc_register()
            g.reg_load(C.slot_g, C.slt_sb[0:1, 0:1])
            g.memset(C.ones_c[:], 1.0)
            g.memset(C.ones_cb[:], 1.0)
            g.memset(C.ones_r[:], 1.0)
            g.memset(C.ones_r512[:], 1.0)
            g.memset(C.bsq[:], 0.7071067811865476)
            g.memset(C.onec[:], 1.0)
            C.sem_g.inc(g.memset(C.c220[:], 220.0), 1)
            for me in (0, 1):
                other = 1 - me
                with g.If_eq(C.slot_g, me):
                    g.dma_start(out=C.shf[me:me + 1, 0:1],
                                in_=C.kv_sb[0:1, 0:1]).then_inc(C.sem_gd.h, 16)
                g.end_ifs()
            C.sem_gd.n += 16
            g.wait_ge(C.sem_gd.h, C.sem_gd.n)
            for me in (0, 1):
                with g.If_eq(C.slot_g, me):
                    g.dma_start(out=C.shf[me:me + 1, 31:32],
                                in_=C.kv_sb[0:1, 31:32]).then_inc(C.sem_gd.h, 16)
                g.end_ifs()
            C.sem_gd.n += 16
            g.wait_ge(C.sem_gd.h, C.sem_gd.n)
            for me in (0, 1):
                other = 1 - me
                with g.If_eq(C.slot_g, me):
                    loop = f"hs_{me}_loop"
                    done = f"hs_{me}_done"
                    nc.br(loop, engines=[g.engine])
                    with nc.body(loop, valid_engines=[g.engine]):
                        g.dma_start(out=C.flg_sb[0:1, 0:1],
                                    in_=C.shf[other:other + 1, 31:32]).then_inc(C.sem_poll.h, 16)
                        g.nop(cycle_cnt=2000)
                        g.reg_load(C.val_g, C.flg_sb[0:1, 0:1])
                        g.br_eq(C.val_g, 31, on_true=done, on_false=loop)
                    nc.switch_bb(done)
                g.end_ifs()
            C.sem_bar.inc(g.engine_nop(), 1)

        PH = []
        PH.append(("aa_p1", lambda: emit_aa_p1(C)))
        PH.append(("comb_aa", lambda: (emit_barrier(C, 1),
                   emit_comb(C, "aa", C.hosh, C.aa_wo, C.aa_bo, C.x0, C.xsh))))
        if nphase > 2:
            PH.append(("ffn0_a", lambda: emit_ffn0(C, C.xsh, C.xsh)))
            PH.append(("ab_p1", lambda: (emit_barrier(C, 2), emit_ab_p1(C))))
            PH.append(("comb_ab", lambda: (emit_barrier(C, 3),
                       emit_comb(C, "ab", C.hosh, C.ab_wo, C.ab_bo, C.xsh, C.xsh))))
            PH.append(("ffn0_b", lambda: emit_ffn0(C, C.xsh, C.xsh)))
            PH.append(("la_p1", lambda: (emit_barrier(C, 4), emit_la_p1(C))))
            PH.append(("comb_la", lambda: (emit_barrier(C, 5),
                       emit_comb(C, "la", C.hosh, C.la_wo, None, C.xsh, C.xsh))))
            PH.append(("ffn1_a", lambda: emit_ffn1(C, C.xsh, C.xsh, final=False)))
            PH.append(("mb_p1", lambda: (emit_barrier(C, 6), emit_mb_p1(C))))
            PH.append(("mb_p2", lambda: (emit_barrier(C, 7), emit_mb_p2(C))))
            PH.append(("comb_mb", lambda: (emit_barrier(C, 8),
                       emit_comb(C, "mb", C.ysh, C.mb_wout, None, C.xsh, C.xsh,
                                 ein=2048))))
            PH.append(("ffn1_b", lambda: emit_ffn1(C, C.xsh, None, final=True, tern=False)))
        for i in range(min(nphase, len(PH))):
            nm, fn = PH[i]
            with nc.named_scope(nm):
                fn()

        if nphase < 13:
            @blk.sync
            def _(s):
                C.sem_st.wait(s)
                for me in (0, 1):
                    with s.If_eq(C.slot_s, me):
                        for f in range(NF):
                            s.dma_start(out=C.y[f * 128:(f + 1) * 128, :],
                                        in_=C.xsh[f * 128:(f + 1) * 128,
                                                  me * TC:(me + 1) * TC]).then_inc(C.sem_st.h, 16)
                    s.end_ifs()
                C.sem_st.n += NF * 16
                C.sem_st.wait(s)
        else:
            @blk.sync
            def _(s):
                C.sem_st.wait(s)

    return C


def _emit_ln2_affine_residual(C, ln, uq, xq, gam_sb, bet_sb, W):
    blk = C.blk
    sem_mm, sem_v, sem_a = C.sem_mm, C.sem_v, C.sem_a

    @blk.scalar
    def _(a):
        sem_v.wait(a)
        sem_a.inc(ln_squares(a, ln, [uq(f) for f in range(NF)], W), 1)

    @blk.tensor
    def _(t):
        sem_a.wait(t)
        sem_mm.inc(ln_stats(t, ln, [uq(f) for f in range(NF)], W), 1)

    @blk.vector
    def _(v):
        sem_mm.wait(v)
        sem_v.inc(ln_rows(v, ln, W), 1)

    @blk.scalar
    def _(a):
        sem_v.wait(a)
        sem_a.inc(ln_rstd(a, ln, W), 1)

    @blk.tensor
    def _(t):
        sem_a.wait(t)
        sem_mm.inc(ln_bcast(t, ln, W), 1)

    @blk.vector
    def _(v):
        sem_mm.wait(v)
        for f in range(NF):
            v.tensor_tensor(out=ln.sq[:, f * ln.W:f * ln.W + W], in0=uq(f),
                            in1=ln.mb[:, 0:W], op=OP.subtract)
        v.drain()
        for f in range(NF):
            v.tensor_tensor(out=ln.sq[:, f * ln.W:f * ln.W + W],
                            in0=ln.sq[:, f * ln.W:f * ln.W + W],
                            in1=ln.rb[:, 0:W], op=OP.mult)
        v.drain()
        for f in range(NF):
            v.tensor_scalar(out=ln.sq[:, f * ln.W:f * ln.W + W],
                            in0=ln.sq[:, f * ln.W:f * ln.W + W],
                            scalar1=gam_sb[:, f:f + 1],
                            scalar2=bet_sb[:, f:f + 1], op0=OP.mult, op1=OP.add)
        v.drain()
        for f in range(NF):
            i = v.tensor_tensor(out=xq(f), in0=xq(f),
                                in1=ln.sq[:, f * ln.W:f * ln.W + W], op=OP.add)
        C.sem_v.inc(i, 1)


def emit_ffn0(C, x_src, x_dst):
    blk = C.blk
    W = 256
    NQ = TC // W
    DF = 4096
    sem_w, sem_b, sem_x = C.sem_w, C.sem_b, C.sem_x
    sem_mm, sem_v, sem_a, sem_st = C.sem_mm, C.sem_v, C.sem_a, C.sem_st

    C.abf.reset(); C.af32.reset()
    w1_sb = C.abf.take(NF * DF)
    w2_sb = C.abf.take(32 * E)
    xn_q = C.abf.take(NF * W)
    h_all = C.abf.take(32 * W)
    x_q = C.af32.take(NF * W)
    xnf_q = C.af32.take(NF * W)
    u_q = C.af32.take(NF * W)
    sq = C.af32.take(NF * W)
    b1_sb = C.af32.take(32)
    b2_sb = C.af32.take(8)
    gam_sb = C.af32.take(8)
    bet_sb = C.af32.take(8)
    ln = _mk_ln(C, sq, W)
    pu = [C.pA[:, 0:W], C.pB[:, 0:W]]
    pd = [C.pC[:, 0:W], C.pM[:, 0:W]]

    @blk.gpsimd
    def _(g):
        sem_st.wait(g)
        sem_v.wait(g)
        sem_a.wait(g)
        for f in range(NF):
            sem_w.inc(g.dma_start(out=w1_sb[:, f * DF:(f + 1) * DF],
                                  in_=C.f0_w1[f * 128:(f + 1) * 128, :]))
        for c in range(32):
            sem_w.inc(g.dma_start(out=w2_sb[:, c * E:(c + 1) * E],
                                  in_=C.f0_w2[c * 128:(c + 1) * 128, :]))

    @blk.sync
    def _(s):
        sem_st.wait(s)
        sem_v.wait(s)
        sem_b.inc(s.dma_start(out=b1_sb[:], in_=C.f0_b1[:]))
        sem_b.inc(s.dma_start(out=b2_sb[:], in_=C.f0_b2[:]))
        sem_b.inc(s.dma_start(out=gam_sb[:], in_=C.f0_g[:]))
        sem_b.inc(s.dma_start(out=bet_sb[:], in_=C.f0_be[:]))

    for q in range(NQ):
        def xq(f):
            return x_q[:, f * W:(f + 1) * W]

        def xnfq(f):
            return xnf_q[:, f * W:(f + 1) * W]

        def xnq(f):
            return xn_q[:, f * W:(f + 1) * W]

        def uq(f):
            return u_q[:, f * W:(f + 1) * W]

        @blk.sync
        def _(s, q=q):
            sem_st.wait(s)
            sem_w.wait(s)
            sem_b.wait(s)
            for me in (0, 1):
                tb = me * TC + q * W
                with s.If_eq(C.slot_s, me):
                    for f in range(NF):
                        s.dma_start(out=xq(f),
                                    in_=x_src[f * 128:(f + 1) * 128, tb:tb + W]).then_inc(sem_x.h, 16)
                s.end_ifs()
            sem_x.n += NF * 16

        emit_ln_chunk(C, ln, [xq(f) for f in range(NF)], [xnq(f) for f in range(NF)], W)

        @blk.vector
        def _(v, q=q):
            v.drain()
            for f in range(NF):
                v.tensor_tensor(out=xnfq(f), in0=xq(f), in1=ln.mb[:, 0:W], op=OP.subtract)
            v.drain()
            for f in range(NF):
                i = v.tensor_tensor(out=xnfq(f), in0=xnfq(f), in1=ln.rb[:, 0:W], op=OP.mult)
            sem_v.inc(i, 1)

        a_base = sem_a.n
        mm_base = sem_mm.n

        @blk.tensor
        def _(t, q=q, a_base=a_base):
            sem_v.wait(t)
            for oc in range(32):
                if oc >= 2:
                    t.wait_ge(sem_a.h, a_base + (oc - 1))
                for f in range(NF):
                    i = t.matmul(pu[oc % 2], w1_sb[:, f * DF + oc * 128:f * DF + (oc + 1) * 128],
                                 xnq(f), start=(f == 0), stop=(f == NF - 1))
                sem_mm.inc(i, 1)

        @blk.scalar
        def _(a, q=q, a_base=a_base, mm_base=mm_base):
            for oc in range(32):
                a.wait_ge(sem_mm.h, mm_base + oc + 1)
                sem_a.inc(a.activation(out=h_all[:, oc * W:(oc + 1) * W],
                                       in_=pu[oc % 2], func=AF.Relu,
                                       bias=b1_sb[:, oc:oc + 1]), 1)

        mm_base2 = sem_mm.n
        v_base = sem_v.n

        @blk.tensor
        def _(t, q=q, a_base=a_base, v_base=v_base):
            t.wait_ge(sem_a.h, a_base + 32)
            for f in range(NF):
                if f >= 2:
                    t.wait_ge(sem_v.h, v_base + (f - 1))
                for oc in range(32):
                    i = t.matmul(pd[f % 2], w2_sb[:, oc * E + f * 128:oc * E + (f + 1) * 128],
                                 h_all[:, oc * W:(oc + 1) * W],
                                 start=(oc == 0), stop=(oc == 31))
                sem_mm.inc(i, 1)

        @blk.vector
        def _(v, q=q, v_base=v_base, mm_base2=mm_base2):
            for f in range(NF):
                v.wait_ge(sem_mm.h, mm_base2 + f + 1)
                sem_v.inc(v.tensor_scalar(out=uq(f), in0=pd[f % 2],
                                          scalar1=b2_sb[:, f:f + 1], scalar2=None,
                                          op0=OP.add), 1)
            v.drain()
            for f in range(NF):
                i = v.tensor_tensor(out=uq(f), in0=uq(f), in1=xnfq(f), op=OP.add)
            sem_v.inc(i, 1)

        _emit_ln2_affine_residual(C, ln, uq, xq, gam_sb, bet_sb, W)

        @blk.sync
        def _(s, q=q):
            sem_v.wait(s)
            for me in (0, 1):
                tb = me * TC + q * W
                with s.If_eq(C.slot_s, me):
                    for f in range(NF):
                        s.dma_start(out=x_dst[f * 128:(f + 1) * 128, tb:tb + W],
                                    in_=xq(f)).then_inc(sem_st.h, 16)
                s.end_ifs()
            sem_st.n += NF * 16


def emit_ffn1(C, x_src, x_dst, final=False, tern=True):
    blk = C.blk
    W = 256
    NQ = TC // W
    DF = 4096
    EPS2 = 1e-8
    sem_w, sem_b, sem_x = C.sem_w, C.sem_b, C.sem_x
    sem_mm, sem_v, sem_a, sem_g, sem_st = C.sem_mm, C.sem_v, C.sem_a, C.sem_g, C.sem_st

    C.abf.reset(); C.af32.reset()
    wu_sb = C.abf.take(NF * DF)
    wd_sb = C.abf.take(32 * E)
    xn_q = C.abf.take(NF * W)
    hg = C.abf.take(32 * W)
    x_q = C.af32.take(NF * W)
    wtmp2 = C.af32.take(2 * DF)
    wtmpv = [wtmp2[:, 0:DF], wtmp2[:, DF:2 * DF]]
    sq = C.af32.take(NF * W)
    hsq = sq[:, 0:W]
    acc = C.af32.take(48)
    thr_col = C.af32.take(2)
    arow = C.lnrow[0:1, 1024:1088]
    ln = _mk_ln(C, sq, W)
    pu = [C.pA[:, 0:W], C.pB[:, 0:W]]
    pd = [C.pC[:, 0:W], C.pM[:, 0:W]]
    hs_ps = C.pR[:, 0:W]
    ar_ps = C.pE[0:1, 0:64]

    NCH_U, NCH_D = NF, 32

    def wchunk_hbm(i):
        if i < NCH_U:
            return C.f1_wu[i * 128:(i + 1) * 128, :]
        return C.f1_wd[(i - NCH_U) * 128:(i - NCH_U + 1) * 128, 0:E]

    if not tern:
        @blk.gpsimd
        def _(g):
            sem_st.wait(g)
            sem_v.wait(g)
            sem_a.wait(g)
            for f in range(NF):
                sem_w.inc(g.dma_start(out=wu_sb[:, f * DF:(f + 1) * DF],
                                      in_=C.wut[f * 128:(f + 1) * 128, :]))
            for c in range(32):
                sem_w.inc(g.dma_start(out=wd_sb[:, c * E:(c + 1) * E],
                                      in_=C.wdt[c * 128:(c + 1) * 128, :]))

        @blk.sync
        def _(s):
            sem_st.wait(s)
            sem_v.wait(s)
            s.nop()
    if tern:
        x0_base = sem_x.n
        v0_base = sem_v.n

        @blk.sync
        def _(s):
            sem_st.wait(s)
            sem_v.wait(s)
            sem_a.wait(s)
            sem_x.inc(s.dma_start(out=wtmpv[0][:, 0:DF], in_=wchunk_hbm(0)))
            sem_x.inc(s.dma_start(out=wtmpv[1][:, 0:DF], in_=wchunk_hbm(1)))

        for i in range(40):
            w_cols = DF if i < NCH_U else E

            @blk.vector
            def _(v, i=i, w_cols=w_cols, x0_base=x0_base):
                v.wait_ge(sem_x.h, x0_base + 16 * (i + 1))
                sem_v.inc(v.reduce_sum(out=acc[:, i:i + 1], in_=wtmpv[i % 2][:, 0:w_cols],
                                       axis=mybir.AxisListType.X,
                                       apply_absolute_value=True), 1)

            if i < 38:
                @blk.sync
                def _(s, i=i, v0_base=v0_base):
                    s.wait_ge(sem_v.h, v0_base + i + 1)
                    nxt = DF if i + 2 < NCH_U else E
                    sem_x.inc(s.dma_start(out=wtmpv[i % 2][:, 0:nxt],
                                          in_=wchunk_hbm(i + 2)))

        mmb = sem_mm.n
        vb = sem_v.n

        @blk.tensor
        def _(t, vb=vb):
            t.wait_ge(sem_v.h, vb)
            sem_a.wait(t)
            t.matmul(ar_ps[0:1, 0:8], C.ones_c[:], acc[:, 0:8], start=True, stop=True)
            sem_mm.inc(t.matmul(ar_ps[0:1, 8:40], C.ones_c[:], acc[:, 8:40],
                                start=True, stop=True), 1)

        @blk.vector
        def _(v, mmb=mmb):
            v.wait_ge(sem_mm.h, mmb + 1)
            v.reduce_sum(out=arow[0:1, 0:1], in_=ar_ps[0:1, 0:8], axis=mybir.AxisListType.X)
            v.reduce_sum(out=arow[0:1, 1:2], in_=ar_ps[0:1, 8:40], axis=mybir.AxisListType.X)
            v.drain()
            sem_v.inc(v.tensor_scalar(out=arow[0:1, 2:4], in0=arow[0:1, 0:2],
                                      scalar1=0.5 / (E * DF), scalar2=None,
                                      op0=OP.mult), 1)

        vb2 = sem_v.n

        @blk.tensor
        def _(t, vb2=vb2):
            t.wait_ge(sem_v.h, vb2)
            sem_mm.inc(t.matmul(hs_ps[:, 0:2], C.ones_r[:], arow[0:1, 2:4],
                                start=True, stop=True), 1)

        mmb2 = sem_mm.n

        @blk.vector
        def _(v, mmb2=mmb2):
            v.wait_ge(sem_mm.h, mmb2)
            sem_v.inc(v.tensor_copy(out=thr_col[:], in_=hs_ps[:, 0:2]), 1)

        x2_base = sem_x.n
        v2_base = sem_v.n

        @blk.sync
        def _(s):
            sem_v.wait(s)
            sem_x.inc(s.dma_start(out=wtmpv[0][:, 0:DF], in_=wchunk_hbm(0)))
            sem_x.inc(s.dma_start(out=wtmpv[1][:, 0:DF], in_=wchunk_hbm(1)))

        for i in range(40):
            w_cols = DF if i < NCH_U else E

            @blk.scalar
            def _(a, i=i, w_cols=w_cols, x2_base=x2_base):
                a.wait_ge(sem_x.h, x2_base + 16 * (i + 1))
                dst = (wu_sb[:, i * DF:(i + 1) * DF] if i < NCH_U
                       else wd_sb[:, (i - NCH_U) * E:(i - NCH_U + 1) * E])
                sem_a.inc(a.activation(out=dst, in_=wtmpv[i % 2][:, 0:w_cols],
                                       func=AF.Sign), 1)

            @blk.vector
            def _(v, i=i, w_cols=w_cols):
                sem_a.wait(v)
                tc = 0 if i < NCH_U else 1
                dst = (wu_sb[:, i * DF:(i + 1) * DF] if i < NCH_U
                       else wd_sb[:, (i - NCH_U) * E:(i - NCH_U + 1) * E])
                v.tensor_tensor(out=wtmpv[i % 2][:, 0:w_cols],
                                in0=wtmpv[i % 2][:, 0:w_cols], in1=dst, op=OP.mult)
                v.drain()
                v.tensor_scalar(out=wtmpv[i % 2][:, 0:w_cols],
                                in0=wtmpv[i % 2][:, 0:w_cols],
                                scalar1=thr_col[:, tc:tc + 1], scalar2=None,
                                op0=OP.is_gt)
                v.drain()
                sem_v.inc(v.tensor_tensor(out=dst, in0=dst,
                                          in1=wtmpv[i % 2][:, 0:w_cols], op=OP.mult), 1)

            if i < 38:
                @blk.sync
                def _(s, i=i, v2_base=v2_base):
                    s.wait_ge(sem_v.h, v2_base + i + 1)
                    nxt = DF if i + 2 < NCH_U else E
                    sem_x.inc(s.dma_start(out=wtmpv[i % 2][:, 0:nxt],
                                          in_=wchunk_hbm(i + 2)))

        @blk.sync
        def _(s):
            sem_v.wait(s)
            for f in range(NF):
                sem_st.inc(s.dma_start(out=C.wut[f * 128:(f + 1) * 128, :],
                                       in_=wu_sb[:, f * DF:(f + 1) * DF]))
            for c in range(32):
                sem_st.inc(s.dma_start(out=C.wdt[c * 128:(c + 1) * 128, :],
                                       in_=wd_sb[:, c * E:(c + 1) * E]))

    for q in range(NQ):
        def xq(f):
            return x_q[:, f * W:(f + 1) * W]

        def xnq(f):
            return xn_q[:, f * W:(f + 1) * W]

        @blk.sync
        def _(s, q=q):
            sem_st.wait(s)
            sem_v.wait(s)
            for me in (0, 1):
                tb = me * TC + q * W
                with s.If_eq(C.slot_s, me):
                    for f in range(NF):
                        s.dma_start(out=xq(f),
                                    in_=x_src[f * 128:(f + 1) * 128, tb:tb + W]).then_inc(sem_x.h, 16)
                s.end_ifs()
            sem_x.n += NF * 16

        @blk.scalar
        def _(a, q=q):
            sem_x.wait(a)
            sem_mm.wait(a)
            sem_v.wait(a)
            sem_a.inc(ln_squares(a, ln, [xq(f) for f in range(NF)], W), 1)

        @blk.tensor
        def _(t, q=q):
            sem_a.wait(t)
            sem_mm.inc(ln_stats(t, ln, [xq(f) for f in range(NF)], W), 1)

        @blk.vector
        def _(v, q=q):
            sem_mm.wait(v)
            sem_v.inc(ln_rows(v, ln, W), 1)

        @blk.scalar
        def _(a, q=q):
            sem_v.wait(a)
            sem_a.inc(ln_rstd(a, ln, W), 1)

        @blk.scalar
        def _(a, q=q):
            sem_a.wait(a)
            sem_a.inc(a.activation(out=ln.row[0:1, 3 * W:4 * W],
                                   in_=ln.row[0:1, W:2 * W], func=AF.Sqrt), 1)

        @blk.vector
        def _(v, q=q):
            sem_a.wait(v)
            rstd = ln.row[0:1, 2 * W:3 * W]
            sq_v = ln.row[0:1, 3 * W:4 * W]
            v.tensor_tensor(out=sq_v, in0=sq_v, in1=rstd, op=OP.mult)
            v.drain()
            v.tensor_scalar(out=sq_v, in0=sq_v, scalar1=EPS2, scalar2=None, op0=OP.add)
            v.drain()
            v.reciprocal(out=sq_v, in_=sq_v)
            v.drain()
            sem_v.inc(v.tensor_tensor(out=rstd, in0=rstd, in1=sq_v, op=OP.mult), 1)

        @blk.tensor
        def _(t, q=q):
            sem_v.wait(t)
            sem_mm.inc(ln_bcast(t, ln, W), 1)

        @blk.vector
        def _(v, q=q):
            sem_mm.wait(v)
            for f in range(NF):
                v.tensor_tensor(out=ln.sq[:, f * ln.W:f * ln.W + W], in0=xq(f),
                                in1=ln.mb[:, 0:W], op=OP.subtract)
            v.drain()
            for f in range(NF):
                i = v.tensor_tensor(out=xnq(f), in0=ln.sq[:, f * ln.W:f * ln.W + W],
                                    in1=ln.rb[:, 0:W], op=OP.mult)
            sem_v.inc(i, 1)

        a_base = sem_a.n
        mm_base = sem_mm.n

        @blk.tensor
        def _(t, q=q, a_base=a_base):
            sem_v.wait(t)
            for oc in range(32):
                if oc >= 2:
                    t.wait_ge(sem_a.h, a_base + (oc - 1))
                for f in range(NF):
                    i = t.matmul(pu[oc % 2], wu_sb[:, f * DF + oc * 128:f * DF + (oc + 1) * 128],
                                 xnq(f), start=(f == 0), stop=(f == NF - 1))
                sem_mm.inc(i, 1)

        @blk.scalar
        def _(a, q=q, a_base=a_base, mm_base=mm_base):
            for oc in range(32):
                a.wait_ge(sem_mm.h, mm_base + oc + 1)
                sem_a.inc(a.activation(out=hg[:, oc * W:(oc + 1) * W],
                                       in_=pu[oc % 2], func=AF.Gelu), 1)

        a2_base = sem_a.n
        mm2_base = sem_mm.n

        @blk.scalar
        def _(a, q=q, a2_base=a2_base, mm2_base=mm2_base):
            a.drain()
            for kk in range(32):
                a.wait_ge(sem_mm.h, mm2_base + max(0, kk))
                sem_a.inc(a.activation(out=hsq[:], in_=hg[:, kk * W:(kk + 1) * W],
                                       func=AF.Square), 1)

        @blk.tensor
        def _(t, q=q, a2_base=a2_base):
            for kk in range(32):
                t.wait_ge(sem_a.h, a2_base + kk + 1)
                i = t.matmul(ln.s2[:, 0:W], C.ones_c[:], hsq[:],
                             start=(kk == 0), stop=(kk == 31))
                sem_mm.inc(i, 1)

        @blk.scalar
        def _(a, q=q):
            sem_mm.wait(a)
            sem_a.inc(a.activation(out=ln.row[0:1, 3 * W:4 * W], in_=ln.s2[:, 0:W],
                                   func=AF.Sqrt, scale=1.0 / DF), 1)

        @blk.vector
        def _(v, q=q):
            sem_a.wait(v)
            r = ln.row[0:1, 3 * W:4 * W]
            v.tensor_scalar(out=r, in0=r, scalar1=EPS2, scalar2=None, op0=OP.add)
            v.drain()
            sem_v.inc(v.reciprocal(out=r, in_=r), 1)

        @blk.tensor
        def _(t, q=q):
            sem_v.wait(t)
            sem_mm.inc(t.matmul(hs_ps[:, 0:W], C.ones_r[:], ln.row[0:1, 3 * W:4 * W],
                                start=True, stop=True), 1)

        @blk.vector
        def _(v, q=q):
            sem_mm.wait(v)
            for kk in range(32):
                i = v.tensor_tensor(out=hg[:, kk * W:(kk + 1) * W],
                                    in0=hg[:, kk * W:(kk + 1) * W],
                                    in1=hs_ps[:, 0:W], op=OP.mult)
            sem_v.inc(i, 1)

        mm3_base = sem_mm.n
        v3_base = sem_v.n

        @blk.tensor
        def _(t, q=q, v3_base=v3_base):
            sem_v.wait(t)
            for f in range(NF):
                if f >= 2:
                    t.wait_ge(sem_v.h, v3_base + (f - 1))
                for oc in range(32):
                    i = t.matmul(pd[f % 2], wd_sb[:, oc * E + f * 128:oc * E + (f + 1) * 128],
                                 hg[:, oc * W:(oc + 1) * W],
                                 start=(oc == 0), stop=(oc == 31))
                sem_mm.inc(i, 1)

        @blk.vector
        def _(v, q=q, mm3_base=mm3_base):
            for f in range(NF):
                v.wait_ge(sem_mm.h, mm3_base + f + 1)
                sem_v.inc(v.tensor_tensor(out=xq(f), in0=xq(f), in1=pd[f % 2],
                                          op=OP.add), 1)

        @blk.sync
        def _(s, q=q, final=final):
            sem_v.wait(s)
            if final:
                for f in range(NF):
                    s.dma_start(out=C.y[f * 128:(f + 1) * 128, q * W:(q + 1) * W],
                                in_=xq(f)).then_inc(sem_st.h, 16)
                sem_st.n += NF * 16
            else:
                for me in (0, 1):
                    tb = me * TC + q * W
                    with s.If_eq(C.slot_s, me):
                        for f in range(NF):
                            s.dma_start(out=x_dst[f * 128:(f + 1) * 128, tb:tb + W],
                                        in_=xq(f)).then_inc(sem_st.h, 16)
                    s.end_ifs()
                sem_st.n += NF * 16


def emit_ab_p1(C):
    blk = C.blk
    W = 512
    NQ = T // W
    SC = 0.125
    POLY = (-0.037, 0.481, -2.3, 4.917, -1.791)
    sem_w, sem_b, sem_x = C.sem_w, C.sem_b, C.sem_x
    sem_mm, sem_v, sem_a, sem_st = C.sem_mm, C.sem_v, C.sem_a, C.sem_st
    sem_g = C.sem_g

    C.abf.reset(); C.af32.reset()
    fq = C.abf.take(4 * W)
    fk = C.abf.take(4 * T)
    fv = C.abf.take(32 * 520)
    lbuf = C.abf.take(32 * W)
    et = C.abf.take(3 * W)
    elt = C.abf.take(2 * W)
    pt = C.abf.take(3 * W)
    lb2 = C.abf.take(3 * W)
    ho_sb = C.abf.take(4 * W)
    wq_sb = C.abf.take(NF * 512)
    wk_sb = C.abf.take(NF * 512)
    wv_sb = C.abf.take(NF * 512)
    xn_q = C.abf.take(NF * W)
    x_q = C.af32.take(NF * W)
    sq = C.af32.take(NF * W)
    dm_sb = C.af32.take(4 * W)
    o_fp = C.af32.take(W)[0:64, :]
    rowa = C.lnrow[0:1, 0:2048]
    ln = _mk_ln(C, sq, W)

    l_ps = [C.pA[:, 0:W], C.pB[:, 0:W], C.pR[:, 0:W]]
    outp_ps = C.pC[0:65, 0:W]
    z_ps = C.pZ[0:1, 0:W]
    e_ps = C.pE[0:1, 0:W]
    beta_ps = C.pM[:, 0:W]
    rdb_ps = C.pR[0:64, 0:W]
    drow = C.lnrow[0:1, 1024:1536]

    @blk.gpsimd
    def _(g):
        sem_st.wait(g)
        sem_v.wait(g)
        sem_a.wait(g)
        for f in range(NF):
            sem_w.inc(g.dma_start(out=wq_sb[:, f * 512:(f + 1) * 512],
                                  in_=C.ab_wq[f * 128:(f + 1) * 128, :]))
            sem_w.inc(g.dma_start(out=wk_sb[:, f * 512:(f + 1) * 512],
                                  in_=C.ab_wk[f * 128:(f + 1) * 128, :]))
            sem_w.inc(g.dma_start(out=wv_sb[:, f * 512:(f + 1) * 512],
                                  in_=C.ab_wv[f * 128:(f + 1) * 128, :]))
        sem_g.inc(g.memset(fv[:], 1.0), 1)

    @blk.sync
    def _(s):
        sem_st.wait(s)
        sem_v.wait(s)
        sem_b.inc(s.dma_start(out=dm_sb[:], in_=C.dmask[:]))

    def emit_lnxn(q):
        tsl = slice(q * W, (q + 1) * W)

        def xq(f):
            return x_q[:, f * W:(f + 1) * W]

        def xnq(f):
            return xn_q[:, f * W:(f + 1) * W]

        @blk.sync
        def _(s, tsl=tsl, q=q):
            sem_v.wait(s)
            sem_a.wait(s)
            sem_w.wait(s)
            sem_b.wait(s)
            sem_g.wait(s)
            C.sem_bar.wait(s)
            for f in range(NF):
                sem_x.inc(s.dma_start(out=xq(f), in_=C.xsh[f * 128:(f + 1) * 128, tsl]))

        emit_ln_chunk(C, ln, [xq(f) for f in range(NF)], [xnq(f) for f in range(NF)], W)
        return xnq

    for q in range(NQ):
        xnq = emit_lnxn(q)
        for r in range(4):
            mmb = sem_mm.n

            @blk.tensor
            def _(t, q=q, r=r):
                sem_v.wait(t)
                for f in range(NF):
                    i = t.matmul(l_ps[0], wk_sb[:, f * 512 + r * 128:f * 512 + (r + 1) * 128],
                                 xnq(f), start=(f == 0), stop=(f == NF - 1))
                sem_mm.inc(i, 1)

            @blk.vector
            def _(v, q=q, r=r, mmb=mmb):
                v.wait_ge(sem_mm.h, mmb + 1)
                sem_v.inc(v.tensor_copy(out=fk[:, r * T + q * W:r * T + (q + 1) * W],
                                        in_=l_ps[0]), 1)

        for j in range(4):
            mmb = sem_mm.n

            @blk.tensor
            def _(t, q=q, j=j):
                sem_v.wait(t)
                for f in range(NF):
                    i = t.matmul(l_ps[0], xnq(f)[:, j * 128:(j + 1) * 128],
                                 wv_sb[:, f * 512:(f + 1) * 512],
                                 start=(f == 0), stop=(f == NF - 1))
                sem_mm.inc(i, 1)

            @blk.vector
            def _(v, q=q, j=j, mmb=mmb):
                v.wait_ge(sem_mm.h, mmb + 1)
                jj = q * 4 + j
                for h8 in range(8):
                    i = v.tensor_copy(out=fv[:, jj * 520 + h8 * 65:jj * 520 + h8 * 65 + 64],
                                      in_=l_ps[0][:, h8 * 64:(h8 + 1) * 64])
                sem_v.inc(i, 1)

    prev_z_last = [0]
    for qc in range(NQ):
        qsl = slice(qc * W, (qc + 1) * W)
        NKT = qc * 4 + 4
        xnq = emit_lnxn(qc)

        for r in range(4):
            mmb = sem_mm.n

            @blk.tensor
            def _(t, qc=qc, r=r):
                sem_v.wait(t)
                for f in range(NF):
                    i = t.matmul(l_ps[0], wq_sb[:, f * 512 + r * 128:f * 512 + (r + 1) * 128],
                                 xnq(f), start=(f == 0), stop=(f == NF - 1))
                sem_mm.inc(i, 1)

            @blk.vector
            def _(v, qc=qc, r=r, mmb=mmb):
                v.wait_ge(sem_mm.h, mmb + 1)
                sem_v.inc(v.tensor_copy(out=fq[:, r * W:(r + 1) * W], in_=l_ps[0]), 1)

        for h in range(8):
            pr, par = h // 2, (h % 2) * 64
            hv = slice(par, par + 64)
            a_base = sem_a.n
            v_base = sem_v.n
            g_base = sem_g.n
            pzl = prev_z_last[0]
            sc_marks = {}
            vmask = {kt: v_base + (kt - 4 * qc) + 1
                     for kt in range(4 * qc, min(NKT, 4 * qc + 4))}

            @blk.tensor
            def _(t, qc=qc, h=h, pr=pr, hv=hv, NKT=NKT, a_base=a_base,
                  g_base=g_base, sc_marks=sc_marks):
                sem_v.wait(t)
                for kt in range(NKT + 2):
                    if kt < NKT:
                        if kt >= 3:
                            t.wait_ge(sem_a.h, a_base + 2 * (kt - 3) + 2)
                        i = t.matmul(l_ps[kt % 3],
                                     fk[:, pr * T + kt * 128:pr * T + (kt + 1) * 128][hv, :],
                                     fq[:, pr * W:(pr + 1) * W][hv, :],
                                     start=True, stop=True)
                        sem_mm.inc(i, 1)
                        sc_marks[kt] = sem_mm.n
                    kd = kt - 2
                    if 0 <= kd:
                        t.wait_ge(sem_v.h, v_base + kd + 1)
                        t.matmul(z_ps, C.ones_cb[:], et[:, kd % 3 * W:(kd % 3 + 1) * W],
                                 start=(kd == 0), stop=(kd == NKT - 1))
                        i = t.matmul(e_ps, C.ones_cb[:], elt[:, kd % 2 * W:(kd % 2 + 1) * W],
                                     start=(kd == 0), stop=(kd == NKT - 1))
                        sem_mm.inc(i, 1)
                        sc_marks[("z", kd)] = sem_mm.n

            @blk.scalar
            def _(a, qc=qc, h=h, NKT=NKT, sc_marks=sc_marks):
                for kt in range(NKT):
                    a.wait_ge(sem_mm.h, sc_marks[kt])
                    if kt >= 3:
                        a.wait_ge(sem_mm.h, sc_marks[("z", kt - 3)])
                    sem_a.inc(a.activation(out=et[:, kt % 3 * W:(kt % 3 + 1) * W],
                                           in_=l_ps[kt % 3], func=AF.Exp,
                                           scale=SC), 1)
                    sem_a.inc(a.activation(out=lbuf[:, kt * W:(kt + 1) * W],
                                           in_=l_ps[kt % 3], func=AF.Copy,
                                           scale=SC), 1)

            @blk.vector
            def _(v, qc=qc, h=h, NKT=NKT, a_base=a_base, sc_marks=sc_marks):
                for kt in range(NKT):
                    d = kt - qc * 4
                    v.wait_ge(sem_a.h, a_base + 2 * kt + 2)
                    if kt >= 2:
                        v.wait_ge(sem_mm.h, sc_marks[("z", kt - 2)])
                    if d >= 0:
                        v.tensor_tensor(out=et[:, kt % 3 * W:(kt % 3 + 1) * W],
                                        in0=et[:, kt % 3 * W:(kt % 3 + 1) * W],
                                        in1=dm_sb[:, d * W:(d + 1) * W], op=OP.mult)
                        v.drain()
                    i = v.tensor_tensor(out=elt[:, kt % 2 * W:(kt % 2 + 1) * W],
                                        in0=et[:, kt % 3 * W:(kt % 3 + 1) * W],
                                        in1=lbuf[:, kt * W:(kt + 1) * W], op=OP.mult)
                    sem_v.inc(i, 1)

            mm3 = sem_mm.n

            @blk.vector
            def _(v, qc=qc, h=h, mm3=mm3):
                v.wait_ge(sem_mm.h, mm3)
                t1 = rowa[0:1, 2 * W:2 * W + W]
                t2 = rowa[0:1, 3 * W:3 * W + W]
                v.reciprocal(out=t1, in_=z_ps)
                v.drain()
                v.tensor_tensor(out=t2, in0=e_ps, in1=t1, op=OP.mult)
                sem_v.inc(v.drain(), 1)

            @blk.scalar
            def _(a, qc=qc, h=h):
                sem_v.wait(a)
                sem_a.inc(a.activation(out=rowa[0:1, 0:W], in_=z_ps, func=AF.Ln), 1)

            @blk.vector
            def _(v, qc=qc, h=h):
                sem_a.wait(v)
                Hrow = rowa[0:1, 0:W]
                brow = rowa[0:1, W:2 * W]
                t2 = rowa[0:1, 3 * W:3 * W + W]
                v.tensor_tensor(out=Hrow, in0=Hrow, in1=t2, op=OP.subtract)
                v.drain()
                v.tensor_scalar(out=brow, in0=Hrow, scalar1=POLY[0], scalar2=POLY[1],
                                op0=OP.mult, op1=OP.add)
                v.drain()
                for cc in POLY[2:]:
                    v.tensor_tensor(out=brow, in0=brow, in1=Hrow, op=OP.mult)
                    v.drain()
                    v.tensor_scalar(out=brow, in0=brow, scalar1=cc, scalar2=None,
                                    op0=OP.add)
                    v.drain()
                v.tensor_scalar(out=brow, in0=brow, scalar1=1.0, scalar2=None,
                                op0=OP.max)
                v.tensor_scalar(out=t2, in0=Hrow, scalar1=0.5, scalar2=None,
                                op0=OP.is_gt)
                v.drain()
                v.tensor_scalar(out=brow, in0=brow, scalar1=1.0, scalar2=None,
                                op0=OP.subtract)
                v.drain()
                v.tensor_tensor(out=brow, in0=brow, in1=t2, op=OP.mult)
                v.drain()
                i = v.tensor_scalar(out=brow, in0=brow, scalar1=1.0, scalar2=None,
                                    op0=OP.add)
                sem_v.inc(i, 1)

            vb4 = sem_v.n

            @blk.tensor
            def _(t, qc=qc, h=h, vb4=vb4):
                t.wait_ge(sem_v.h, vb4)
                sem_mm.inc(t.matmul(beta_ps, C.ones_r[:], rowa[0:1, W:2 * W],
                                    start=True, stop=True), 1)

            mm4 = sem_mm.n
            a2_base = sem_a.n
            v2_base = sem_v.n

            @blk.vector
            def _(v, qc=qc, h=h, NKT=NKT, mm4=mm4, a2_base=a2_base, v2_base=v2_base):
                v.wait_ge(sem_mm.h, mm4)
                for kt in range(NKT):
                    if kt >= 3:
                        v.wait_ge(sem_a.h, a2_base + (kt - 3) + 1)
                    i = v.tensor_tensor(out=lb2[:, kt % 3 * W:(kt % 3 + 1) * W],
                                        in0=lbuf[:, kt * W:(kt + 1) * W],
                                        in1=beta_ps, op=OP.mult)
                    sem_v.inc(i, 1)

            @blk.scalar
            def _(a, qc=qc, h=h, NKT=NKT, mm4=mm4, v2_base=v2_base):
                for kt in range(NKT):
                    a.wait_ge(sem_v.h, v2_base + kt + 1)
                    if kt >= 3:
                        a.wait_ge(sem_mm.h, mm4 + (kt - 3) + 1)
                    sem_a.inc(a.activation(out=pt[:, kt % 3 * W:(kt % 3 + 1) * W],
                                           in_=lb2[:, kt % 3 * W:(kt % 3 + 1) * W],
                                           func=AF.Exp), 1)

            v3_base = sem_v.n

            @blk.vector
            def _(v, qc=qc, h=h, NKT=NKT, a2_base=a2_base):
                for idx, kt in enumerate(range(qc * 4, NKT)):
                    v.wait_ge(sem_a.h, a2_base + kt + 1)
                    i = v.tensor_tensor(out=pt[:, kt % 3 * W:(kt % 3 + 1) * W],
                                        in0=pt[:, kt % 3 * W:(kt % 3 + 1) * W],
                                        in1=dm_sb[:, idx * W:(idx + 1) * W], op=OP.mult)
                    sem_v.inc(i, 1)

            @blk.tensor
            def _(t, qc=qc, h=h, NKT=NKT, a2_base=a2_base, v3_base=v3_base):
                for kt in range(NKT):
                    d = kt - qc * 4
                    if d >= 0:
                        t.wait_ge(sem_v.h, v3_base + d + 1)
                    else:
                        t.wait_ge(sem_a.h, a2_base + kt + 1)
                    i = t.matmul(outp_ps[:],
                                 fv[:, kt * 520 + h * 65:kt * 520 + h * 65 + 65],
                                 pt[:, kt % 3 * W:(kt % 3 + 1) * W],
                                 start=(kt == 0), stop=(kt == NKT - 1))
                    sem_mm.inc(i, 1)

            mm6 = sem_mm.n

            @blk.vector
            def _(v, qc=qc, h=h, mm6=mm6):
                v.wait_ge(sem_mm.h, mm6)
                v.reciprocal(out=drow[0:1, 0:W], in_=outp_ps[64:65, 0:W])
                v.drain()
                sem_v.inc(v.tensor_copy(out=o_fp[:], in_=outp_ps[0:64, 0:W]), 1)

            vb7 = sem_v.n

            @blk.tensor
            def _(t, qc=qc, h=h, vb7=vb7):
                t.wait_ge(sem_v.h, vb7)
                sem_mm.inc(t.matmul(rdb_ps, C.ones_r[0:1, 0:64], drow[0:1, 0:W],
                                    start=True, stop=True), 1)

            mm7 = sem_mm.n

            @blk.vector
            def _(v, qc=qc, h=h, pr=pr, par=par, mm7=mm7):
                v.wait_ge(sem_mm.h, mm7)
                if h == 0:
                    C.sem_st.wait(v)
                sem_v.inc(v.tensor_tensor(
                    out=ho_sb[par:par + 64, pr * W:(pr + 1) * W],
                    in0=o_fp[:], in1=rdb_ps, op=OP.mult), 1)

        @blk.sync
        def _(s, qsl=qsl, qc=qc):
            sem_v.wait(s)
            for me in (0, 1):
                with s.If_eq(C.slot_s, me):
                    for r in range(4):
                        s.dma_start(out=C.hosh[me * 512 + r * 128:me * 512 + (r + 1) * 128, qsl],
                                    in_=ho_sb[:, r * W:(r + 1) * W]).then_inc(sem_st.h, 16)
                s.end_ifs()
            sem_st.n += 4 * 16


def emit_la_p1(C):
    blk, nc = C.blk, C.nc
    W = 512
    NQ = T // W
    SC = 0.125
    ASC = SC / 1.4142135623730951
    TWO_PI = 6.283185307179586
    PI = 3.14159265358979
    MAGIC = 12582912.0
    import struct

    def trunc7(x):
        b = struct.unpack(">I", struct.pack(">f", np.float32(x)))[0]
        b &= ~((1 << 17) - 1)
        return struct.unpack(">f", struct.pack(">I", b))[0]

    c1 = trunc7(TWO_PI)
    c2 = trunc7(TWO_PI - c1)
    c3 = float(np.float32(TWO_PI - c1 - np.float64(c2)))

    sem_w, sem_b, sem_x = C.sem_w, C.sem_b, C.sem_x
    sem_mm, sem_v, sem_a, sem_g, sem_st = C.sem_mm, C.sem_v, C.sem_a, C.sem_g, C.sem_st

    C.abf.reset(); C.af32.reset()
    fq = C.abf.take(4 * T)
    fk = C.abf.take(4 * T)
    fv = C.abf.take(32 * 520)
    wq_sb = C.abf.take(NF * 512)
    wk_sb = C.abf.take(NF * 512)
    wv_sb = C.abf.take(NF * 512)
    ho_sb = C.abf.take(4 * W)
    wt = C.abf.take(3 * W)
    xn_q = C.abf.take(NF * W)
    x_q = C.af32.take(NF * W)
    sq = C.af32.take(NF * W)
    lnw_sb = C.af32.take(8)
    lnb_sb = C.af32.take(8)
    th_sb = C.af32.take(1)
    frep_sb = C.af32.take(1)
    sgn_sb = C.af32.take(1)
    frq_sb = C.af32.take(2)
    psw_sb = C.af32.take(128)
    trow = sq[:, 0:W]
    ang = sq[:, W:2 * W]
    krnd = sq[:, 2 * W:3 * W]
    red = sq[:, 3 * W:4 * W]
    qf = sq[:, 4 * W:5 * W]
    cosb = C.af32.take(W)
    sinb = C.af32.take(W)
    cs2 = C.af32.take(W)
    r1 = C.af32.take(W)
    r2 = C.af32.take(W)
    o_fp = C.af32.take(W)[0:64, :]
    csum = C.lnrow[0:1, 1024:1544]
    drow = C.lnrow[0:1, 0:512]
    ln = _mk_ln(C, sq, W)

    fv_prev = [None]
    pu = [C.pA[:, 0:W], C.pB[:, 0:W]]
    sc_ps = [C.pA[:, 0:W], C.pB[:, 0:W], C.pR[:, 0:W]]
    outp_ps = C.pC[0:65, 0:W]
    rdb_ps = C.pM[0:64, 0:W]
    qp_ps = C.pR[:, 0:W]

    @blk.gpsimd
    def _(g):
        sem_st.wait(g)
        sem_v.wait(g)
        sem_a.wait(g)
        for f in range(NF):
            sem_w.inc(g.dma_start(out=wq_sb[:, f * 512:(f + 1) * 512],
                                  in_=C.la_wq[f * 128:(f + 1) * 128, :]))
            sem_w.inc(g.dma_start(out=wk_sb[:, f * 512:(f + 1) * 512],
                                  in_=C.la_wk[f * 128:(f + 1) * 128, :]))
            sem_w.inc(g.dma_start(out=wv_sb[:, f * 512:(f + 1) * 512],
                                  in_=C.la_wv[f * 128:(f + 1) * 128, :]))
        sem_g.inc(g.memset(fv[:], 1.0), 1)

    @blk.sync
    def _(s):
        sem_st.wait(s)
        sem_v.wait(s)
        sem_b.inc(s.dma_start(out=lnw_sb[:], in_=C.la_lnw[:]))
        sem_b.inc(s.dma_start(out=lnb_sb[:], in_=C.la_lnb[:]))
        sem_b.inc(s.dma_start(out=th_sb[:], in_=C.la_th[:]))
        sem_b.inc(s.dma_start(out=frep_sb[:], in_=C.la_frep[:]))
        sem_b.inc(s.dma_start(out=sgn_sb[:], in_=C.la_sgn[:]))
        sem_b.inc(s.dma_start(out=psw_sb[:], in_=C.la_psw[:]))

    @blk.vector
    def _(v):
        sem_b.wait(v)
        v.tensor_copy(out=frq_sb[:, 0:1], in_=C.c220[:])
        v.drain()
        v.reciprocal(out=frq_sb[:, 0:1], in_=frq_sb[:, 0:1])
        v.drain()
        v.tensor_tensor(out=r1[:, 0:1], in0=th_sb[:], in1=frq_sb[:, 0:1], op=OP.mult)
        v.drain()
        v.tensor_scalar(out=r2[:, 0:1], in0=r1[:, 0:1], scalar1=220.0, scalar2=None,
                        op0=OP.mult)
        v.drain()
        v.tensor_tensor(out=r2[:, 0:1], in0=th_sb[:], in1=r2[:, 0:1], op=OP.subtract)
        v.drain()
        v.tensor_tensor(out=r2[:, 0:1], in0=r2[:, 0:1], in1=frq_sb[:, 0:1], op=OP.mult)
        v.drain()
        v.tensor_tensor(out=r1[:, 0:1], in0=r1[:, 0:1], in1=r2[:, 0:1], op=OP.add)
        v.drain()
        sem_v.inc(v.tensor_tensor(out=frq_sb[:, 1:2], in0=r1[:, 0:1],
                                  in1=frep_sb[:], op=OP.mult), 1)

    for q in range(NQ):
        tsl = slice(q * W, (q + 1) * W)

        def xq(f):
            return x_q[:, f * W:(f + 1) * W]

        def xnq(f):
            return xn_q[:, f * W:(f + 1) * W]

        @blk.sync
        def _(s, tsl=tsl, q=q):
            sem_v.wait(s)
            sem_a.wait(s)
            sem_w.wait(s)
            sem_g.wait(s)
            C.sem_bar.wait(s)
            for f in range(NF):
                sem_x.inc(s.dma_start(out=xq(f), in_=C.xsh[f * 128:(f + 1) * 128, tsl]))

        @blk.gpsimd
        def _(g, q=q):
            sem_v.wait(g)
            sem_g.inc(g.iota(trow[:], pattern=[[1, W]], base=q * W,
                             channel_multiplier=0,
                             allow_small_or_imprecise_dtypes=True), 1)

        @blk.vector
        def _(v, q=q):
            sem_g.wait(v)
            sem_a.wait(v)
            v.tensor_scalar(out=ang[:], in0=trow[:], scalar1=frq_sb[:, 1:2],
                            scalar2=None, op0=OP.mult)
            v.drain()
            v.tensor_scalar(out=krnd[:], in0=ang[:], scalar1=1.0 / TWO_PI,
                            scalar2=MAGIC, op0=OP.mult, op1=OP.add)
            v.drain()
            v.tensor_scalar(out=krnd[:], in0=krnd[:], scalar1=MAGIC,
                            scalar2=None, op0=OP.subtract)
            v.drain()
            v.tensor_scalar(out=red[:], in0=krnd[:], scalar1=c1, scalar2=None, op0=OP.mult)
            v.drain()
            v.tensor_tensor(out=red[:], in0=ang[:], in1=red[:], op=OP.subtract)
            v.drain()
            v.tensor_scalar(out=r1[:], in0=krnd[:], scalar1=c2, scalar2=None, op0=OP.mult)
            v.drain()
            v.tensor_tensor(out=red[:], in0=red[:], in1=r1[:], op=OP.subtract)
            v.drain()
            v.tensor_scalar(out=r1[:], in0=krnd[:], scalar1=c3, scalar2=None, op0=OP.mult)
            v.drain()
            v.tensor_tensor(out=red[:], in0=red[:], in1=r1[:], op=OP.subtract)
            v.drain()
            v.tensor_scalar(out=r1[:], in0=red[:], scalar1=PI, scalar2=None, op0=OP.is_gt)
            v.tensor_scalar(out=r2[:], in0=red[:], scalar1=-PI, scalar2=None, op0=OP.is_lt)
            v.drain()
            v.tensor_tensor(out=r1[:], in0=r2[:], in1=r1[:], op=OP.subtract)
            v.drain()
            v.scalar_tensor_tensor(out=r1[:], in0=r1[:], scalar=TWO_PI, in1=red[:],
                                   op0=OP.mult, op1=OP.add)
            v.tensor_scalar(out=r2[:], in0=red[:], scalar1=1.5707963267948966,
                            scalar2=None, op0=OP.add)
            v.drain()
            v.tensor_scalar(out=red[:], in0=r2[:], scalar1=PI, scalar2=None, op0=OP.is_gt)
            v.tensor_scalar(out=krnd[:], in0=r2[:], scalar1=-PI, scalar2=None, op0=OP.is_lt)
            v.drain()
            v.tensor_tensor(out=red[:], in0=krnd[:], in1=red[:], op=OP.subtract)
            v.drain()
            i = v.scalar_tensor_tensor(out=r2[:], in0=red[:], scalar=TWO_PI, in1=r2[:],
                                       op0=OP.mult, op1=OP.add)
            sem_v.inc(i, 1)

        @blk.scalar
        def _(a, q=q):
            sem_v.wait(a)
            a.activation(out=sinb[:], in_=r1[:], func=AF.Sin)
            sem_a.inc(a.activation(out=cosb[:], in_=r2[:], func=AF.Sin), 1)

        @blk.vector
        def _(v, q=q):
            sem_a.wait(v)
            sem_v.inc(v.tensor_scalar(out=cs2[:], in0=sinb[:], scalar1=sgn_sb[:, 0:1],
                                      scalar2=None, op0=OP.mult), 1)

        emit_ln_chunk(C, ln, [xq(f) for f in range(NF)], [xnq(f) for f in range(NF)],
                      W, affine=(lambda f: lnw_sb[:, f:f + 1], lambda f: lnb_sb[:, f:f + 1]))

        vb_ln = sem_v.n
        mm_b = sem_mm.n
        qf8 = [sq[:, (4 + i) * W:(5 + i) * W] for i in range(4)]
        pw = [C.pC[:, 0:W], C.pM[:, 0:W]]
        v_order = []
        for r8 in range(8):
            v_order.append(("cp", r8))
            if r8 >= 2:
                v_order.append(("rp", r8 - 2))
        v_order += [("rp", 6), ("rp", 7)] + [("fv", j) for j in range(4)]
        vmk2 = {it: vb_ln + i + 1 for i, it in enumerate(v_order)}
        t_order = [("pj", 0), ("pj", 1)]
        for r8 in range(2, 8):
            t_order += [("pj", r8), ("pw", r8 - 2)]
        t_order += [("pw", 6), ("pw", 7)] + [("vj", j) for j in range(4)]
        tmk = {it: mm_b + i + 1 for i, it in enumerate(t_order)}
        pj_mark = {r8: tmk[("pj", r8)] for r8 in range(8)}
        pw_mark = {r8: tmk[("pw", r8)] for r8 in range(8)}
        vj_mark = {j: tmk[("vj", j)] for j in range(4)}
        fvp = fv_prev[0]

        @blk.tensor
        def _(t, q=q, vmk2=vmk2, fvp=fvp, t_order=t_order):
            t.wait_ge(sem_v.h, vb_ln)
            for it in t_order:
                kind, r8 = it
                if kind == "pj":
                    wsb = wq_sb if r8 < 4 else wk_sb
                    rr = r8 % 4
                    if r8 >= 2:
                        t.wait_ge(sem_v.h, vmk2[("cp", r8 - 2)])
                    elif fvp is not None:
                        t.wait_ge(sem_v.h, fvp[2 + r8])
                    for f in range(NF):
                        i = t.matmul(pu[r8 % 2],
                                     wsb[:, f * 512 + rr * 128:f * 512 + (rr + 1) * 128],
                                     xnq(f), start=(f == 0), stop=(f == NF - 1))
                    sem_mm.inc(i, 1)
                elif kind == "pw":
                    t.wait_ge(sem_v.h, vmk2[("cp", r8)])
                    if r8 >= 2:
                        t.wait_ge(sem_v.h, vmk2[("rp", r8 - 2)])
                    sem_mm.inc(t.matmul(pw[r8 % 2], psw_sb[:], qf8[r8 % 4][:],
                                        start=True, stop=True), 1)
                else:
                    j = r8
                    if j >= 2:
                        t.wait_ge(sem_v.h, vmk2[("fv", j - 2)])
                    else:
                        t.wait_ge(sem_v.h, vmk2[("rp", 4 + j)])
                    for f in range(NF):
                        i = t.matmul(pu[j % 2], xnq(f)[:, j * 128:(j + 1) * 128],
                                     wv_sb[:, f * 512:(f + 1) * 512],
                                     start=(f == 0), stop=(f == NF - 1))
                    sem_mm.inc(i, 1)

        @blk.vector
        def _(v, q=q, v_order=v_order):
            for it in v_order:
                kind, r8 = it
                if kind == "cp":
                    v.wait_ge(sem_mm.h, pj_mark[r8])
                    sem_v.inc(v.tensor_copy(out=qf8[r8 % 4][:], in_=pu[r8 % 2]), 1)
                elif kind == "rp":
                    dst = fq if r8 < 4 else fk
                    rr = r8 % 4
                    v.wait_ge(sem_mm.h, pw_mark[r8])
                    v.tensor_tensor(out=r1[:], in0=qf8[r8 % 4][:], in1=cosb[:],
                                    op=OP.mult)
                    v.tensor_tensor(out=r2[:], in0=pw[r8 % 2], in1=cs2[:], op=OP.mult)
                    v.drain()
                    i = v.tensor_tensor(out=dst[:, rr * T + q * W:rr * T + (q + 1) * W],
                                        in0=r1[:], in1=r2[:], op=OP.add)
                    sem_v.inc(i, 1)
                else:
                    j = r8
                    jj = q * 4 + j
                    v.wait_ge(sem_mm.h, vj_mark[j])
                    for h8 in range(8):
                        i = v.tensor_copy(
                            out=fv[:, jj * 520 + h8 * 65:jj * 520 + h8 * 65 + 64],
                            in_=pu[j % 2][:, h8 * 64:(h8 + 1) * 64])
                    sem_v.inc(i, 1)
        fv_prev[0] = {2: vmk2[("fv", 2)], 3: vmk2[("fv", 3)]}

    mmb = sem_mm.n
    vb = sem_v.n

    @blk.tensor
    def _(t, vb=vb):
        t.wait_ge(sem_v.h, vb)
        for jj in range(32):
            t.matmul(C.pE[0:1, 0:512], C.ones_cb[:], fv[:, jj * 520:jj * 520 + 512],
                     start=(jj == 0), stop=(jj == 31))
            i = t.matmul(C.pZ[0:1, 0:8], C.ones_cb[:], fv[:, jj * 520 + 512:(jj + 1) * 520],
                         start=(jj == 0), stop=(jj == 31))
        sem_mm.inc(i, 1)

    @blk.vector
    def _(v, mmb=mmb):
        v.wait_ge(sem_mm.h, mmb + 1)
        v.tensor_scalar(out=csum[0:1, 0:512], in0=C.pE[0:1, 0:512],
                        scalar1=0.5, scalar2=None, op0=OP.mult)
        sem_v.inc(v.tensor_scalar(out=csum[0:1, 512:520], in0=C.pZ[0:1, 0:8],
                                  scalar1=0.5, scalar2=None, op0=OP.mult), 1)

    for qc in range(NQ):
        qsl = slice(qc * W, (qc + 1) * W)
        for h in range(8):
            pr, par = h // 2, (h % 2) * 64
            hv = slice(par, par + 64)
            a_base = sem_a.n
            sc_marks = {}

            @blk.tensor
            def _(t, h=h, qc=qc, pr=pr, par=par, hv=hv, a_base=a_base,
                  sc_marks=sc_marks):
                sem_v.wait(t)
                t.matmul(outp_ps[:], csum[0:1, h * 65:h * 65 + 65], C.ones_r512[:],
                         start=True, stop=False)
                for kt in range(34):
                    if kt < 32:
                        if kt >= 3:
                            t.wait_ge(sem_a.h, a_base + kt - 2)
                        i = t.matmul(sc_ps[kt % 3],
                                     fk[:, pr * T + kt * 128:pr * T + (kt + 1) * 128][hv, :],
                                     fq[:, pr * T + qc * W:pr * T + (qc + 1) * W][hv, :],
                                     start=True, stop=True)
                        sem_mm.inc(i, 1)
                        sc_marks[kt] = sem_mm.n
                    kd = kt - 2
                    if 0 <= kd < 32:
                        t.wait_ge(sem_a.h, a_base + kd + 1)
                        i = t.matmul(outp_ps[:],
                                     fv[:, kd * 520 + h * 65:kd * 520 + h * 65 + 65],
                                     wt[:, kd % 3 * W:(kd % 3 + 1) * W],
                                     start=False, stop=(kd == 31))
                        sem_mm.inc(i, 1)

            @blk.scalar
            def _(a, h=h, qc=qc, a_base=a_base, sc_marks=sc_marks):
                for kt in range(32):
                    a.wait_ge(sem_mm.h, sc_marks[kt])
                    sem_a.inc(a.activation(out=wt[:, kt % 3 * W:(kt % 3 + 1) * W],
                                           in_=sc_ps[kt % 3], func=AF.Square,
                                           scale=ASC, bias=C.bsq[:, 0:1]), 1)

            mmb4 = sem_mm.n

            @blk.vector
            def _(v, h=h, qc=qc, mmb4=mmb4):
                v.wait_ge(sem_mm.h, mmb4)
                v.reciprocal(out=drow[0:1, 0:W], in_=outp_ps[64:65, 0:W])
                v.drain()
                sem_v.inc(v.tensor_copy(out=o_fp[:], in_=outp_ps[0:64, 0:W]), 1)

            vb5 = sem_v.n

            @blk.tensor
            def _(t, h=h, qc=qc, vb5=vb5):
                t.wait_ge(sem_v.h, vb5)
                sem_mm.inc(t.matmul(rdb_ps, C.ones_r[0:1, 0:64], drow[0:1, 0:W],
                                    start=True, stop=True), 1)

            mmb6 = sem_mm.n

            @blk.vector
            def _(v, h=h, qc=qc, pr=pr, par=par, mmb6=mmb6):
                v.wait_ge(sem_mm.h, mmb6)
                if h == 0:
                    sem_st.wait(v)
                sem_v.inc(v.tensor_tensor(
                    out=ho_sb[par:par + 64, pr * W:(pr + 1) * W],
                    in0=o_fp[:], in1=rdb_ps, op=OP.mult), 1)

        @blk.sync
        def _(s, qsl=qsl, qc=qc):
            sem_v.wait(s)
            for me in (0, 1):
                with s.If_eq(C.slot_s, me):
                    for r in range(4):
                        s.dma_start(out=C.hosh[me * 512 + r * 128:me * 512 + (r + 1) * 128, qsl],
                                    in_=ho_sb[:, r * W:(r + 1) * W]).then_inc(sem_st.h, 16)
                s.end_ifs()
            sem_st.n += 4 * 16


def emit_mb_p1(C):
    blk = C.blk
    W = 512
    NQ = T // W
    PW = W + 4
    sem_w, sem_b, sem_x = C.sem_w, C.sem_b, C.sem_x
    sem_mm, sem_v, sem_a, sem_g, sem_st = C.sem_mm, C.sem_v, C.sem_a, C.sem_g, C.sem_st

    C.abf.reset(); C.af32.reset()
    wxin_sb = C.abf.take(NF * 1024)
    wz_sb = C.abf.take(NF * 1024)
    xn_q = C.abf.take(NF * W)
    hc_sb = C.abf.take(NF * W)
    sz_sb = C.abf.take(NF * W)
    x_q = C.af32.take(NF * W)
    sq = C.af32.take(NF * W)
    xin_b = C.af32.take(NF * PW)
    cacc = sq[:, 8:8 + W]
    cw_sb = C.af32.take(32)
    cb_sb = C.af32.take(8)
    ln = _mk_ln(C, sq, W)
    pu = [C.pA[:, 0:W], C.pB[:, 0:W]]
    pz = C.pC[:, 0:W]

    @blk.gpsimd
    def _(g):
        sem_st.wait(g)
        sem_v.wait(g)
        sem_a.wait(g)
        for f in range(NF):
            sem_w.inc(g.dma_start(out=wxin_sb[:, f * 1024:(f + 1) * 1024],
                                  in_=C.mb_wxin[f * 128:(f + 1) * 128, :]))
            sem_w.inc(g.dma_start(out=wz_sb[:, f * 1024:(f + 1) * 1024],
                                  in_=C.mb_wz[f * 128:(f + 1) * 128, :]))
        sem_g.inc(g.memset(xin_b[:], 0.0), 1)

    @blk.sync
    def _(s):
        sem_st.wait(s)
        sem_v.wait(s)
        sem_b.inc(s.dma_start(out=cw_sb[:], in_=C.mb_cw[:]))
        sem_b.inc(s.dma_start(out=cb_sb[:], in_=C.mb_cb[:]))

    for q in range(NQ):
        tsl = slice(q * W, (q + 1) * W)

        def xq(f):
            return x_q[:, f * W:(f + 1) * W]

        def xnq(f):
            return xn_q[:, f * W:(f + 1) * W]

        @blk.sync
        def _(s, tsl=tsl, q=q):
            sem_v.wait(s)
            sem_a.wait(s)
            sem_w.wait(s)
            sem_b.wait(s)
            sem_g.wait(s)
            C.sem_bar.wait(s)
            if q > 0:
                sem_st.wait(s)
            for f in range(NF):
                sem_x.inc(s.dma_start(out=xq(f), in_=C.xsh[f * 128:(f + 1) * 128, tsl]))

        emit_ln_chunk(C, ln, [xq(f) for f in range(NF)], [xnq(f) for f in range(NF)], W)

        for f2 in range(NF):
            mmb = sem_mm.n
            vb = sem_v.n

            @blk.tensor
            def _(t, q=q, f2=f2, vb=vb):
                sem_v.wait(t)
                for f in range(NF):
                    i = t.matmul(pu[f2 % 2],
                                 wxin_sb[:, f * 1024 + f2 * 128:f * 1024 + (f2 + 1) * 128],
                                 xnq(f), start=(f == 0), stop=(f == NF - 1))
                sem_mm.inc(i, 1)

            @blk.vector
            def _(v, q=q, f2=f2, mmb=mmb):
                v.wait_ge(sem_mm.h, mmb + 1)
                sem_a.wait(v)
                buf = xin_b[:, f2 * PW:(f2 + 1) * PW]
                v.tensor_copy(out=ln.sq[:, 0:4], in_=buf[:, W:W + 4])
                v.drain()
                v.tensor_copy(out=buf[:, 0:4], in_=ln.sq[:, 0:4])
                v.drain()
                sem_v.inc(v.tensor_copy(out=buf[:, 4:4 + W], in_=pu[f2 % 2]), 1)
                v.drain()
                v.tensor_scalar(out=cacc[:], in0=buf[:, 1:1 + W],
                                scalar1=cw_sb[:, f2 * 4:f2 * 4 + 1], scalar2=None,
                                op0=OP.mult)
                for j in (1, 2, 3):
                    v.drain()
                    v.scalar_tensor_tensor(out=cacc[:], in0=buf[:, 1 + j:1 + j + W],
                                           scalar=cw_sb[:, f2 * 4 + j:f2 * 4 + j + 1],
                                           in1=cacc[:], op0=OP.mult, op1=OP.add)
                sem_v.inc(v.drain(), 1)

            vb2 = sem_v.n

            @blk.scalar
            def _(a, q=q, f2=f2, vb2=vb2):
                a.wait_ge(sem_v.h, vb2)
                sem_a.inc(a.activation(out=hc_sb[:, f2 * W:(f2 + 1) * W],
                                       in_=cacc[:], func=AF.Silu,
                                       bias=cb_sb[:, f2:f2 + 1]), 1)

        for f2 in range(NF):
            mmb = sem_mm.n

            @blk.tensor
            def _(t, q=q, f2=f2):
                sem_a.wait(t)
                for f in range(NF):
                    i = t.matmul(pz, wz_sb[:, f * 1024 + f2 * 128:f * 1024 + (f2 + 1) * 128],
                                 xnq(f), start=(f == 0), stop=(f == NF - 1))
                sem_mm.inc(i, 1)

            @blk.scalar
            def _(a, q=q, f2=f2, mmb=mmb):
                a.wait_ge(sem_mm.h, mmb + 1)
                sem_a.inc(a.activation(out=sz_sb[:, f2 * W:(f2 + 1) * W],
                                       in_=pz, func=AF.Silu), 1)

        @blk.sync
        def _(s, tsl=tsl, q=q):
            sem_a.wait(s)
            sem_v.wait(s)
            for me in (0, 1):
                with s.If_eq(C.slot_s, me):
                    for f in range(NF):
                        s.dma_start(out=C.hcsh[me * 1024 + f * 128:me * 1024 + (f + 1) * 128, tsl],
                                    in_=hc_sb[:, f * W:(f + 1) * W]).then_inc(sem_st.h, 16)
                s.end_ifs()
            sem_st.n += NF * 16
            for f in range(NF):
                sem_st.inc(s.dma_start(out=C.szp[f * 128:(f + 1) * 128, tsl],
                                       in_=sz_sb[:, f * W:(f + 1) * W]))


def emit_mb_p2(C):
    blk = C.blk
    W = 512
    NQ = T // W
    NKF = 16
    sem_w, sem_b, sem_x = C.sem_w, C.sem_b, C.sem_x
    sem_mm, sem_v, sem_a, sem_g, sem_st = C.sem_mm, C.sem_v, C.sem_a, C.sem_g, C.sem_st

    C.abf.reset(); C.af32.reset()
    wxd_sb = C.abf.take(NKF * 1024)
    hc_q = C.abf.take(NKF * W)
    Bb = C.abf.take(16 * W)
    Cb = C.abf.take(16 * W)
    sz_q = C.abf.take(NF * W)
    y_sb = C.abf.take(NF * W)
    hcn = C.abf.take(2 * W)
    hcn_p = C.abf.take(2 * W)
    dh_bf = C.abf.take(W)
    bub_bf = C.abf.take(3 * W)
    wxbc_sb = C.abf.take(512)
    bcs = C.abf.take(W)[0:32, :]
    idb = C.abf.take(128)
    sel_sb = C.abf.take(4096)[0:32, :]
    dl_q = C.af32.take(NF * W)
    dsc = C.af32.take(W)
    dsc2 = C.af32.take(W)
    abuf = C.af32.take(3 * W)
    hn_sb = C.af32.take(2 * W)
    st_sb = C.af32.take(NF * 16)
    acol = C.af32.take(NF * 16)
    dcol = C.af32.take(8)
    ysc = C.af32.take(W)
    dum = C.af32.take(1)

    pu = [C.pA[:, 0:W], C.pB[:, 0:W]]
    pbc = C.pC[0:32, 0:W]
    pbb = C.pM[:, 0:W]
    py = C.pR[:, 0:W]

    @blk.gpsimd
    def _(g):
        sem_st.wait(g)
        sem_v.wait(g)
        sem_a.wait(g)
        for f in range(NKF):
            sem_w.inc(g.dma_start(out=wxd_sb[:, f * 1024:(f + 1) * 1024],
                                  in_=C.mb_wxd[f * 128:(f + 1) * 128, :]))
        sem_w.inc(g.dma_start(out=wxbc_sb[:], in_=C.mb_wxbc[:]))
        sem_w.inc(g.dma_start(out=idb[:], in_=C.identf[:]))
        sem_w.inc(g.dma_start(out=sel_sb[:], in_=C.mb_sel[:]))
        sem_g.inc(g.memset(st_sb[:], 0.0), 1)

    @blk.sync
    def _(s):
        sem_st.wait(s)
        sem_v.wait(s)
        sem_b.inc(s.dma_start(out=acol[:], in_=C.mb_alog[:]))
        sem_b.inc(s.dma_start(out=dcol[:], in_=C.mb_dd[:]))

    @blk.scalar
    def _(a):
        sem_b.wait(a)
        sem_a.inc(a.activation(out=acol[:], in_=acol[:], func=AF.Exp), 1)

    @blk.vector
    def _(v):
        sem_a.wait(v)
        sem_v.inc(v.tensor_scalar(out=acol[:], in0=acol[:], scalar1=-1.0,
                                  scalar2=None, op0=OP.mult), 1)

    P_HCN = (1, 2, 3, 5, 7, 9, 11, 13, 14, 15)
    chain_ln = {}
    chain_copy = {}
    chain_v = {}
    proj_mark = {}
    prev_scan = None
    prev_hcnp = {}
    prev_pool_end = sem_g.n
    pf2 = None

    for q in range(NQ):
        tsl = slice(q * W, (q + 1) * W)

        @blk.sync
        def _(s, tsl=tsl, q=q):
            if q > 0:
                sem_st.wait(s)
            sem_v.wait(s)
            sem_w.wait(s)
            sem_g.wait(s)
            C.sem_bar.wait(s)
            for me in (0, 1):
                with s.If_eq(C.slot_s, me):
                    for f in range(NKF):
                        srcf = (me * 8 + f) if f < 8 else ((1 - me) * 8 + (f - 8))
                        s.dma_start(out=hc_q[:, f * W:(f + 1) * W],
                                    in_=C.hcsh[srcf * 128:(srcf + 1) * 128, tsl]).then_inc(sem_x.h, 16)
                s.end_ifs()
            sem_x.n += NKF * 16
            for f in range(NF):
                sem_x.inc(s.dma_start(out=sz_q[:, f * W:(f + 1) * W],
                                      in_=C.szp[f * 128:(f + 1) * 128, tsl]))

        mmb = sem_mm.n

        @blk.tensor
        def _(t, q=q):
            sem_x.wait(t)
            for f in range(NKF):
                i = t.matmul(pbc, wxbc_sb[:, f * 32:(f + 1) * 32],
                             hc_q[:, f * W:(f + 1) * W],
                             start=(f == 0), stop=(f == NKF - 1))
            sem_mm.inc(i, 1)
            if q > 0:
                t.wait_ge(sem_a.h, chain_copy[(q - 1, 6)])
            for f in range(NKF):
                i = t.matmul(pu[0],
                             wxd_sb[:, f * 1024 + 0 * 128:f * 1024 + 1 * 128],
                             hc_q[:, f * W:(f + 1) * W],
                             start=(f == 0), stop=(f == NKF - 1))
            sem_mm.inc(i, 1)
        proj_mark[(q, 0)] = sem_mm.n

        @blk.vector
        def _(v, q=q, mmb=mmb):
            v.wait_ge(sem_mm.h, mmb + 1)
            sem_v.inc(v.tensor_copy(out=bcs[:], in_=pbc), 1)

        vb2 = sem_v.n
        bc_marks = {}

        @blk.tensor
        def _(t, q=q, vb2=vb2, bc_marks=bc_marks):
            t.wait_ge(sem_v.h, vb2)
            for r in range(32):
                if r >= 1:
                    t.wait_ge(sem_v.h, vb2 + r)
                i = t.matmul(pbb, sel_sb[:, r * 128:(r + 1) * 128], bcs[:],
                             start=True, stop=True)
                sem_mm.inc(i, 1)
                bc_marks[r] = sem_mm.n

        @blk.vector
        def _(v, q=q, bc_marks=bc_marks):
            for r in range(32):
                v.wait_ge(sem_mm.h, bc_marks[r])
                dst = (Bb if r < 16 else Cb)
                n = r % 16
                sem_v.inc(v.tensor_copy(out=dst[:, n * W:(n + 1) * W], in_=pbb), 1)

        a_ch = sem_a.n
        pm0 = proj_mark[(q, 0)]
        cvw = chain_v.get((q - 1, 7))

        @blk.scalar
        def _(a, q=q, pm0=pm0, cvw=cvw):
            a.wait_ge(sem_mm.h, pm0)
            if cvw is not None:
                a.wait_ge(sem_v.h, cvw)
            sem_a.inc(a.activation(out=dsc[:], in_=pu[0], func=AF.Copy), 1)
            a.drain()
            sem_a.inc(a.activation(out=dsc2[:], in_=dsc[:], func=AF.Abs), 1)
            a.drain()
            sem_a.inc(a.activation(out=dsc2[:], in_=dsc2[:], func=AF.Exp, scale=-1.0), 1)
            a.drain()
            sem_a.inc(a.activation(out=dsc2[:], in_=dsc2[:], func=AF.Ln,
                                   bias=C.onec[:, 0:1]), 1)
        chain_copy[(q, 0)] = a_ch + 1
        chain_ln[(q, 0)] = a_ch + 4

        @blk.vector
        def _(v, q=q):
            v.wait_ge(sem_a.h, chain_ln[(q, 0)])
            v.tensor_scalar(out=dsc[:], in0=dsc[:], scalar1=0.0, scalar2=None,
                            op0=OP.max)
            v.drain()
            v.tensor_tensor(out=dl_q[:, 0:W],
                            in0=dsc[:], in1=dsc2[:], op=OP.add)
            sem_v.inc(v.drain(), 1)
        chain_v[(q, 0)] = sem_v.n

        for f2 in range(NF):
            first_sec = (q == 0 and f2 == 0)
            has_chain = (f2 < 7)
            v_base = sem_v.n
            a_base = sem_a.n
            g_base = sem_g.n
            mm_base = sem_mm.n

            v_list = (["chv"] if f2 > 0 else []) + ["dh"]
            for n in range(16):
                v_list.append(("scan", n))
                if n not in P_HCN:
                    v_list.append(("hcn", n))
            v_list.append("evac")
            vmk = {it: v_base + i + 1 for i, it in enumerate(v_list)}

            g_list = []
            for n in range(16):
                g_list.append(("bub", n))
                if n - 2 in P_HCN:
                    g_list.append(("hcnp", n - 2))
            for m in P_HCN:
                if m + 2 > 15:
                    g_list.append(("hcnp", m))
            gmk = {it: g_base + i + 1 for i, it in enumerate(g_list)}

            a_list = ([("dum", 0), ("dum", 1)] if first_sec
                      else [("tail", 14), ("tail", 15)])
            a_list += [("E", 0), ("E", 1),
                       ("E", 2), ("Cp", 0), ("E", 3), ("Cp", 1),
                       ("E", 4), ("Cp", 2)]
            if has_chain:
                a_list += [("CH", 0), ("CH", 1), ("CH", 2), ("CH", 3)]
            for n in range(5, 16):
                a_list += [("E", n), ("Cp", n - 2)]
            amk = {it: a_base + i + 1 for i, it in enumerate(a_list)}

            n_proj = 1 if has_chain else 0
            y_base = mm_base + n_proj
            ymk = {n: y_base + n + 1 for n in range(16)}
            if has_chain:
                proj_mark[(q, f2 + 1)] = mm_base + 1
                chain_copy[(q, f2 + 1)] = amk[("CH", 0)]
                chain_ln[(q, f2 + 1)] = amk[("CH", 3)]

            sx_cnt = sem_x.n
            ppe = prev_pool_end
            psc = prev_scan
            phc = dict(prev_hcnp)
            vset = [n for n in range(16) if n not in P_HCN]

            @blk.vector
            def _(v, q=q, f2=f2, vmk=vmk, gmk=gmk, amk=amk, ymk=ymk,
                  y_base=y_base, mm_base=mm_base, sx_cnt=sx_cnt, ppe=ppe,
                  phc=phc, vset=vset, first_sec=first_sec):
                if f2 > 0:
                    v.wait_ge(sem_a.h, chain_ln[(q, f2)])
                    v.tensor_scalar(out=dsc[:], in0=dsc[:], scalar1=0.0,
                                    scalar2=None, op0=OP.max)
                    v.drain()
                    v.tensor_tensor(out=dl_q[:, f2 * W:(f2 + 1) * W],
                                    in0=dsc[:], in1=dsc2[:], op=OP.add)
                    sem_v.inc(v.drain(), 1)
                v.wait_ge(sem_x.h, sx_cnt)
                v.wait_ge(sem_g.h, ppe)
                v.tensor_tensor(out=dh_bf[:], in0=dl_q[:, f2 * W:(f2 + 1) * W],
                                in1=hc_q[:, f2 * W:(f2 + 1) * W], op=OP.mult)
                sem_v.inc(v.drain(), 1)
                for n in range(16):
                    if n >= 2:
                        v.wait_ge(sem_a.h, amk[("Cp", n - 2)])
                    else:
                        v.wait_ge(sem_a.h, amk[("E", n)])
                    if n - 2 in P_HCN:
                        v.wait_ge(sem_g.h, gmk[("hcnp", n - 2)])
                    else:
                        v.wait_ge(sem_g.h, gmk[("bub", n)])
                    if n < 2 and (14 + n) in phc:
                        v.wait_ge(sem_g.h, phc[14 + n])
                    v.tensor_tensor_scan(out=hn_sb[:, n % 2 * W:(n % 2 + 1) * W],
                                         data0=abuf[:, n % 3 * W:(n % 3 + 1) * W],
                                         data1=bub_bf[:, n % 3 * W:(n % 3 + 1) * W],
                                         initial=st_sb[:, f2 * 16 + n:f2 * 16 + n + 1],
                                         op0=OP.mult, op1=OP.add)
                    sem_v.inc(v.drain(), 1)
                    if n not in P_HCN:
                        iv = vset.index(n)
                        pv = None
                        for n2 in vset[:iv][::-1]:
                            if n2 % 2 == n % 2:
                                pv = n2
                                break
                        v.wait_ge(sem_mm.h, ymk[pv] if pv is not None else mm_base)
                        sem_v.inc(v.tensor_tensor(
                            out=hcn[:, n % 2 * W:(n % 2 + 1) * W],
                            in0=hn_sb[:, n % 2 * W:(n % 2 + 1) * W],
                            in1=Cb[:, n * W:(n + 1) * W], op=OP.mult), 1)
                v.wait_ge(sem_mm.h, ymk[15])
                v.scalar_tensor_tensor(out=ysc[:], in0=hc_q[:, f2 * W:(f2 + 1) * W],
                                       scalar=dcol[:, f2:f2 + 1], in1=py,
                                       op0=OP.mult, op1=OP.add)
                v.drain()
                sem_v.inc(v.tensor_tensor(out=y_sb[:, f2 * W:(f2 + 1) * W],
                                          in0=ysc[:], in1=sz_q[:, f2 * W:(f2 + 1) * W],
                                          op=OP.mult), 1)
            assert sem_v.n == v_base + len(v_list), (q, f2, sem_v.n, v_base, len(v_list))
            chain_v[(q, f2)] = vmk["chv"] if f2 > 0 else chain_v[(q, f2)]

            @blk.tensor
            def _(t, q=q, f2=f2, vmk=vmk, gmk=gmk, has_chain=has_chain):
                if has_chain:
                    key = (q, f2 - 1) if f2 >= 1 else (q - 1, 7)
                    if key in chain_copy:
                        t.wait_ge(sem_a.h, chain_copy[key])
                    for f in range(NKF):
                        i = t.matmul(pu[(f2 + 1) % 2],
                                     wxd_sb[:, f * 1024 + (f2 + 1) * 128:
                                            f * 1024 + (f2 + 2) * 128],
                                     hc_q[:, f * W:(f + 1) * W],
                                     start=(f == 0), stop=(f == NKF - 1))
                    sem_mm.inc(i, 1)
                for n in range(16):
                    if n in P_HCN:
                        t.wait_ge(sem_g.h, gmk[("hcnp", n)])
                        src = hcn_p[:, (P_HCN.index(n) % 2) * W:
                                    (P_HCN.index(n) % 2 + 1) * W]
                    else:
                        t.wait_ge(sem_v.h, vmk[("hcn", n)])
                        src = hcn[:, n % 2 * W:(n % 2 + 1) * W]
                    i = t.matmul(py, idb[:], src, start=(n == 0), stop=(n == 15))
                    sem_mm.inc(i, 1)
            assert sem_mm.n == mm_base + n_proj + 16

            @blk.scalar
            def _(a, q=q, f2=f2, vmk=vmk, a_list=a_list, psc=psc, pf2=pf2,
                  has_chain=has_chain, first_sec=first_sec):
                for it in a_list:
                    kind = it[0]
                    if kind == "dum":
                        sem_a.inc(a.activation(out=dum[:], in_=dsc[:, 0:1],
                                               func=AF.Copy), 1)
                    elif kind == "tail":
                        n2 = it[1]
                        a.wait_ge(sem_v.h, psc[n2])
                        sem_a.inc(a.activation(
                            out=st_sb[:, pf2 * 16 + n2:pf2 * 16 + n2 + 1],
                            in_=hn_sb[:, (n2 % 2 + 1) * W - 1:(n2 % 2 + 1) * W],
                            func=AF.Copy), 1)
                    elif kind == "E":
                        n = it[1]
                        if n >= 3:
                            a.wait_ge(sem_v.h, vmk[("scan", n - 3)])
                        else:
                            a.wait_ge(sem_v.h, vmk["dh"])
                        sem_a.inc(a.activation(
                            out=abuf[:, n % 3 * W:(n % 3 + 1) * W],
                            in_=dl_q[:, f2 * W:(f2 + 1) * W], func=AF.Exp,
                            scale=acol[:, f2 * 16 + n:f2 * 16 + n + 1]), 1)
                    elif kind == "Cp":
                        n = it[1]
                        a.wait_ge(sem_v.h, vmk[("scan", n)])
                        sem_a.inc(a.activation(
                            out=st_sb[:, f2 * 16 + n:f2 * 16 + n + 1],
                            in_=hn_sb[:, (n % 2 + 1) * W - 1:(n % 2 + 1) * W],
                            func=AF.Copy), 1)
                    else:
                        k = it[1]
                        if k == 0:
                            a.wait_ge(sem_mm.h, proj_mark[(q, f2 + 1)])
                            a.wait_ge(sem_v.h, chain_v[(q, f2)])
                            sem_a.inc(a.activation(out=dsc[:], in_=pu[(f2 + 1) % 2],
                                                   func=AF.Copy), 1)
                            a.drain()
                        elif k == 1:
                            sem_a.inc(a.activation(out=dsc2[:], in_=dsc[:],
                                                   func=AF.Abs), 1)
                            a.drain()
                        elif k == 2:
                            sem_a.inc(a.activation(out=dsc2[:], in_=dsc2[:],
                                                   func=AF.Exp, scale=-1.0), 1)
                            a.drain()
                        else:
                            sem_a.inc(a.activation(out=dsc2[:], in_=dsc2[:],
                                                   func=AF.Ln, bias=C.onec[:, 0:1]), 1)
            assert sem_a.n == a_base + len(a_list), (q, f2, sem_a.n, a_base, len(a_list))

            @blk.gpsimd
            def _(g, q=q, f2=f2, vmk=vmk, ymk=ymk, y_base=y_base,
                  mm_base=mm_base, g_list=g_list):
                g.wait_ge(sem_v.h, vmk["dh"])
                for it in g_list:
                    kind, n = it
                    if kind == "bub":
                        if n >= 3:
                            g.wait_ge(sem_v.h, vmk[("scan", n - 3)])
                        sem_g.inc(g.tensor_tensor(
                            out=bub_bf[:, n % 3 * W:(n % 3 + 1) * W],
                            in0=dh_bf[:], in1=Bb[:, n * W:(n + 1) * W],
                            op=OP.mult), 1)
                    else:
                        i2 = P_HCN.index(n)
                        g.wait_ge(sem_v.h, vmk[("scan", n)])
                        g.wait_ge(sem_mm.h, ymk[P_HCN[i2 - 2]] if i2 >= 2 else mm_base)
                        sem_g.inc(g.tensor_tensor(
                            out=hcn_p[:, (i2 % 2) * W:(i2 % 2 + 1) * W],
                            in0=hn_sb[:, n % 2 * W:(n % 2 + 1) * W],
                            in1=Cb[:, n * W:(n + 1) * W], op=OP.mult), 1)
            assert sem_g.n == g_base + len(g_list)

            prev_scan = {n: vmk[("scan", n)] for n in range(16)}
            prev_hcnp = {m: gmk[("hcnp", m)] for m in P_HCN if m + 2 > 15}
            prev_pool_end = sem_g.n
            pf2 = f2

        @blk.sync
        def _(s, tsl=tsl, q=q):
            sem_v.wait(s)
            for me in (0, 1):
                with s.If_eq(C.slot_s, me):
                    for f in range(NF):
                        s.dma_start(out=C.ysh[me * 1024 + f * 128:me * 1024 + (f + 1) * 128, tsl],
                                    in_=y_sb[:, f * W:(f + 1) * W]).then_inc(sem_st.h, 16)
                s.end_ifs()
            sem_st.n += NF * 16


def fused_in_maps(inp):
    x = np.ascontiguousarray(np.transpose(np.asarray(inp["L0_attn_a_x"]), (0, 2, 1)))
    ident = np.eye(128, dtype=np.float32)
    trimask = np.triu(np.ones((128, 128), np.float32))
    dm = np.zeros((128, 4 * 512), np.float32)
    for d in range(4):
        s_idx = 128 * d + np.arange(128)[:, None]
        dm[:, d * 512:(d + 1) * 512] = (s_idx <= np.arange(512)[None, :])
    kv = np.arange(32, dtype=np.uint32)[None, :]
    fb = _freqs_base_np()
    frep = np.ascontiguousarray(np.tile(fb, 4)[:, None])
    sgn = np.ascontiguousarray(
        np.concatenate([np.full(32, -1.0), np.full(32, 1.0)] * 2).astype(np.float32)[:, None])
    psw = np.zeros((128, 128), np.float32)
    for i in range(128):
        blk64, r = i // 64, i % 64
        psw[i, blk64 * 64 + (r + 32) % 64] = 1.0
    perm = np.concatenate([np.concatenate([h * 64 + 2 * np.arange(32),
                                           h * 64 + 2 * np.arange(32) + 1])
                           for h in range(8)])
    sel = np.zeros((32, 32 * 128), np.float32)
    for r in range(32):
        sel[r, r * 128:(r + 1) * 128] = 1.0

    aa_wq = np.asarray(inp["L0_attn_a_wq"]); aa_wk = np.asarray(inp["L0_attn_a_wk"])
    aa_wv = np.asarray(inp["L0_attn_a_wv"]); aa_wo = np.ascontiguousarray(inp["L0_attn_a_wo"])
    ab_wqT = np.ascontiguousarray(np.asarray(inp["L0_attn_b_Wq"]).T)
    ab_wkT = np.ascontiguousarray(np.asarray(inp["L0_attn_b_Wk"]).T)
    ab_wvT = np.ascontiguousarray(np.asarray(inp["L0_attn_b_Wv"]).T)
    ab_woT = np.ascontiguousarray(np.asarray(inp["L0_attn_b_Wo"]).T)
    la_wq = np.asarray(inp["L1_attn_a_w_q"]); la_wkv = np.asarray(inp["L1_attn_a_w_kv"])
    la_wo = np.ascontiguousarray(np.asarray(inp["L1_attn_a_w_o"]))
    W_inT = np.ascontiguousarray(np.asarray(inp["L1_attn_b_W_in"]).T)
    conv_w = np.asarray(inp["L1_attn_b_conv_w"])[:, 0, :]
    conv_b = np.asarray(inp["L1_attn_b_conv_b"])
    W_xT = np.ascontiguousarray(np.asarray(inp["L1_attn_b_W_x"]).T)
    A_log = np.asarray(inp["L1_attn_b_A_log"])
    Dp = np.asarray(inp["L1_attn_b_D"])
    W_outT = np.ascontiguousarray(np.asarray(inp["L1_attn_b_W_out"]).T)
    theta = np.full((128, 1), np.asarray(inp["L1_attn_a_theta"]), np.float32)
    _ = (np.asarray(inp["L0_ffn_router_w"]).shape, np.asarray(inp["L0_ffn_router_b"]).shape)

    in_maps = []
    for c in range(8):
        b, s = c // 2, c % 2
        cols = slice(s * 512, (s + 1) * 512)
        esl = slice(s * 1024, (s + 1) * 1024)
        perm_rows = np.concatenate([np.arange(s * 1024, (s + 1) * 1024),
                                    np.arange((1 - s) * 1024, (2 - s) * 1024)])
        cw = conv_w[esl]
        cwm = np.ascontiguousarray(cw.reshape(8, 128, 4).transpose(1, 0, 2).reshape(128, 32))
        al = A_log[esl]
        alm = np.ascontiguousarray(al.reshape(8, 128, 16).transpose(1, 0, 2).reshape(128, 128))
        m = {
            "x0": np.ascontiguousarray(x[b]),
            "slt": np.array([[s, 1 - s]], np.uint32),
            "kv": kv, "identf": ident, "trimask": trimask, "dmask": dm,
            "aa_wq": np.ascontiguousarray(aa_wq[:, cols]),
            "aa_wk": np.ascontiguousarray(aa_wk[:, cols]),
            "aa_wv": np.ascontiguousarray(aa_wv[:, cols]),
            "aa_wo": aa_wo,
            "aa_bq": _vec128(np.asarray(inp["L0_attn_a_bq"])[cols], 4),
            "aa_bk": _vec128(np.asarray(inp["L0_attn_a_bk"])[cols], 4),
            "aa_bv": _vec128(np.asarray(inp["L0_attn_a_bv"])[cols], 4),
            "aa_bo": _vec128(np.asarray(inp["L0_attn_a_bo"]), NF),
            "ab_wq": np.ascontiguousarray(ab_wqT[:, cols]),
            "ab_wk": np.ascontiguousarray(ab_wkT[:, cols]),
            "ab_wv": np.ascontiguousarray(ab_wvT[:, cols]),
            "ab_wo": ab_woT,
            "ab_bo": _vec128(np.asarray(inp["L0_attn_b_bo"]), NF),
            "f0_w1": np.ascontiguousarray(inp["L0_ffn_w1"][0]),
            "f0_b1": _vec128(np.asarray(inp["L0_ffn_b1"][0]), 32),
            "f0_w2": np.ascontiguousarray(inp["L0_ffn_w2"][0]),
            "f0_b2": _vec128(np.asarray(inp["L0_ffn_b2"][0]), NF),
            "f0_g": _vec128(np.asarray(inp["L0_ffn_gamma"]), NF),
            "f0_be": _vec128(np.asarray(inp["L0_ffn_beta"]), NF),
            "la_lnw": _vec128(np.asarray(inp["L1_attn_a_ln_w"]), NF),
            "la_lnb": _vec128(np.asarray(inp["L1_attn_a_ln_b"]), NF),
            "la_wq": np.ascontiguousarray(la_wq[:, cols][:, perm]),
            "la_wk": np.ascontiguousarray(la_wkv[:, 0:E][:, cols][:, perm]),
            "la_wv": np.ascontiguousarray(la_wkv[:, E:2 * E][:, cols]),
            "la_wo": la_wo,
            "la_th": theta, "la_frep": frep, "la_sgn": sgn, "la_psw": psw,
            "f1_wu": np.ascontiguousarray(np.asarray(inp["L1_ffn_w_up"]).T),
            "f1_wd": np.ascontiguousarray(np.asarray(inp["L1_ffn_w_down"]).T),
            "mb_wxin": np.ascontiguousarray(W_inT[:, esl]),
            "mb_wz": np.ascontiguousarray(W_inT[:, 2048 + s * 1024:2048 + (s + 1) * 1024]),
            "mb_cw": cwm, "mb_cb": _vec128(conv_b[esl], NF),
            "mb_wxd": np.ascontiguousarray(W_xT[perm_rows][:, esl]),
            "mb_wxbc": np.ascontiguousarray(
                W_xT[perm_rows][:, 2048:2080].reshape(16, 128, 32)
                .transpose(1, 0, 2).reshape(128, 512)),
            "mb_alog": alm, "mb_dd": _vec128(Dp[esl], NF),
            "mb_sel": sel, "mb_wout": W_outT,
        }
        in_maps.append(m)
    return in_maps


_FUSED_CACHE = {}


def kernel_fused(inp, nphase=13):
    key = f"fused{nphase}"
    if key not in _FUSED_CACHE:
        stack = contextlib.ExitStack()
        Cx = build_fused(stack, nphase=nphase)
        _FUSED_CACHE[key] = (Cx, stack)
    Cx, _ = _FUSED_CACHE[key]
    in_maps = fused_in_maps(inp)
    res = run_bass_kernel_spmd(Cx.p.nc, in_maps, core_ids=list(range(8)))
    xo = np.empty((B, E, T), np.float32)
    for c in range(8):
        b, s = c // 2, c % 2
        xo[b][:, s * TC:(s + 1) * TC] = res.results[c]["y"]
    return np.ascontiguousarray(np.transpose(xo, (0, 2, 1)))



_BUILD_CACHE = {}


def _run(pname, build_fn, in_maps, out_names):
    if pname not in _BUILD_CACHE:
        stack = contextlib.ExitStack()
        p = build_fn(stack)
        _BUILD_CACHE[pname] = (p, stack)
    p, _ = _BUILD_CACHE[pname]
    res = run_bass_kernel_spmd(p.nc, in_maps, core_ids=list(range(8)))
    return [[r[n] for n in out_names] for r in res.results]


def _vec128(v, cols):
    return np.ascontiguousarray(v.reshape(cols, 128).T)


def sub_l0_ffn(x, inp):
    w1 = np.ascontiguousarray(inp["L0_ffn_w1"][0])
    w2 = np.ascontiguousarray(inp["L0_ffn_w2"][0])
    b1 = _vec128(np.asarray(inp["L0_ffn_b1"][0]), 32)
    b2 = _vec128(np.asarray(inp["L0_ffn_b2"][0]), NF)
    gam = _vec128(np.asarray(inp["L0_ffn_gamma"]), NF)
    bet = _vec128(np.asarray(inp["L0_ffn_beta"]), NF)
    in_maps = []
    for c in range(8):
        b, s = c // 2, c % 2
        xs = np.ascontiguousarray(x[b][:, s * TC:(s + 1) * TC])
        in_maps.append({"x": xs, "w1": w1, "b1": b1, "w2": w2, "b2": b2,
                        "gamma": gam, "beta": bet})
    outs = _run("l0ffn", build_l0_ffn, in_maps, ["y"])
    xo = np.empty_like(x)
    for c in range(8):
        b, s = c // 2, c % 2
        xo[b][:, s * TC:(s + 1) * TC] = outs[c][0]
    return xo


def kernel(**inp):
    return kernel_fused(inp, nphase=13)
